# version 20
# baseline (speedup 1.0000x reference)
import numpy as np
import ml_dtypes
import concourse.bass as bass
import concourse.mybir as mybir
from concourse.bass_utils import run_bass_kernel_spmd

F32 = mybir.dt.float32
BF16 = mybir.dt.bfloat16
AF = mybir.ActivationFunctionType
ALU = mybir.AluOpType

B, T, D = 4, 4096, 1024
TOK, NCH, CH, NTS = 2048, 4, 512, 16
HEADS, HD, NPAIR = 16, 64, 8
DH = 4096
DI, DIH, NS = 2048, 1024, 16
NE, IE = 8, 512
SCALE = HD ** (-0.5)
EPS = 1e-5


def _lhsT_tiles(wT):
    K, M = wT.shape
    return np.ascontiguousarray(
        wT.reshape(K // 128, 128, M // 128, 128).transpose(2, 1, 0, 3)
        .reshape(M // 128, 128, (K // 128) * 128))


def _rhs_tiles(w):
    K, N = w.shape
    return np.ascontiguousarray(w.reshape(K // 128, 128, N))


def prep_core_inputs(core, inp):
    b, h = core // 2, core % 2
    own_g = [2 * c + h for c in range(NCH)]
    d = {}
    x = np.asarray(inp["L0_attn_x"])[b]
    xo = np.concatenate([x[g * CH:(g + 1) * CH] for g in own_g], 0)
    d["x_fm"] = np.ascontiguousarray(xo.T).reshape(D // 128, 128, TOK)

    pos = np.concatenate([np.arange(g * CH, (g + 1) * CH) for g in own_g])
    inv = 1.0 / (500000.0 ** (np.arange(0, HD, 2) / HD))
    fr = pos[:, None] * inv[None, :]
    emb = np.concatenate([fr, fr], axis=1)
    d["rope_cos"] = np.tile(np.cos(emb), (1, HEADS)).astype(
        ml_dtypes.bfloat16).reshape(NTS, 128, HEADS * HD)
    d["rope_sin"] = np.tile(np.sin(emb), (1, HEADS)).astype(
        ml_dtypes.bfloat16).reshape(NTS, 128, HEADS * HD)

    d["ident"] = np.eye(128, dtype=np.float32)
    d["ones_row"] = np.ones((1, 128), dtype=np.float32)
    ex2 = np.zeros((2, 128), dtype=np.float32)
    ex2[0, :64] = 1.0
    ex2[1, 64:] = 1.0
    d["expand2"] = ex2
    d["flag0"] = np.full((128, 1), 1.0 if h == 1 else 0.0, np.float32)
    d["flag1"] = np.full((128, 1), 1.0 if h == 0 else 0.0, np.float32)
    ac = np.zeros((128, 8), np.float32)
    ac[:, 0], ac[:, 1], ac[:, 2] = EPS, -0.5, -1.0
    ac[:, 3], ac[:, 4], ac[:, 5] = SCALE, -SCALE, 1.0
    d["actc"] = ac
    s2b = np.zeros((2, 128), np.float32); s2b[1, :] = 1.0
    d["sel2b"] = s2b
    s8 = np.zeros((8, 8 * 128), np.float32)
    for e in range(8):
        s8[e, e * 128:(e + 1) * 128] = 1.0
    d["sel8"] = s8
    s16 = np.zeros((16, 16 * 128), np.float32)
    for si in range(16):
        s16[si, si * 128:(si + 1) * 128] = 1.0
    d["sel16"] = s16

    d["wqkv"] = _rhs_tiles(np.asarray(inp["L0_attn_W_qkv"]))
    d["wout"] = _lhsT_tiles(np.asarray(inp["L0_attn_W_out"]))

    d["w1"] = _lhsT_tiles(np.asarray(inp["L0_ffn_w1_w"]).T)
    d["w2"] = _lhsT_tiles(np.asarray(inp["L0_ffn_w2_w"]).T)
    d["h1"] = _lhsT_tiles(np.asarray(inp["L0_ffn_h1_w"]).T)
    d["h2"] = _lhsT_tiles(np.asarray(inp["L0_ffn_h2_w"]).T)
    d["w3"] = _lhsT_tiles(np.asarray(inp["L0_ffn_w3_w"]).T)
    d["s1"] = np.asarray(inp["L0_ffn_w1_s"]).reshape(32, 128).T.copy()
    d["s2"] = np.asarray(inp["L0_ffn_w2_s"]).reshape(32, 128).T.copy()
    d["sh1"] = np.asarray(inp["L0_ffn_h1_s"]).reshape(32, 128).T.copy()
    d["sh2"] = np.asarray(inp["L0_ffn_h2_s"]).reshape(32, 128).T.copy()
    d["s3"] = np.asarray(inp["L0_ffn_w3_s"]).reshape(8, 128).T.copy()

    rows = slice(h * DIH, (h + 1) * DIH)
    d["xproj"] = _lhsT_tiles(np.asarray(inp["L1_attn_x_proj_w"])[rows].T)
    d["dtproj"] = _lhsT_tiles(np.asarray(inp["L1_attn_dt_proj_w"])[rows].T)
    d["dtb"] = np.asarray(inp["L1_attn_dt_proj_b"])[rows].reshape(8, 128).T.copy()
    d["alog"] = np.ascontiguousarray(
        np.asarray(inp["L1_attn_A_log"])[rows].reshape(8, 128, NS)
        .transpose(1, 0, 2).reshape(128, 8 * NS))
    d["dvec"] = np.asarray(inp["L1_attn_D"])[rows].reshape(8, 128).T.copy()
    d["convw"] = np.ascontiguousarray(
        np.asarray(inp["L1_attn_conv_w"])[rows, 0].reshape(8, 128, 4)
        .transpose(1, 0, 2).reshape(128, 32))
    d["convb"] = np.asarray(inp["L1_attn_conv_b"])[rows].reshape(8, 128).T.copy()
    d["bproj"] = _rhs_tiles(np.asarray(inp["L1_attn_B_proj_w"]).T)
    d["cproj"] = _rhs_tiles(np.asarray(inp["L1_attn_C_proj_w"]).T)
    opw = np.asarray(inp["L1_attn_out_proj_w"]).T
    mine, other = opw[rows], opw[slice((1 - h) * DIH, (2 - h) * DIH)]
    d["oproj"] = _lhsT_tiles(np.concatenate([mine, other], 0))

    d["gate"] = _rhs_tiles(np.asarray(inp["L1_ffn_gate_w"]).T)
    wg, wu, wd = (np.asarray(inp["L1_ffn_w_gate"]),
                  np.asarray(inp["L1_ffn_w_up"]),
                  np.asarray(inp["L1_ffn_w_down"]))
    d["wgate"] = np.stack([_lhsT_tiles(wg[e].T) for e in range(NE)])
    d["wup"] = np.stack([_lhsT_tiles(wu[e].T) for e in range(NE)])
    d["wdown"] = np.stack([_lhsT_tiles(wd[e].T) for e in range(NE)])
    return d


INPUT_SPECS = {
    "x_fm": ([8, 128, TOK], F32),
    "rope_cos": ([NTS, 128, HEADS * HD], BF16),
    "rope_sin": ([NTS, 128, HEADS * HD], BF16),
    "ident": ([128, 128], F32),
    "ones_row": ([1, 128], F32),
    "expand2": ([2, 128], F32),
    "flag0": ([128, 1], F32),
    "flag1": ([128, 1], F32),
    "actc": ([128, 8], F32),
    "sel2b": ([2, 128], F32),
    "sel8": ([8, 8 * 128], F32),
    "sel16": ([16, 16 * 128], F32),
    "wqkv": ([8, 128, 3072], F32),
    "wout": ([8, 128, 8 * 128], F32),
    "w1": ([32, 128, 8 * 128], F32),
    "w2": ([32, 128, 8 * 128], F32),
    "h1": ([32, 128, 32 * 128], F32),
    "h2": ([32, 128, 32 * 128], F32),
    "w3": ([8, 128, 32 * 128], F32),
    "s1": ([128, 32], F32),
    "s2": ([128, 32], F32),
    "sh1": ([128, 32], F32),
    "sh2": ([128, 32], F32),
    "s3": ([128, 8], F32),
    "xproj": ([8, 128, 8 * 128], F32),
    "dtproj": ([8, 128, 8 * 128], F32),
    "dtb": ([128, 8], F32),
    "alog": ([128, 8 * NS], F32),
    "dvec": ([128, 8], F32),
    "convw": ([128, 32], F32),
    "convb": ([128, 8], F32),
    "bproj": ([8, 128, 16], F32),
    "cproj": ([8, 128, 16], F32),
    "oproj": ([8, 128, 16 * 128], F32),
    "gate": ([8, 128, 8], F32),
    "wgate": ([NE, 4, 128, 8 * 128], F32),
    "wup": ([NE, 4, 128, 8 * 128], F32),
    "wdown": ([NE, 8, 128, 4 * 128], F32),
}


def build(n_cores=8, stop_after=None, taps=()):
    nc = bass.Bass("TRN2", target_bir_lowering=False, debug=False,
                   detect_race_conditions=False)
    rg = [[i, i + 1] for i in range(0, n_cores, 2)]
    taps = list(taps)

    ins = {k: nc.dram_tensor(k, list(shp), dt, kind="ExternalInput").ap()
           for k, (shp, dt) in INPUT_SPECS.items()}
    out = nc.dram_tensor("out", [8, 128, TOK], F32, kind="ExternalOutput").ap()

    acc_park = nc.dram_tensor("acc_park", [NCH, 8, 128, CH], F32,
                              kind="Internal").ap()
    STREAMW = {"wout": [8, 128, 8 * 128], "w1": [32, 128, 8 * 128],
               "w2": [32, 128, 8 * 128], "h1": [32, 128, 32 * 128],
               "h2": [32, 128, 32 * 128], "w3": [8, 128, 32 * 128],
               "xproj": [8, 128, 8 * 128], "dtproj": [8, 128, 8 * 128],
               "oproj": [8, 128, 16 * 128]}
    wbf = {k: nc.dram_tensor("bf_" + k, shp, BF16, kind="Internal").ap()
           for k, shp in STREAMW.items()}
    MOEW = {"wgate": [NE, 4, 128, 8 * 128], "wup": [NE, 4, 128, 8 * 128],
            "wdown": [NE, 8, 128, 4 * 128]}
    for k, shp in MOEW.items():
        wbf[k] = nc.dram_tensor("bf_" + k, shp, BF16, kind="Internal").ap()
    cast_done = {}
    xn0_park = nc.dram_tensor("xn0_park", [8, 128, TOK], BF16,
                              kind="Internal").ap()
    q2_park = nc.dram_tensor("q2_park", [8, 128, TOK], BF16,
                             kind="Internal").ap()
    kv_in = nc.dram_tensor("kv_in", [128, 520], F32, kind="Internal").ap()
    kv_out = nc.dram_tensor("kv_out", [2, 128, 520], F32, kind="Internal").ap()
    ag_in = [nc.dram_tensor(f"ag_in{c}", [8, 128, CH], BF16, kind="Internal").ap()
             for c in range(NCH)]
    ag_out = [nc.dram_tensor(f"ag_out{c}", [2, 8, 128, CH], BF16,
                             kind="Internal").ap() for c in range(NCH)]
    yag_in = [nc.dram_tensor(f"yag_in{c}", [8, 128, 1024], BF16,
                             kind="Internal").ap() for c in range(NCH)]
    yag_out = [nc.dram_tensor(f"yag_out{c}", [2, 8, 128, 1024], BF16,
                              kind="Internal").ap() for c in range(NCH)]

    live = {}

    def sbuf(name, shape, dtype):
        m = nc.sbuf_tensor(name, list(shape), dtype)
        hdl = m.__enter__()
        live[name] = m
        return hdl

    def sfree(*names):
        for n in names:
            live.pop(n).__exit__(None, None, None)

    def psum_(name, shape):
        m = nc.psum_tensor(name, list(shape), F32)
        hdl = m.__enter__()
        live["psum_" + name] = m
        return hdl

    farena = sbuf("farena", [128, 787], F32)
    scal = farena[:, 0:160]
    onescol = farena[:, 160:161]
    flag0 = farena[:, 161:162]
    flag1 = farena[:, 162:163]
    actc = farena[:, 163:171]
    convw_sb = farena[:, 171:203]
    a_sb = farena[:, 203:331]
    alog_sb = farena[:, 331:459]
    state = farena[:, 459:587]
    halo = farena[:, 587:611]
    logit = farena[:, 611:659]
    identf = farena[:, 659:787]
    barena = sbuf("barena", [128, 480], BF16)
    ident = barena[:, 0:128]
    ones1 = barena[0:1, 128:256]
    exp2 = barena[0:2, 256:384]
    gate_sb = barena[:, 384:448]
    logitb = barena[:, 448:480]
    sel8 = sbuf("sel8s", [8, 8 * 128], BF16)
    sel16 = sbuf("sel16s", [16, 16 * 128], BF16)
    bcw = sbuf("bcw", [128, 256], BF16)
    xchunk = sbuf("xchunk", [128, 8 * CH], F32)
    tmp1 = sbuf("tmp1", [128, 1024], F32)
    tmp2 = sbuf("tmp2", [128, 1024], F32)
    tmp3 = sbuf("tmp3", [128, 1024], F32)
    wbufA = sbuf("wbufA", [128, 16 * 128], BF16)
    wbufB = sbuf("wbufB", [128, 16 * 128], BF16)
    cfm = sbuf("cfm", [8, CH], BF16)
    silb = sbuf("silb", [128, CH], BF16)
    kvsb = sbuf("kvsb", [128, 8 * 65], F32)
    kv2d = sbuf("kv2d", [128, 8 * 128], BF16)
    ks2 = sbuf("ks2", [128, 16], BF16)
    bfrow = sbuf("bfrow", [2, 1024], BF16)

    ps = [psum_(f"psb{i}", [128, 512]) for i in range(4)]
    psW0 = psum_("psW0", [128, 1024])
    psW1 = psum_("psW1", [128, 1024])
    ps.append(psW0[:, 0:512])
    ps.append(psW0[:, 512:1024])
    ps.append(psW1[:, 0:512])
    ps.append(psW1[:, 512:1024])

    class Ctx:
        def __init__(self):
            self.sems, self.cnt = {}, {}

        def sem(self, name):
            m = nc.semaphore(name)
            self.sems[name] = m.__enter__()
            live["sem_" + name] = m
            self.cnt[name] = 0

        def inc(self, instr, name, n=1):
            instr.then_inc(self.sems[name], n)
            self.cnt[name] += n
            return self.cnt[name]

        def wait(self, eng, name, val=None):
            eng.wait_ge(self.sems[name], self.cnt[name] if val is None else val)

    ctx = Ctx()
    for s in ["ld", "cast", "st", "coll", "td", "sc", "ve", "gA", "gB", "gC",
              "gD", "gp", "sA", "sB", "sC", "sD"]:
        ctx.sem(s)

    blk = nc.Block()
    block = blk.__enter__()

    def T(f): block.tensor(f)
    def V(f): block.vector(f)
    def S(f): block.scalar(f)
    def G(f): block.gpsimd(f)
    def Y(f): block.sync(f)

    def barrier():
        names = ["td", "sc", "ve", "ld", "cast", "st", "coll"]
        for engname in ["tensor", "vector", "scalar", "gpsimd", "sync"]:
            def _(e, engname=engname):
                for snm in names:
                    ctx.wait(e, snm)
            getattr(block, engname)(_)

    tap_list = []

    def tap(name, src_ap, shape, dtype):
        if name not in taps:
            return
        tt = nc.dram_tensor("tap_" + name, list(shape), dtype,
                            kind="ExternalOutput").ap()
        tap_list.append(name)

        def _(s_):
            ctx.wait(s_, "td"); ctx.wait(s_, "sc"); ctx.wait(s_, "ve")
            ctx.inc(s_.dma_start(out=tt, in_=src_ap), "st", 16)
        Y(_)

        def _b(v):
            ctx.wait(v, "st")
        V(_b)

    def finish():
        def _(s_):
            ctx.wait(s_, "st")
            ctx.wait(s_, "coll")
        Y(_)
        blk.__exit__(None, None, None)
        for m in reversed(list(live.values())):
            m.__exit__(None, None, None)
        return nc

    wfree = {"A": 0, "B": 0, "C": 0, "D": 0}
    wb = {"A": wbufA, "B": wbufB}
    gsem_hw = {"A": "gA", "B": "gB", "C": "gC", "D": "gD"}
    gsem_sw = {"A": "sA", "B": "sB", "C": "sC", "D": "sD"}
    pair = {"A": "C", "B": "D"}

    def mm_stream(strips, n_oc, n_kc, rhs_fn, evac, strips2=None,
                  tensor_first_wait=None, pbanks=(0, 1), pbanks2=(2, 3),
                  cast_gate=None, bufs=("A", "B"), gq=False, steps=None):
        gsem = gsem_sw if gq else gsem_hw
        evfree = [None, None]

        def do_oc(oc):
            bi = bufs[0] if oc % 2 == 0 else bufs[1]
            bw = wfree[bi]
            bw2 = wfree[pair[bi]] if strips2 is not None else 0

            def _g(g, oc=oc, bi=bi, bw=bw, bw2=bw2):
                if oc == 0 and cast_gate is not None:
                    ctx.wait(g, "cast", cast_gate)
                if bw:
                    ctx.wait(g, "td", bw)
                ctx.inc(g.dma_start(out=wb[bi][:, 0:n_kc * 128],
                                    in_=strips[oc]), gsem[bi], 16)
                if strips2 is not None:
                    bj = pair[bi]
                    if bw2:
                        ctx.wait(g, "td", bw2)
                    ctx.inc(g.dma_start(out=wb[bj][:, 0:n_kc * 128],
                                        in_=strips2[oc]), gsem[bj], 16)
            (G if gq else Y)(_g)

            gw = ctx.cnt[gsem[bi]]
            gw2 = ctx.cnt[gsem[pair[bi]]] if strips2 is not None else 0
            evw = evfree[oc % 2]

            def _t(t, oc=oc, bi=bi, gw=gw, gw2=gw2, evw=evw):
                if oc == 0 and tensor_first_wait is not None:
                    tensor_first_wait(t)
                if oc < 2:
                    ctx.wait(t, "sc")
                    ctx.wait(t, "ve")
                ctx.wait(t, gsem[bi], gw)
                if strips2 is not None:
                    ctx.wait(t, gsem[pair[bi]], gw2)
                if evw is not None:
                    ctx.wait(t, evw[0], evw[1])
                pm = ps[pbanks[oc % 2]]
                pm2 = ps[pbanks2[oc % 2]] if strips2 is not None else None
                for kc in range(n_kc):
                    mm = t.matmul(pm[:, :], wb[bi][:, kc * 128:(kc + 1) * 128],
                                  rhs_fn(kc), start=(kc == 0),
                                  stop=(kc == n_kc - 1))
                    if strips2 is not None:
                        mm = t.matmul(pm2[:, :],
                                      wb[pair[bi]][:, kc * 128:(kc + 1) * 128],
                                      rhs_fn(kc), start=(kc == 0),
                                      stop=(kc == n_kc - 1))
                ctx.inc(mm, "td")
            T(_t)
            wfree[bi] = ctx.cnt["td"]
            if strips2 is not None:
                wfree[pair[bi]] = ctx.cnt["td"]
            evfree[oc % 2] = evac(oc, pbanks[oc % 2],
                                  pbanks2[oc % 2] if strips2 is not None else None)

        for oc in range(n_oc):
            if steps is None:
                do_oc(oc)
            else:
                steps.append(lambda oc=oc: do_oc(oc))

    def _ph0(s_):
        ctx.inc(s_.dma_start(out=flag0[:, :], in_=ins["flag0"]), "ld", 16)
        ctx.inc(s_.dma_start(out=flag1[:, :], in_=ins["flag1"]), "ld", 16)
        ctx.inc(s_.dma_start(out=scal[:, 0:32], in_=ins["s1"]), "ld", 16)
        ctx.inc(s_.dma_start(out=scal[:, 32:64], in_=ins["s2"]), "ld", 16)
        ctx.inc(s_.dma_start(out=scal[:, 64:96], in_=ins["sh1"]), "ld", 16)
        ctx.inc(s_.dma_start(out=scal[:, 96:128], in_=ins["sh2"]), "ld", 16)
        ctx.inc(s_.dma_start(out=scal[:, 128:136], in_=ins["s3"]), "ld", 16)
        ctx.inc(s_.dma_start(out=scal[:, 136:144], in_=ins["dtb"]), "ld", 16)
        ctx.inc(s_.dma_start(out=scal[:, 144:152], in_=ins["convb"]), "ld", 16)
        ctx.inc(s_.dma_start(out=scal[:, 152:160], in_=ins["dvec"]), "ld", 16)
        ctx.inc(s_.dma_start(out=alog_sb[:, :], in_=ins["alog"]), "ld", 16)
        ctx.inc(s_.dma_start(out=convw_sb[:, :], in_=ins["convw"]), "ld", 16)
        ctx.inc(s_.dma_start(out=actc[:, :], in_=ins["actc"]), "ld", 16)
    Y(_ph0)

    def _ph0g(g):
        ctx.inc(g.dma_start(out=ident[:, :], in_=ins["ident"]), "cast", 16)
        ctx.inc(g.dma_start(out=identf[:, :], in_=ins["ident"]), "cast", 16)
        ctx.inc(g.dma_start(out=ones1[:, :], in_=ins["ones_row"]), "cast", 16)
        ctx.inc(g.dma_start(out=exp2[:, :], in_=ins["expand2"]), "cast", 16)
        ctx.inc(g.dma_start(out=sel8[:, :], in_=ins["sel8"]), "cast", 16)
        ctx.inc(g.dma_start(out=sel16[:, :], in_=ins["sel16"]), "cast", 16)
        for k in range(8):
            ctx.inc(g.dma_start(out=gate_sb[:, k * 8:(k + 1) * 8],
                                in_=ins["gate"][k]), "cast", 16)
            ctx.inc(g.dma_start(out=bcw[:, k * 16:(k + 1) * 16],
                                in_=ins["bproj"][k]), "cast", 16)
            ctx.inc(g.dma_start(out=bcw[:, 128 + k * 16:128 + (k + 1) * 16],
                                in_=ins["cproj"][k]), "cast", 16)
    G(_ph0g)

    def _ph0s(sc):
        ctx.wait(sc, "ld")
        ctx.inc(sc.activation(a_sb[:, :], alog_sb[:, :], AF.Exp), "sc")
    S(_ph0s)

    def _ph0v(v):
        ctx.wait(v, "sc")
        ctx.wait(v, "cast")
        ctx.inc(v.tensor_scalar_mul(a_sb[:, :], a_sb[:, :], -1.0), "ve")
        ctx.inc(v.memset(onescol[:, :], 1.0), "ve")
        ctx.inc(v.memset(state[:, :], 0.0), "ve")
        ctx.inc(v.memset(kv2d[:, :], 0.0), "ve")
        ctx.inc(v.memset(ks2[:, :], 0.0), "ve")
    V(_ph0v)

    if stop_after == "ph0":
        return finish(), tap_list

    def emit_ln(src_fn, dst_fn, nt):
        for k in range(8):
            def _s(sc, k=k):
                ctx.wait(sc, "ld")
                ctx.wait(sc, "td")
                if k == 0:
                    ctx.wait(sc, "ve")
                ctx.inc(sc.activation(tmp1[:, 0:nt], src_fn(k), AF.Square),
                        "sc")
            S(_s)

            def _t(t, k=k):
                ctx.wait(t, "sc")
                ctx.wait(t, "ve")
                t.matmul(ps[0][0:1, 0:nt], onescol[:, :], src_fn(k),
                         start=(k == 0), stop=(k == 7))
                mm = t.matmul(ps[1][0:1, 0:nt], onescol[:, :], tmp1[:, 0:nt],
                              start=(k == 0), stop=(k == 7))
                ctx.inc(mm, "td")
            T(_t)

        def _v(v):
            ctx.wait(v, "td")
            ctx.inc(v.tensor_scalar_mul(tmp2[0:1, 0:nt], ps[0][0:1, 0:nt],
                                        1.0 / D), "ve")
            ctx.inc(v.tensor_scalar_mul(tmp2[0:1, nt:2 * nt],
                                        ps[1][0:1, 0:nt], 1.0 / D), "ve")
            ctx.inc(v.tensor_mul(tmp3[0:1, 0:nt], tmp2[0:1, 0:nt],
                                 tmp2[0:1, 0:nt]), "ve")
            ctx.inc(v.tensor_sub(tmp3[0:1, nt:2 * nt], tmp2[0:1, nt:2 * nt],
                                 tmp3[0:1, 0:nt]), "ve")
        V(_v)

        def _s2(sc):
            ctx.wait(sc, "ve")
            ctx.inc(sc.activation(tmp3[0:1, 0:nt], tmp3[0:1, nt:2 * nt],
                                  AF.Ln, bias=actc[0:1, 0:1]), "sc")
            ctx.inc(sc.activation(tmp3[0:1, nt:2 * nt], tmp3[0:1, 0:nt],
                                  AF.Exp, scale=actc[0:1, 1:2]), "sc")
        S(_s2)

        def _v2(v):
            ctx.wait(v, "sc")
            ctx.inc(v.tensor_mul(tmp1[0:1, 0:nt], tmp2[0:1, 0:nt],
                                 tmp3[0:1, nt:2 * nt]), "ve")
            ctx.inc(v.tensor_copy(bfrow[0:1, 0:nt], tmp3[0:1, nt:2 * nt]),
                    "ve")
            ctx.inc(v.tensor_scalar_mul(bfrow[0:1, nt:2 * nt],
                                        tmp1[0:1, 0:nt], -1.0), "ve")
        V(_v2)

        def _t2(t):
            ctx.wait(t, "ve")
            t.matmul(ps[2][:, 0:nt], ones1[0:1, :], bfrow[0:1, 0:nt],
                     start=True, stop=True)
            mm = t.matmul(ps[3][:, 0:nt], ones1[0:1, :],
                          bfrow[0:1, nt:2 * nt], start=True, stop=True)
            ctx.inc(mm, "td")
        T(_t2)

        for k in range(8):
            def _v3(v, k=k):
                if k == 0:
                    ctx.wait(v, "td")
                ctx.inc(v.tensor_mul(tmp1[:, 0:nt], src_fn(k), ps[2][:, 0:nt]),
                        "ve")
                ctx.inc(v.tensor_add(dst_fn(k), tmp1[:, 0:nt], ps[3][:, 0:nt]),
                        "ve")
            V(_v3)

    kftm = sbuf("kftm", [128, NTS * 1024], BF16)
    vext = sbuf("vext", [128, NTS * 1040], BF16)
    xn0 = sbuf("xn0", [128, 8 * TOK], BF16)
    for c in range(NCH):
        def _lx(s_, c=c):
            ctx.wait(s_, "ve")
            for k in range(8):
                ctx.inc(s_.dma_start(out=xchunk[:, k * CH:(k + 1) * CH],
                                     in_=ins["x_fm"][k][:, c * CH:(c + 1) * CH]),
                        "ld", 16)
        Y(_lx)
        emit_ln(lambda k, c=c: xchunk[:, k * CH:(k + 1) * CH],
                lambda k, c=c: xn0[:, k * TOK + c * CH:k * TOK + (c + 1) * CH],
                CH)

    def _pk0(s_):
        ctx.wait(s_, "ve")
        for k in range(8):
            ctx.inc(s_.dma_start(out=xn0_park[k],
                                 in_=xn0[:, k * TOK:(k + 1) * TOK]), "st", 16)
    Y(_pk0)

    tap("xn0", xn0[:, :], [128, 8 * TOK], BF16)
    if stop_after == "ph1":
        return finish(), tap_list

    rope_sb = sbuf("rope_sb", [128, 2 * 2048], BF16)
    qtm = sbuf("qtm", [128, 1024], F32)
    q2c = sbuf("q2c", [128, 8 * 128], BF16)
    wqq = sbuf("wqq", [128, 8 * 1024], BF16)

    def _gq(g):
        for k in range(8):
            ctx.inc(g.dma_start(out=wqq[:, k * 1024:(k + 1) * 1024],
                                in_=ins["wqkv"][k][:, 0:1024]), "cast", 16)
    G(_gq)

    def emit_rope_feat(ts, psq, dst_ap, sc_pos, sc_neg, td_w=None, ld_w=None):
        rb = (ts % 2) * 2048

        def _v(v, ts=ts, rb=rb, td_w=td_w, ld_w=ld_w):
            ctx.wait(v, "td", td_w)
            ctx.wait(v, "ld", ld_w)
            ctx.wait(v, "sc")
            for half in range(2):
                qp = ps[psq + half][:, :]
                sl = slice(half * 512, (half + 1) * 512)
                ctx.inc(v.tensor_mul(tmp1[:, sl], qp,
                                     rope_sb[:, rb + half * 512:
                                             rb + (half + 1) * 512]), "ve")
                qv = qp.rearrange("p (h two d) -> p h two d", two=2, d=32)
                t2v = tmp2[:, sl].rearrange("p (h two d) -> p h two d",
                                            two=2, d=32)
                ctx.inc(v.tensor_scalar_mul(t2v[:, :, 0, :], qv[:, :, 1, :],
                                            -1.0), "ve")
                ctx.inc(v.tensor_copy(t2v[:, :, 1, :], qv[:, :, 0, :]), "ve")
                ctx.inc(v.tensor_mul(tmp2[:, sl], tmp2[:, sl],
                                     rope_sb[:, rb + 1024 + half * 512:
                                             rb + 1024 + (half + 1) * 512]),
                        "ve")
                ctx.inc(v.tensor_add(tmp1[:, sl], tmp1[:, sl], tmp2[:, sl]),
                        "ve")
        V(_v)

        def _s2(sc, ts=ts):
            ctx.wait(sc, "ve")
            ctx.inc(sc.activation(tmp2[:, 0:1024], tmp1[:, 0:1024], AF.Relu,
                                  scale=sc_neg), "sc")
            ctx.inc(sc.activation(tmp2[:, 0:1024], tmp2[:, 0:1024], AF.Exp,
                                  scale=actc[:, 2:3]), "sc")
            ctx.inc(sc.activation(tmp1[:, 0:1024], tmp1[:, 0:1024], AF.Relu,
                                  scale=sc_pos), "sc")
        S(_s2)

        def _v2(v, ts=ts):
            ctx.wait(v, "sc")
            ctx.inc(v.tensor_add(dst_ap, tmp1[:, 0:1024], tmp2[:, 0:1024]),
                    "ve")
        V(_v2)

    q_ve_done = {}
    q_ld_done = {}
    for ts in range(NTS):
        def _lr(s_, ts=ts):
            if ts >= 2:
                ctx.wait(s_, "ve", q_ve_done[ts - 2])
            ctx.inc(s_.dma_start(out=rope_sb[:, (ts % 2) * 2048:
                                             (ts % 2) * 2048 + 1024],
                                 in_=ins["rope_cos"][ts]), "ld", 16)
            ctx.inc(s_.dma_start(out=rope_sb[:, (ts % 2) * 2048 + 1024:
                                             (ts % 2) * 2048 + 2048],
                                 in_=ins["rope_sin"][ts]), "ld", 16)
        Y(_lr)
        q_ld_done[ts] = ctx.cnt["ld"]

        def _t(t, ts=ts):
            ctx.wait(t, "cast")
            if ts == 0:
                ctx.wait(t, "ve")
                ctx.wait(t, "sc")
            if ts >= 2:
                ctx.wait(t, "ve", q_ve_done[ts - 2])
            pq = 4 if ts % 2 == 0 else 6
            for k in range(8):
                lhs = xn0[:, k * TOK + ts * 128:k * TOK + (ts + 1) * 128]
                for half in range(2):
                    mm = t.matmul(ps[pq + half][:, :], lhs,
                                  wqq[:, k * 1024 + half * 512:
                                      k * 1024 + (half + 1) * 512],
                                  start=(k == 0), stop=(k == 7))
            ctx.inc(mm, "td")
        T(_t)
        td_q = ctx.cnt["td"]

        emit_rope_feat(ts, 4 if ts % 2 == 0 else 6, qtm[:, 0:1024],
                       actc[:, 3:4], actc[:, 4:5], td_w=td_q,
                       ld_w=q_ld_done[ts])
        q_ve_done[ts] = ctx.cnt["ve"]

        for pr in range(8):
            def _tt(t, ts=ts, pr=pr):
                ctx.wait(t, "ve")
                if pr == 0:
                    ctx.wait(t, "st")
                mm = t.transpose(ps[2][:, 0:128],
                                 qtm[:, pr * 128:(pr + 1) * 128], identf[:, :])
                ctx.inc(mm, "td")
            T(_tt)

            def _se(v, ts=ts, pr=pr):
                ctx.wait(v, "td")
                ctx.inc(v.tensor_copy(q2c[:, pr * 128:(pr + 1) * 128],
                                      ps[2][:, 0:128]), "ve")
            V(_se)

        def _pq(s_, ts=ts):
            ctx.wait(s_, "ve")
            for pr in range(8):
                ctx.inc(s_.dma_start(
                    out=q2_park[pr][:, ts * 128:(ts + 1) * 128],
                    in_=q2c[:, pr * 128:(pr + 1) * 128]), "st", 16)
        Y(_pq)

    barrier()
    sfree("wqq")

    if stop_after == "ph1q":
        return finish(), tap_list

    wqkv2 = sbuf("wqkv2", [128, 8 * 2048], BF16)

    cast_wk = [0]

    def _gk(g):
        for k in range(8):
            ctx.inc(g.dma_start(out=wqkv2[:, k * 2048:(k + 1) * 2048],
                                in_=ins["wqkv"][k][:, 1024:3072]), "cast", 16)
        cast_wk[0] = ctx.cnt["cast"]
    G(_gk)

    def _vo(v):
        vv = vext[:, :].rearrange("p (t h c) -> p t h c", t=NTS, h=HEADS)
        ctx.inc(v.memset(vv[:, :, :, 64:65], 1.0), "ve")
    V(_vo)

    k_ve_done = {}
    k_ld_done = {}
    for ts in range(NTS):
        def _lr(s_, ts=ts):
            if ts >= 2:
                ctx.wait(s_, "ve", k_ve_done[ts - 2])
            ctx.inc(s_.dma_start(out=rope_sb[:, (ts % 2) * 2048:
                                             (ts % 2) * 2048 + 1024],
                                 in_=ins["rope_cos"][ts]), "ld", 16)
            ctx.inc(s_.dma_start(out=rope_sb[:, (ts % 2) * 2048 + 1024:
                                             (ts % 2) * 2048 + 2048],
                                 in_=ins["rope_sin"][ts]), "ld", 16)
        Y(_lr)
        k_ld_done[ts] = ctx.cnt["ld"]

        def _t(t, ts=ts):
            ctx.wait(t, "cast", cast_wk[0])
            if ts == 0:
                ctx.wait(t, "ve")
                ctx.wait(t, "sc")
            if ts >= 2:
                ctx.wait(t, "ve", k_ve_done[ts - 2])
            pk, pv = (4, 6) if ts % 2 == 0 else (0, 2)
            for k in range(8):
                lhs = xn0[:, k * TOK + ts * 128:k * TOK + (ts + 1) * 128]
                for half in range(2):
                    t.matmul(ps[pk + half][:, :], lhs,
                             wqkv2[:, k * 2048 + half * 512:
                                   k * 2048 + (half + 1) * 512],
                             start=(k == 0), stop=(k == 7))
                for half in range(2):
                    mm = t.matmul(ps[pv + half][:, :], lhs,
                                  wqkv2[:, k * 2048 + 1024 + half * 512:
                                        k * 2048 + 1024 + (half + 1) * 512],
                                  start=(k == 0), stop=(k == 7))
            ctx.inc(mm, "td")
        T(_t)
        td_k = ctx.cnt["td"]

        def _sv(v, ts=ts):
            ctx.wait(v, "td", td_k)
            pv = 6 if ts % 2 == 0 else 2
            vv = vext[:, ts * 1040:(ts + 1) * 1040].rearrange(
                "p (h c) -> p h c", h=HEADS)
            for half in range(2):
                src = ps[pv + half][:, :].rearrange("p (h c) -> p h c", h=8)
                ctx.inc(v.tensor_copy(vv[:, half * 8:(half + 1) * 8, 0:64],
                                      src), "ve")
        V(_sv)

        emit_rope_feat(ts, 4 if ts % 2 == 0 else 0,
                       kftm[:, ts * 1024:(ts + 1) * 1024],
                       actc[:, 5:6], actc[:, 2:3], td_w=td_k,
                       ld_w=k_ld_done[ts])
        k_ve_done[ts] = ctx.cnt["ve"]

    barrier()
    sfree("wqkv2", "q2c", "qtm", "rope_sb", "xn0")
    tap("kftm", kftm[:, :], [128, NTS * 1024], BF16)
    tap("vext", vext[:, :], [128, NTS * 1040], BF16)
    if stop_after == "ph1k":
        return finish(), tap_list

    for hd_ in range(HEADS):
        bank = hd_ % 2
        po = (hd_ % 2) * 64

        def _t(t, hd_=hd_, bank=bank, po=po):
            ctx.wait(t, "ve")
            for tsl in range(NTS):
                mm = t.matmul(
                    ps[bank][po:po + 64, 0:65],
                    kftm[:, tsl * 1024 + hd_ * 64:tsl * 1024 + (hd_ + 1) * 64],
                    vext[:, tsl * 1040 + hd_ * 65:tsl * 1040 + (hd_ + 1) * 65],
                    start=(tsl == 0), stop=(tsl == NTS - 1),
                    tile_position=(0, po))
            ctx.inc(mm, "td")
        T(_t)

        def _s(v, hd_=hd_, bank=bank, po=po):
            ctx.wait(v, "td")
            ctx.inc(v.tensor_copy(
                kvsb[po:po + 64, (hd_ // 2) * 65:(hd_ // 2 + 1) * 65],
                ps[bank][po:po + 64, 0:65]), "ve")
        V(_s)

    def _st2(s_):
        ctx.wait(s_, "ve")
        ctx.inc(s_.dma_start(out=kv_in, in_=kvsb[:, 0:520]), "st", 16)
    Y(_st2)

    def _cc(g):
        ctx.wait(g, "st")
        ctx.inc(g.collective_compute("AllGather", ALU.bypass,
                                     replica_groups=rg, ins=[kv_in],
                                     outs=[kv_out]), "coll")
        for k in ["wgate", "wup", "wdown"]:
            for e in range(NE):
                ctx.inc(g.dma_start(out=wbf[k][e], in_=ins[k][e]),
                        "cast", 16)
            cast_done[k] = ctx.cnt["cast"]
        for k in ["xproj", "dtproj", "oproj"]:
            ctx.inc(g.dma_start(out=wbf[k], in_=ins[k]), "cast", 16)
            cast_done[k] = ctx.cnt["cast"]
    G(_cc)

    def _rb2(s_):
        ctx.wait(s_, "coll")
        ctx.inc(s_.dma_start(out=tmp1[:, 0:520], in_=kv_out[0]), "ld", 16)
        ctx.inc(s_.dma_start(out=tmp2[:, 0:520], in_=kv_out[1]), "ld", 16)
    Y(_rb2)

    def _vc(v):
        ctx.wait(v, "ld")
        ctx.inc(v.tensor_add(kvsb[:, 0:520], tmp1[:, 0:520], tmp2[:, 0:520]),
                "ve")
        for pr in range(8):
            ctx.inc(v.tensor_copy(kv2d[0:64, pr * 128:pr * 128 + 64],
                                  kvsb[0:64, pr * 65:pr * 65 + 64]), "ve")
            ctx.inc(v.tensor_copy(kv2d[64:128, pr * 128 + 64:pr * 128 + 128],
                                  kvsb[64:128, pr * 65:pr * 65 + 64]), "ve")
            ctx.inc(v.tensor_copy(ks2[0:64, pr * 2:pr * 2 + 1],
                                  kvsb[0:64, pr * 65 + 64:pr * 65 + 65]), "ve")
            ctx.inc(v.tensor_copy(ks2[64:128, pr * 2 + 1:pr * 2 + 2],
                                  kvsb[64:128, pr * 65 + 64:pr * 65 + 65]),
                    "ve")
    V(_vc)

    barrier()
    sfree("vext", "kftm")
    tap("kvsb", kvsb[:, :], [128, 8 * 65], F32)
    if stop_after == "ph2":
        return finish(), tap_list

    wb["C"] = sbuf("wbufC", [128, 32 * 128], BF16)
    wb["D"] = sbuf("wbufD", [128, 32 * 128], BF16)
    xn0c = sbuf("xn0c", [128, 8 * CH], BF16)
    q2cc = sbuf("q2cc", [128, 8 * CH], BF16)
    xn1c = sbuf("xn1c", [128, 8 * CH], BF16)
    ha = sbuf("ha", [128, 32 * CH], BF16)
    hb = sbuf("hb", [128, 32 * CH], BF16)

    for c in range(NCH):
        def _lc(s_, c=c):
            ctx.wait(s_, "ve")
            ctx.wait(s_, "td")
            for k in range(8):
                ctx.inc(s_.dma_start(out=xn0c[:, k * CH:(k + 1) * CH],
                                     in_=xn0_park[k][:, c * CH:(c + 1) * CH]),
                        "ld", 16)
                ctx.inc(s_.dma_start(out=q2cc[:, k * CH:(k + 1) * CH],
                                     in_=q2_park[k][:, c * CH:(c + 1) * CH]),
                        "ld", 16)
        Y(_lc)

        for pr in range(NPAIR):
            def _t(t, c=c, pr=pr):
                ctx.wait(t, "ve")
                ctx.wait(t, "sc")
                if pr == 0:
                    ctx.wait(t, "ld")
                t.matmul(ps[2][0:2, 0:CH], ks2[:, pr * 2:(pr + 1) * 2],
                         q2cc[:, pr * CH:(pr + 1) * CH], start=True, stop=True)
                mm = t.matmul(ps[3][:, 0:CH], kv2d[:, pr * 128:(pr + 1) * 128],
                              q2cc[:, pr * CH:(pr + 1) * CH], start=True,
                              stop=True)
                ctx.inc(mm, "td")
            T(_t)

            def _v(v, c=c, pr=pr):
                ctx.wait(v, "td")
                ctx.inc(v.tensor_scalar_max(tmp1[0:2, 0:CH], ps[2][0:2, 0:CH],
                                            1e-6), "ve")
                ctx.inc(v.reciprocal(tmp1[0:2, CH:2 * CH], tmp1[0:2, 0:CH]),
                        "ve")
                ctx.inc(v.tensor_copy(bfrow[0:2, 0:CH], tmp1[0:2, CH:2 * CH]),
                        "ve")
            V(_v)

            def _t2(t, c=c, pr=pr):
                ctx.wait(t, "ve")
                mm = t.matmul(ps[4][:, 0:CH], exp2[:, :], bfrow[0:2, 0:CH],
                              start=True, stop=True)
                ctx.inc(mm, "td")
            T(_t2)

            def _v2(v, c=c, pr=pr):
                ctx.wait(v, "td")
                ctx.inc(v.tensor_copy(tmp1[:, 0:CH], ps[4][:, 0:CH]), "ve")
                ctx.inc(v.tensor_mul(hb[:, pr * CH:(pr + 1) * CH],
                                     ps[3][:, 0:CH], tmp1[:, 0:CH]), "ve")
            V(_v2)

        def _lx(s_, c=c):
            ctx.wait(s_, "ve")
            ctx.wait(s_, "st")
            for k in range(8):
                ctx.inc(s_.dma_start(out=xchunk[:, k * CH:(k + 1) * CH],
                                     in_=ins["x_fm"][k][:, c * CH:(c + 1) * CH]),
                        "ld", 16)
        Y(_lx)

        def ev_wout(oc, pb, pb2):
            def _v(v, oc=oc, pb=pb):
                ctx.wait(v, "td")
                ctx.wait(v, "ld")
                ctx.inc(v.scalar_tensor_tensor(
                    xchunk[:, oc * CH:(oc + 1) * CH], ps[pb][:, 0:CH], 1.0,
                    xchunk[:, oc * CH:(oc + 1) * CH], ALU.bypass, ALU.add),
                    "ve")
            V(_v)
            return ("ve", ctx.cnt["ve"])

        mm_stream(ins["wout"], 8, 8,
                  lambda kc: hb[:, kc * CH:(kc + 1) * CH], ev_wout,
                  pbanks=(5, 6), gq=True)
        if c == 3:
            tap("attnfm3", hb[:, 0:8 * CH], [128, 8 * CH], BF16)
            tap("accA", xchunk[:, :], [128, 8 * CH], F32)

        def ev_w12(oc, pb, pb2, c=c):
            def _s(sc, oc=oc, pb=pb):
                ctx.wait(sc, "td")
                ctx.wait(sc, "ve")
                ctx.inc(sc.activation(silb[:, 0:CH], ps[pb][:, 0:CH], AF.Silu,
                                      scale=scal[:, oc:oc + 1]), "sc")
            S(_s)

            def _v(v, oc=oc, pb2=pb2):
                ctx.wait(v, "sc")
                ctx.inc(v.tensor_scalar_mul(tmp1[:, 0:CH], ps[pb2][:, 0:CH],
                                            scal[:, 32 + oc:33 + oc]), "ve")
                ctx.inc(v.tensor_mul(ha[:, oc * CH:(oc + 1) * CH],
                                     silb[:, 0:CH], tmp1[:, 0:CH]), "ve")
            V(_v)
            return ("ve", ctx.cnt["ve"])

        def fw(t):
            ctx.wait(t, "ve")
        mm_stream(ins["w1"], 32, 8,
                  lambda kc: xn0c[:, kc * CH:(kc + 1) * CH], ev_w12,
                  strips2=ins["w2"], tensor_first_wait=fw,
                  pbanks=(0, 1), pbanks2=(2, 3), gq=True)

        def ev_h1(oc, pb, pb2):
            def _s(sc, oc=oc, pb=pb):
                ctx.wait(sc, "td")
                ctx.inc(sc.activation(hb[:, oc * CH:(oc + 1) * CH],
                                      ps[pb][:, 0:CH], AF.Gelu,
                                      scale=scal[:, 64 + oc:65 + oc]), "sc")
            S(_s)
            return ("sc", ctx.cnt["sc"])

        if c == 3:
            tap("ha3", ha[:, :], [128, 32 * CH], BF16)

        def fw1(t):
            ctx.wait(t, "ve")
        mm_stream(ins["h1"], 32, 32,
                  lambda kc: ha[:, kc * CH:(kc + 1) * CH], ev_h1,
                  tensor_first_wait=fw1, pbanks=(0, 1),
                  bufs=("C", "D"), gq=True)

        def ev_h2(oc, pb, pb2):
            def _s(sc, oc=oc, pb=pb):
                ctx.wait(sc, "td")
                ctx.inc(sc.activation(ha[:, oc * CH:(oc + 1) * CH],
                                      ps[pb][:, 0:CH], AF.Gelu,
                                      scale=scal[:, 96 + oc:97 + oc]), "sc")
            S(_s)
            return ("sc", ctx.cnt["sc"])

        def fw2(t):
            ctx.wait(t, "sc")
        mm_stream(ins["h2"], 32, 32,
                  lambda kc: hb[:, kc * CH:(kc + 1) * CH], ev_h2,
                  tensor_first_wait=fw2, pbanks=(0, 1),
                  bufs=("C", "D"), gq=True)

        def ev_w3(oc, pb, pb2):
            def _v(v, oc=oc, pb=pb):
                ctx.wait(v, "td")
                ctx.inc(v.scalar_tensor_tensor(
                    xchunk[:, oc * CH:(oc + 1) * CH], ps[pb][:, 0:CH],
                    scal[:, 128 + oc:129 + oc],
                    xchunk[:, oc * CH:(oc + 1) * CH], ALU.mult, ALU.add), "ve")
            V(_v)
            return ("ve", ctx.cnt["ve"])

        def fw3(t):
            ctx.wait(t, "sc")
        mm_stream(ins["w3"], 8, 32,
                  lambda kc: ha[:, kc * CH:(kc + 1) * CH], ev_w3,
                  tensor_first_wait=fw3, pbanks=(0, 1),
                  bufs=("C", "D"), gq=True)

        if c == 3:
            tap("accB", xchunk[:, :], [128, 8 * CH], F32)

        emit_ln(lambda k: xchunk[:, k * CH:(k + 1) * CH],
                lambda k: xn1c[:, k * CH:(k + 1) * CH], CH)

        def _pk(s_, c=c):
            ctx.wait(s_, "ve")
            for k in range(8):
                ctx.inc(s_.dma_start(out=acc_park[c][k],
                                     in_=xchunk[:, k * CH:(k + 1) * CH]),
                        "st", 16)
                ctx.inc(s_.dma_start(out=ag_in[c][k],
                                     in_=xn1c[:, k * CH:(k + 1) * CH]),
                        "st", 16)
        Y(_pk)

        def _cg(g, c=c):
            ctx.wait(g, "st")
            ctx.inc(g.collective_compute("AllGather", ALU.bypass,
                                         replica_groups=rg, ins=[ag_in[c]],
                                         outs=[ag_out[c]]), "coll")
        G(_cg)

    barrier()
    sfree("hb", "ha", "xn1c", "q2cc", "xn0c", "wbufD", "wbufC",
          "bfrow", "ks2", "kv2d", "kvsb")
    del wb["C"], wb["D"]
    tap("acc3", xchunk[:, :], [128, 8 * CH], F32)
    tap("xn1c", xn1c[:, :], [128, 8 * CH], BF16)
    if stop_after == "ph3":
        return finish(), tap_list

    xn1g = sbuf("xn1g", [128, 8 * 1024], BF16)
    dtbuf = sbuf("dtbuf", [128, 8 * 1024], F32)
    xcbuf = sbuf("xcbuf", [128, 8 * 1024], F32)
    xiy = sbuf("xiy", [128, 8 * 1027], F32)
    yown = sbuf("yown", [128, 8 * 1024], BF16)
    tmpa = sbuf("tmpa", [128, 1024], F32)
    brow = sbuf("brow", [16, 2048], BF16)
    mscratch = sbuf("mscratch", [128, 8 * 1024], BF16)
    xn1cR = mscratch[:, 0:8 * CH]
    pe_sb = mscratch[:, 8 * CH:12 * CH]
    cbc = mscratch[:, 12 * CH:13 * CH]
    gsc = sbuf("gsc", [128, CH], F32)
    wb["C"] = sbuf("wbufC2", [128, 8 * 128], BF16)
    wb["D"] = sbuf("wbufD2", [128, 8 * 128], BF16)
    wfree["C"] = 0
    wfree["D"] = 0

    xinner = xiy
    ybuf = xiy

    pending = []

    def make_tail(j):
        steps = []
        yagc = 6 + j

        def st_ry():
            def _ry(s_, j=j, yagc=yagc):
                ctx.wait(s_, "coll", yagc)
                ctx.wait(s_, "ve")
                for k in range(8):
                    ctx.inc(s_.dma_start(
                        out=mscratch[:, k * 1024:(k + 1) * 1024],
                        in_=yag_out[j][0][k]), "ld", 16)
                    ctx.inc(s_.dma_start(
                        out=xn1g[:, k * 1024:(k + 1) * 1024],
                        in_=yag_out[j][1][k]), "ld", 16)
            Y(_ry)
        steps.append(st_ry)

        for k in range(8):
            def st_vy(k=k):
                def _vy(v, j=j, k=k):
                    ctx.wait(v, "ld")
                    ctx.inc(v.tensor_scalar_mul(
                        tmp3[:, 0:1024], xn1g[:, k * 1024:(k + 1) * 1024],
                        flag1[:, 0:1]), "ve")
                    ctx.inc(v.scalar_tensor_tensor(
                        xn1g[:, k * 1024:(k + 1) * 1024],
                        mscratch[:, k * 1024:(k + 1) * 1024], flag0[:, 0:1],
                        tmp3[:, 0:1024], ALU.mult, ALU.add), "ve")
                V(_vy)
            steps.append(st_vy)

        def st_ua():
            def _ua(s_, j=j):
                ctx.wait(s_, "ve")
                ctx.wait(s_, "st")
                for k in range(8):
                    ctx.inc(s_.dma_start(out=xchunk[:, k * CH:(k + 1) * CH],
                                         in_=acc_park[j][k]), "ld", 16)
                for k in range(8):
                    ctx.inc(s_.dma_start(out=xn1cR[:, k * CH:(k + 1) * CH],
                                         in_=ag_in[j][k]), "ld", 16)
            Y(_ua)
        steps.append(st_ua)

        def st_gate1():
            ld_moe = ctx.cnt["ld"]
            for t4 in range(4):
                def _t5(t, j=j, t4=t4, ld_moe=ld_moe):
                    ctx.wait(t, "ld", ld_moe)
                    ctx.wait(t, "ve")
                    pb = t4 % 2
                    for kc in range(8):
                        mm = t.matmul(ps[pb][:, 0:8],
                                      xn1cR[:, kc * CH + t4 * 128:
                                            kc * CH + (t4 + 1) * 128],
                                      gate_sb[:, kc * 8:(kc + 1) * 8],
                                      start=(kc == 0), stop=(kc == 7))
                    ctx.inc(mm, "td")
                T(_t5)

                def _v5(v, t4=t4):
                    ctx.wait(v, "td")
                    pb = t4 % 2
                    ctx.inc(v.reduce_max(logit[:, 32 + t4:33 + t4],
                                         ps[pb][:, 0:8],
                                         axis=mybir.AxisListType.X), "ve")
                    ctx.wait(v, "ve")
                    ctx.inc(v.tensor_scalar(logit[:, t4 * 8:(t4 + 1) * 8],
                                            ps[pb][:, 0:8],
                                            logit[:, 32 + t4:33 + t4], None,
                                            ALU.is_ge), "ve")
                V(_v5)
        steps.append(st_gate1)

        def st_gate2():
            def _s6(sc, j=j):
                ctx.wait(sc, "ve")
                ctx.inc(sc.activation(logit[:, 36:40], logit[:, 32:36],
                                      AF.Sigmoid), "sc")
            S(_s6)

            def _v6(v, j=j):
                ctx.wait(v, "sc")
                for t4 in range(4):
                    ctx.inc(v.tensor_scalar_mul(
                        logitb[:, t4 * 8:(t4 + 1) * 8],
                        logit[:, t4 * 8:(t4 + 1) * 8],
                        logit[:, 36 + t4:37 + t4]), "ve")
            V(_v6)

            for t4 in range(4):
                def _t6(t, j=j, t4=t4):
                    ctx.wait(t, "ve")
                    mm = t.matmul(ps[2][0:8, 0:128],
                                  logitb[:, t4 * 8:(t4 + 1) * 8],
                                  ident[:, :], start=True, stop=True)
                    ctx.inc(mm, "td")
                T(_t6)

                def _s7(v, j=j, t4=t4):
                    ctx.wait(v, "td")
                    ctx.inc(v.tensor_copy(cfm[0:8, t4 * 128:(t4 + 1) * 128],
                                          ps[2][0:8, 0:128]), "ve")
                V(_s7)
        steps.append(st_gate2)

        for e in range(NE):
            def st_cbc(e=e, j=j):
                def _t7(t, e=e):
                    ctx.wait(t, "ve")
                    mm = t.matmul(ps[3][:, 0:CH],
                                  sel8[:, e * 128:(e + 1) * 128],
                                  cfm[0:8, 0:CH], start=True, stop=True)
                    ctx.inc(mm, "td")
                T(_t7)

                def _s8(v, e=e):
                    ctx.wait(v, "td")
                    ctx.inc(v.tensor_copy(cbc[:, 0:CH], ps[3][:, 0:CH]),
                            "ve")
                V(_s8)
            steps.append(st_cbc)

            def ev_gu(oc, pb, pb2, e=e):
                def _v(v, oc=oc, pb=pb, pb2=pb2):
                    ctx.wait(v, "td")
                    ctx.inc(v.tensor_mul(gsc[:, 0:CH], ps[pb][:, 0:CH],
                                         cbc[:, 0:CH]), "ve")
                    ctx.inc(v.tensor_mul(tmp2[:, 0:CH], ps[pb2][:, 0:CH],
                                         cbc[:, 0:CH]), "ve")
                V(_v)

                def _s(sc, oc=oc):
                    ctx.wait(sc, "ve")
                    ctx.inc(sc.activation(silb[:, 0:CH], gsc[:, 0:CH],
                                          AF.Silu), "sc")
                S(_s)

                def _v2(v, oc=oc):
                    ctx.wait(v, "sc")
                    ctx.inc(v.tensor_mul(pe_sb[:, oc * CH:(oc + 1) * CH],
                                         silb[:, 0:CH], tmp2[:, 0:CH]),
                            "ve")
                V(_v2)
                return ("ve", ctx.cnt["ve"])

            def fwg(t):
                ctx.wait(t, "ve")
            mm_stream(wbf["wgate"][e], 4, 8,
                      lambda kc: xn1cR[:, kc * CH:(kc + 1) * CH], ev_gu,
                      strips2=wbf["wup"][e], tensor_first_wait=fwg,
                      pbanks=(0, 1), pbanks2=(2, 3),
                      cast_gate=cast_done["wup"] if (j == 0 and e == 0)
                      else None, steps=steps)

            def ev_dn(oc, pb, pb2):
                def _v(v, oc=oc, pb=pb):
                    ctx.wait(v, "td")
                    ctx.inc(v.scalar_tensor_tensor(
                        xchunk[:, oc * CH:(oc + 1) * CH], ps[pb][:, 0:CH],
                        1.0, xchunk[:, oc * CH:(oc + 1) * CH],
                        ALU.bypass, ALU.add), "ve")
                V(_v)
                return ("ve", ctx.cnt["ve"])

            def fwd(t):
                ctx.wait(t, "ve")
            mm_stream(wbf["wdown"][e], 8, 4,
                      lambda kc: pe_sb[:, kc * CH:(kc + 1) * CH], ev_dn,
                      tensor_first_wait=fwd, pbanks=(0, 1),
                      cast_gate=cast_done["wdown"] if (j == 0 and e == 0)
                      else None, steps=steps)

        def yrhs(kc, g_):
            if kc < 8:
                return yown[:, kc * 1024 + g_ * CH:kc * 1024 + (g_ + 1) * CH]
            return xn1g[:, (kc - 8) * 1024 + g_ * CH:
                        (kc - 8) * 1024 + (g_ + 1) * CH]

        for g_ in range(2):
            sel = flag1 if g_ == 0 else flag0

            def ev_op(oc, pb, pb2, j=j, g_=g_, sel=sel):
                def _v(v, oc=oc, pb=pb, sel=sel):
                    ctx.wait(v, "td")
                    ctx.inc(v.tensor_scalar_mul(
                        tmp3[:, 0:CH], ps[pb][:, 0:CH], sel[:, 0:1]),
                        "ve")
                    ctx.inc(v.scalar_tensor_tensor(
                        xchunk[:, oc * CH:(oc + 1) * CH], tmp3[:, 0:CH],
                        1.0, xchunk[:, oc * CH:(oc + 1) * CH],
                        ALU.bypass, ALU.add), "ve")
                V(_v)
                return ("ve", ctx.cnt["ve"])

            def fwo(t):
                ctx.wait(t, "ve")
            mm_stream(wbf["oproj"], 8, 16,
                      lambda kc, g_=g_: yrhs(kc, g_), ev_op,
                      tensor_first_wait=fwo, pbanks=(0, 1),
                      cast_gate=cast_done["oproj"] if (j == 0 and g_ == 0)
                      else None, steps=steps)

        def st_so():
            def _so(s_, j=j):
                ctx.wait(s_, "ve")
                for k in range(8):
                    ctx.inc(s_.dma_start(out=out[k][:, j * CH:(j + 1) * CH],
                                         in_=xchunk[:, k * CH:(k + 1) * CH]),
                            "st", 16)
            Y(_so)
        steps.append(st_so)
        return steps

    for j in range(NCH):
        agc = 2 + j

        def _rb(s_, j=j, agc=agc):
            ctx.wait(s_, "coll", agc)
            ctx.wait(s_, "ve")
            ctx.wait(s_, "td")
            for k in range(8):
                ctx.inc(s_.dma_start(out=xn1g[:, k * 1024:k * 1024 + CH],
                                     in_=ag_out[j][0][k]), "ld", 16)
                ctx.inc(s_.dma_start(out=xn1g[:, k * 1024 + CH:(k + 1) * 1024],
                                     in_=ag_out[j][1][k]), "ld", 16)
        Y(_rb)

        def _vh(v, j=j):
            ctx.wait(v, "sc")
            if j > 0:
                for k in range(8):
                    ctx.inc(v.tensor_copy(
                        halo[:, k * 3:(k + 1) * 3],
                        xinner[:, k * 1027 + 1024:k * 1027 + 1027]), "ve")
            else:
                ctx.inc(v.memset(halo[:, :], 0.0), "ve")
            for k in range(8):
                ctx.inc(v.tensor_copy(xinner[:, k * 1027:k * 1027 + 3],
                                      halo[:, k * 3:(k + 1) * 3]), "ve")
        V(_vh)

        for g_ in range(2):
            def ev_xpg(oc, pb, pb2, j=j, g_=g_):
                def _s(v, oc=oc, pb=pb):
                    ctx.wait(v, "td")
                    ctx.inc(v.tensor_copy(
                        xinner[:, oc * 1027 + 3 + g_ * CH:
                               oc * 1027 + 3 + (g_ + 1) * CH],
                        ps[pb][:, 0:CH]), "ve")
                V(_s)
                return ("ve", ctx.cnt["ve"])

            def fwx(t, j=j, g_=g_):
                ctx.wait(t, "ld")
                ctx.wait(t, "ve")
            mm_stream(wbf["xproj"], 8, 8,
                      lambda kc, g_=g_: xn1g[:, kc * 1024 + g_ * CH:
                                             kc * 1024 + (g_ + 1) * CH],
                      ev_xpg, tensor_first_wait=fwx, pbanks=(0, 1),
                      cast_gate=cast_done["xproj"] if (j == 0 and g_ == 0)
                      else None)

        for k in range(8):
            def _v(v, j=j, k=k):
                ctx.wait(v, "sc")
                ctx.inc(v.tensor_scalar_mul(
                    xcbuf[:, k * 1024:(k + 1) * 1024],
                    xinner[:, k * 1027 + 0:k * 1027 + 1024],
                    convw_sb[:, k * 4:k * 4 + 1]), "ve")
                for kk in range(1, 4):
                    ctx.inc(v.scalar_tensor_tensor(
                        xcbuf[:, k * 1024:(k + 1) * 1024],
                        xinner[:, k * 1027 + kk:k * 1027 + 1024 + kk],
                        convw_sb[:, k * 4 + kk:k * 4 + kk + 1],
                        xcbuf[:, k * 1024:(k + 1) * 1024],
                        ALU.mult, ALU.add), "ve")
            V(_v)

            def _s(sc, j=j, k=k):
                ctx.wait(sc, "ve")
                ctx.inc(sc.activation(xcbuf[:, k * 1024:(k + 1) * 1024],
                                      xcbuf[:, k * 1024:(k + 1) * 1024],
                                      AF.Silu, bias=scal[:, 144 + k:145 + k]),
                        "sc")
            S(_s)

        for g_ in range(2):
            def ev_dt(oc, pb, pb2, j=j, g_=g_):
                def _s(sc, oc=oc, pb=pb):
                    ctx.wait(sc, "td")
                    dst = dtbuf[:, oc * 1024 + g_ * CH:
                                oc * 1024 + (g_ + 1) * CH]
                    ctx.inc(sc.activation(dst, ps[pb][:, 0:CH], AF.Exp,
                                          bias=scal[:, 136 + oc:137 + oc]),
                            "sc")
                S(_s)
                return ("sc", ctx.cnt["sc"])

            def fwd2(t):
                ctx.wait(t, "sc")
                ctx.wait(t, "ve")
            mm_stream(wbf["dtproj"], 8, 8,
                      lambda kc, g_=g_: xn1g[:, kc * 1024 + g_ * CH:
                                             kc * 1024 + (g_ + 1) * CH],
                      ev_dt, tensor_first_wait=fwd2, pbanks=(0, 1),
                      cast_gate=cast_done["dtproj"] if (j == 0 and g_ == 0)
                      else None)

        def _sp2(sc, j=j):
            for oc in range(8):
                for gg in range(2):
                    dst = dtbuf[:, oc * 1024 + gg * CH:
                                oc * 1024 + (gg + 1) * CH]
                    ctx.inc(sc.activation(dst, dst, AF.Ln,
                                          bias=actc[:, 5:6]), "sc")
        S(_sp2)
        sc_dt = ctx.cnt["sc"]

        def _tbc(t, j=j):
            ctx.wait(t, "cast")
            ctx.wait(t, "sc")
            for g_ in range(2):
                for kc in range(8):
                    t.matmul(ps[4 + g_][0:16, 0:CH],
                             bcw[:, kc * 16:(kc + 1) * 16],
                             xn1g[:, kc * 1024 + g_ * CH:
                                  kc * 1024 + (g_ + 1) * CH],
                             start=(kc == 0), stop=(kc == 7))
                for kc in range(8):
                    mm = t.matmul(ps[6 + g_][0:16, 0:CH],
                                  bcw[:, 128 + kc * 16:128 + (kc + 1) * 16],
                                  xn1g[:, kc * 1024 + g_ * CH:
                                       kc * 1024 + (g_ + 1) * CH],
                                  start=(kc == 0), stop=(kc == 7))
            ctx.inc(mm, "td")
        T(_tbc)

        def _sbc(v, j=j):
            ctx.wait(v, "td")
            for g_ in range(2):
                ctx.inc(v.tensor_copy(brow[0:16, g_ * CH:(g_ + 1) * CH],
                                      ps[4 + g_][0:16, 0:CH]), "ve")
                ctx.inc(v.tensor_copy(
                    brow[0:16, 1024 + g_ * CH:1024 + (g_ + 1) * CH],
                    ps[6 + g_][0:16, 0:CH]), "ve")
        V(_sbc)

        for k in range(8):
            def _v(v, j=j, k=k):
                ctx.wait(v, "sc")
                ctx.inc(v.tensor_scalar_mul(
                    ybuf[:, k * 1027:k * 1027 + 1024],
                    xcbuf[:, k * 1024:(k + 1) * 1024],
                    scal[:, 152 + k:153 + k]), "ve")
                ctx.inc(v.tensor_mul(xcbuf[:, k * 1024:(k + 1) * 1024],
                                     xcbuf[:, k * 1024:(k + 1) * 1024],
                                     dtbuf[:, k * 1024:(k + 1) * 1024]), "ve")
            V(_v)

        ve_after = {}
        sc_after = {}
        abufs = [tmp1, tmpa]
        n_steps = len(pending)
        done_steps = 0

        def emit_sa(s_i, k0):
            i0 = s_i * 8 + k0
            ab0 = abufs[i0 % 2]

            def _sa(sc, s_i=s_i, k0=k0, i0=i0, ab0=ab0):
                if i0 >= 2:
                    ctx.wait(sc, "ve", ve_after[i0 - 2])
                ctx.inc(sc.activation(
                    ab0[:, 0:1024], dtbuf[:, k0 * 1024:(k0 + 1) * 1024],
                    AF.Exp,
                    scale=a_sb[:, k0 * 16 + s_i:k0 * 16 + s_i + 1]), "sc")
                sc_after[i0] = ctx.cnt["sc"]
            S(_sa)

        for s_i in range(NS):
            ve_bc = ve_after.get(s_i * 8 - 1)

            def _tb(t, j=j, s_i=s_i, ve_bc=ve_bc):
                if ve_bc is None:
                    ctx.wait(t, "ve")
                else:
                    ctx.wait(t, "ve", ve_bc)
                for g_ in range(2):
                    t.matmul(psW0[:, g_ * CH:(g_ + 1) * CH],
                             sel16[:, s_i * 128:(s_i + 1) * 128],
                             brow[0:16, g_ * CH:(g_ + 1) * CH],
                             start=True, stop=True)
                    mm = t.matmul(psW1[:, g_ * CH:(g_ + 1) * CH],
                                  sel16[:, s_i * 128:(s_i + 1) * 128],
                                  brow[0:16, 1024 + g_ * CH:
                                       1024 + (g_ + 1) * CH],
                                  start=True, stop=True)
                ctx.inc(mm, "td")
            T(_tb)
            td_tb = ctx.cnt["td"]

            emit_sa(s_i, 0)
            emit_sa(s_i, 1)

            for k in range(8):
                col = s_i * 8 + k
                i = s_i * 8 + k
                ab = abufs[i % 2]

                def _vs(v, j=j, s_i=s_i, k=k, col=col, i=i, ab=ab,
                        td_tb=td_tb):
                    ctx.wait(v, "sc", sc_after[i])
                    if k == 0:
                        if s_i == NS - 1:
                            ctx.wait(v, "td")
                        else:
                            ctx.wait(v, "td", td_tb)
                    ctx.inc(v.tensor_mul(tmp2[:, 0:1024],
                                         xcbuf[:, k * 1024:(k + 1) * 1024],
                                         psW0[:, 0:1024]), "ve")
                    ctx.inc(v.tensor_tensor_scan(tmp3[:, 0:1024],
                                                 ab[:, 0:1024],
                                                 tmp2[:, 0:1024],
                                                 state[:, col:col + 1],
                                                 ALU.mult, ALU.add), "ve")
                    ctx.wait(v, "ve")
                    ctx.inc(v.tensor_copy(state[:, col:col + 1],
                                          tmp3[:, 1023:1024]), "ve")
                    ctx.inc(v.tensor_mul(tmp2[:, 0:1024], tmp3[:, 0:1024],
                                         psW1[:, 0:1024]), "ve")
                    if s_i < NS - 1:
                        ctx.inc(v.tensor_add(
                            ybuf[:, k * 1027:k * 1027 + 1024],
                            ybuf[:, k * 1027:k * 1027 + 1024],
                            tmp2[:, 0:1024]), "ve")
                    else:
                        ctx.inc(v.tensor_add(
                            yown[:, k * 1024:(k + 1) * 1024],
                            ybuf[:, k * 1027:k * 1027 + 1024],
                            tmp2[:, 0:1024]), "ve")
                V(_vs)
                ve_after[i] = ctx.cnt["ve"]

                if k + 2 < 8:
                    emit_sa(s_i, k + 2)

                slot = s_i * 8 + k
                while pending and done_steps * 120 < n_steps * (slot + 1):
                    pending.pop(0)()
                    done_steps += 1

        while pending:
            pending.pop(0)()

        def _sy(s_, j=j):
            ctx.wait(s_, "ve")
            for k in range(8):
                ctx.inc(s_.dma_start(out=yag_in[j][k],
                                     in_=yown[:, k * 1024:(k + 1) * 1024]),
                        "st", 16)
        Y(_sy)

        def _cy(g, j=j):
            ctx.wait(g, "st")
            ctx.inc(g.collective_compute("AllGather", ALU.bypass,
                                         replica_groups=rg, ins=[yag_in[j]],
                                         outs=[yag_out[j]]), "coll")
        G(_cy)

        pending = make_tail(j)

    while pending:
        pending.pop(0)()

    tap("dtbuf", dtbuf[:, :], [128, 8 * 1024], F32)
    tap("xcbuf", xcbuf[:, :], [128, 8 * 1024], F32)
    tap("yown", yown[:, :], [128, 8 * 1024], BF16)
    return finish(), tap_list


def kernel(**inputs):
    nc, _ = build(8)
    in_maps = [prep_core_inputs(c, inputs) for c in range(8)]
    res = run_bass_kernel_spmd(nc, in_maps, core_ids=list(range(8)))
    full = np.zeros((B, T, D), dtype=np.float32)
    for core in range(8):
        b, h = core // 2, core % 2
        o = np.asarray(res.results[core]["out"]).reshape(D, NCH, CH)
        for c in range(NCH):
            g = 2 * c + h
            full[b, g * CH:(g + 1) * CH, :] = o[:, c, :].T
    return full


# version 21
# speedup vs baseline: 1.0674x; 1.0674x over previous
import numpy as np
import ml_dtypes
import concourse.bass as bass
import concourse.mybir as mybir
from concourse.bass_utils import run_bass_kernel_spmd

F32 = mybir.dt.float32
BF16 = mybir.dt.bfloat16
AF = mybir.ActivationFunctionType
ALU = mybir.AluOpType

B, T, D = 4, 4096, 1024
TOK, NCH, CH, NTS = 2048, 4, 512, 16
HEADS, HD, NPAIR = 16, 64, 8
DH = 4096
DI, DIH, NS = 2048, 1024, 16
NE, IE = 8, 512
SCALE = HD ** (-0.5)
EPS = 1e-5


def _lhsT_tiles(wT):
    K, M = wT.shape
    return np.ascontiguousarray(
        wT.reshape(K // 128, 128, M // 128, 128).transpose(2, 1, 0, 3)
        .reshape(M // 128, 128, (K // 128) * 128))


def _rhs_tiles(w):
    K, N = w.shape
    return np.ascontiguousarray(w.reshape(K // 128, 128, N))


def prep_core_inputs(core, inp):
    b, h = core // 2, core % 2
    own_g = [2 * c + h for c in range(NCH)]
    d = {}
    x = np.asarray(inp["L0_attn_x"])[b]
    xo = np.concatenate([x[g * CH:(g + 1) * CH] for g in own_g], 0)
    d["x_fm"] = np.ascontiguousarray(xo.T).reshape(D // 128, 128, TOK)

    pos = np.concatenate([np.arange(g * CH, (g + 1) * CH) for g in own_g])
    inv = 1.0 / (500000.0 ** (np.arange(0, HD, 2) / HD))
    fr = pos[:, None] * inv[None, :]
    emb = np.concatenate([fr, fr], axis=1)
    d["rope_cos"] = np.tile(np.cos(emb), (1, HEADS)).astype(
        ml_dtypes.bfloat16).reshape(NTS, 128, HEADS * HD)
    d["rope_sin"] = np.tile(np.sin(emb), (1, HEADS)).astype(
        ml_dtypes.bfloat16).reshape(NTS, 128, HEADS * HD)

    d["ident"] = np.eye(128, dtype=np.float32)
    d["ones_row"] = np.ones((1, 128), dtype=np.float32)
    ex2 = np.zeros((2, 128), dtype=np.float32)
    ex2[0, :64] = 1.0
    ex2[1, 64:] = 1.0
    d["expand2"] = ex2
    d["flag0"] = np.full((128, 1), 1.0 if h == 1 else 0.0, np.float32)
    d["flag1"] = np.full((128, 1), 1.0 if h == 0 else 0.0, np.float32)
    ac = np.zeros((128, 8), np.float32)
    ac[:, 0], ac[:, 1], ac[:, 2] = EPS, -0.5, -1.0
    ac[:, 3], ac[:, 4], ac[:, 5] = SCALE, -SCALE, 1.0
    d["actc"] = ac
    s2b = np.zeros((2, 128), np.float32); s2b[1, :] = 1.0
    d["sel2b"] = s2b
    s8 = np.zeros((8, 8 * 128), np.float32)
    for e in range(8):
        s8[e, e * 128:(e + 1) * 128] = 1.0
    d["sel8"] = s8
    s16 = np.zeros((16, 16 * 128), np.float32)
    for si in range(16):
        s16[si, si * 128:(si + 1) * 128] = 1.0
    d["sel16"] = s16

    d["wqkv"] = _rhs_tiles(np.asarray(inp["L0_attn_W_qkv"]))
    d["wout"] = _lhsT_tiles(np.asarray(inp["L0_attn_W_out"]))

    d["w1"] = _lhsT_tiles(np.asarray(inp["L0_ffn_w1_w"]).T)
    d["w2"] = _lhsT_tiles(np.asarray(inp["L0_ffn_w2_w"]).T)
    d["h1"] = _lhsT_tiles(np.asarray(inp["L0_ffn_h1_w"]).T)
    d["h2"] = _lhsT_tiles(np.asarray(inp["L0_ffn_h2_w"]).T)
    d["w3"] = _lhsT_tiles(np.asarray(inp["L0_ffn_w3_w"]).T)
    d["s1"] = np.asarray(inp["L0_ffn_w1_s"]).reshape(32, 128).T.copy()
    d["s2"] = np.asarray(inp["L0_ffn_w2_s"]).reshape(32, 128).T.copy()
    d["sh1"] = np.asarray(inp["L0_ffn_h1_s"]).reshape(32, 128).T.copy()
    d["sh2"] = np.asarray(inp["L0_ffn_h2_s"]).reshape(32, 128).T.copy()
    d["s3"] = np.asarray(inp["L0_ffn_w3_s"]).reshape(8, 128).T.copy()

    rows = slice(h * DIH, (h + 1) * DIH)
    d["xproj"] = _lhsT_tiles(np.asarray(inp["L1_attn_x_proj_w"])[rows].T)
    d["dtproj"] = _lhsT_tiles(np.asarray(inp["L1_attn_dt_proj_w"])[rows].T)
    d["dtb"] = np.asarray(inp["L1_attn_dt_proj_b"])[rows].reshape(8, 128).T.copy()
    d["alog"] = np.ascontiguousarray(
        np.asarray(inp["L1_attn_A_log"])[rows].reshape(8, 128, NS)
        .transpose(1, 0, 2).reshape(128, 8 * NS))
    d["dvec"] = np.asarray(inp["L1_attn_D"])[rows].reshape(8, 128).T.copy()
    d["convw"] = np.ascontiguousarray(
        np.asarray(inp["L1_attn_conv_w"])[rows, 0].reshape(8, 128, 4)
        .transpose(1, 0, 2).reshape(128, 32))
    d["convb"] = np.asarray(inp["L1_attn_conv_b"])[rows].reshape(8, 128).T.copy()
    d["bproj"] = _rhs_tiles(np.asarray(inp["L1_attn_B_proj_w"]).T)
    d["cproj"] = _rhs_tiles(np.asarray(inp["L1_attn_C_proj_w"]).T)
    opw = np.asarray(inp["L1_attn_out_proj_w"]).T
    mine, other = opw[rows], opw[slice((1 - h) * DIH, (2 - h) * DIH)]
    d["oproj"] = _lhsT_tiles(np.concatenate([mine, other], 0))

    d["gate"] = _rhs_tiles(np.asarray(inp["L1_ffn_gate_w"]).T)
    wg, wu, wd = (np.asarray(inp["L1_ffn_w_gate"]),
                  np.asarray(inp["L1_ffn_w_up"]),
                  np.asarray(inp["L1_ffn_w_down"]))
    d["wgate"] = np.stack([_lhsT_tiles(wg[e].T) for e in range(NE)])
    d["wup"] = np.stack([_lhsT_tiles(wu[e].T) for e in range(NE)])
    d["wdown"] = np.stack([_lhsT_tiles(wd[e].T) for e in range(NE)])
    return d


INPUT_SPECS = {
    "x_fm": ([8, 128, TOK], F32),
    "rope_cos": ([NTS, 128, HEADS * HD], BF16),
    "rope_sin": ([NTS, 128, HEADS * HD], BF16),
    "ident": ([128, 128], F32),
    "ones_row": ([1, 128], F32),
    "expand2": ([2, 128], F32),
    "flag0": ([128, 1], F32),
    "flag1": ([128, 1], F32),
    "actc": ([128, 8], F32),
    "sel2b": ([2, 128], F32),
    "sel8": ([8, 8 * 128], F32),
    "sel16": ([16, 16 * 128], F32),
    "wqkv": ([8, 128, 3072], F32),
    "wout": ([8, 128, 8 * 128], F32),
    "w1": ([32, 128, 8 * 128], F32),
    "w2": ([32, 128, 8 * 128], F32),
    "h1": ([32, 128, 32 * 128], F32),
    "h2": ([32, 128, 32 * 128], F32),
    "w3": ([8, 128, 32 * 128], F32),
    "s1": ([128, 32], F32),
    "s2": ([128, 32], F32),
    "sh1": ([128, 32], F32),
    "sh2": ([128, 32], F32),
    "s3": ([128, 8], F32),
    "xproj": ([8, 128, 8 * 128], F32),
    "dtproj": ([8, 128, 8 * 128], F32),
    "dtb": ([128, 8], F32),
    "alog": ([128, 8 * NS], F32),
    "dvec": ([128, 8], F32),
    "convw": ([128, 32], F32),
    "convb": ([128, 8], F32),
    "bproj": ([8, 128, 16], F32),
    "cproj": ([8, 128, 16], F32),
    "oproj": ([8, 128, 16 * 128], F32),
    "gate": ([8, 128, 8], F32),
    "wgate": ([NE, 4, 128, 8 * 128], F32),
    "wup": ([NE, 4, 128, 8 * 128], F32),
    "wdown": ([NE, 8, 128, 4 * 128], F32),
}


def build(n_cores=8, stop_after=None, taps=()):
    nc = bass.Bass("TRN2", target_bir_lowering=False, debug=False,
                   detect_race_conditions=False)
    rg = [[i, i + 1] for i in range(0, n_cores, 2)]
    taps = list(taps)

    ins = {k: nc.dram_tensor(k, list(shp), dt, kind="ExternalInput").ap()
           for k, (shp, dt) in INPUT_SPECS.items()}
    out = nc.dram_tensor("out", [8, 128, TOK], F32, kind="ExternalOutput").ap()

    acc_park = nc.dram_tensor("acc_park", [NCH, 8, 128, CH], F32,
                              kind="Internal").ap()
    STREAMW = {"wout": [8, 128, 8 * 128], "w1": [32, 128, 8 * 128],
               "w2": [32, 128, 8 * 128], "h1": [32, 128, 32 * 128],
               "h2": [32, 128, 32 * 128], "w3": [8, 128, 32 * 128],
               "xproj": [8, 128, 8 * 128], "dtproj": [8, 128, 8 * 128],
               "oproj": [8, 128, 16 * 128]}
    wbf = {k: nc.dram_tensor("bf_" + k, shp, BF16, kind="Internal").ap()
           for k, shp in STREAMW.items()}
    MOEW = {"wgate": [NE, 4, 128, 8 * 128], "wup": [NE, 4, 128, 8 * 128],
            "wdown": [NE, 8, 128, 4 * 128]}
    for k, shp in MOEW.items():
        wbf[k] = nc.dram_tensor("bf_" + k, shp, BF16, kind="Internal").ap()
    cast_done = {}
    xn0_park = nc.dram_tensor("xn0_park", [8, 128, TOK], BF16,
                              kind="Internal").ap()
    q2_park = nc.dram_tensor("q2_park", [8, 128, TOK], BF16,
                             kind="Internal").ap()
    kv_in = nc.dram_tensor("kv_in", [128, 520], F32, kind="Internal").ap()
    kv_out = nc.dram_tensor("kv_out", [2, 128, 520], F32, kind="Internal").ap()
    ag_in = [nc.dram_tensor(f"ag_in{c}", [8, 128, CH], BF16, kind="Internal").ap()
             for c in range(NCH)]
    ag_out = [nc.dram_tensor(f"ag_out{c}", [2, 8, 128, CH], BF16,
                             kind="Internal").ap() for c in range(NCH)]
    yag_in = [nc.dram_tensor(f"yag_in{c}", [8, 128, 1024], BF16,
                             kind="Internal").ap() for c in range(NCH)]
    yag_out = [nc.dram_tensor(f"yag_out{c}", [2, 8, 128, 1024], BF16,
                              kind="Internal").ap() for c in range(NCH)]

    live = {}

    def sbuf(name, shape, dtype):
        m = nc.sbuf_tensor(name, list(shape), dtype)
        hdl = m.__enter__()
        live[name] = m
        return hdl

    def sfree(*names):
        for n in names:
            live.pop(n).__exit__(None, None, None)

    def psum_(name, shape):
        m = nc.psum_tensor(name, list(shape), F32)
        hdl = m.__enter__()
        live["psum_" + name] = m
        return hdl

    farena = sbuf("farena", [128, 787], F32)
    scal = farena[:, 0:160]
    onescol = farena[:, 160:161]
    flag0 = farena[:, 161:162]
    flag1 = farena[:, 162:163]
    actc = farena[:, 163:171]
    convw_sb = farena[:, 171:203]
    a_sb = farena[:, 203:331]
    alog_sb = farena[:, 331:459]
    state = farena[:, 459:587]
    halo = farena[:, 587:611]
    logit = farena[:, 611:659]
    identf = farena[:, 659:787]
    barena = sbuf("barena", [128, 480], BF16)
    ident = barena[:, 0:128]
    ones1 = barena[0:1, 128:256]
    exp2 = barena[0:2, 256:384]
    gate_sb = barena[:, 384:448]
    logitb = barena[:, 448:480]
    sel8 = sbuf("sel8s", [8, 8 * 128], BF16)
    sel16 = sbuf("sel16s", [16, 16 * 128], BF16)
    bcw = sbuf("bcw", [128, 256], BF16)
    xchunk = sbuf("xchunk", [128, 8 * CH], F32)
    tmp1 = sbuf("tmp1", [128, 1024], F32)
    tmp2 = sbuf("tmp2", [128, 1024], F32)
    tmp3 = sbuf("tmp3", [128, 1024], F32)
    wbufA = sbuf("wbufA", [128, 16 * 128], BF16)
    wbufB = sbuf("wbufB", [128, 16 * 128], BF16)
    cfm = sbuf("cfm", [8, CH], BF16)
    silb = sbuf("silb", [128, CH], BF16)
    kvsb = sbuf("kvsb", [128, 8 * 65], F32)
    kv2d = sbuf("kv2d", [128, 8 * 128], BF16)
    ks2 = sbuf("ks2", [128, 16], BF16)
    bfrow = sbuf("bfrow", [2, 1024], BF16)

    ps = [psum_(f"psb{i}", [128, 512]) for i in range(4)]
    psW0 = psum_("psW0", [128, 1024])
    psW1 = psum_("psW1", [128, 1024])
    ps.append(psW0[:, 0:512])
    ps.append(psW0[:, 512:1024])
    ps.append(psW1[:, 0:512])
    ps.append(psW1[:, 512:1024])

    class Ctx:
        def __init__(self):
            self.sems, self.cnt = {}, {}

        def sem(self, name):
            m = nc.semaphore(name)
            self.sems[name] = m.__enter__()
            live["sem_" + name] = m
            self.cnt[name] = 0

        def inc(self, instr, name, n=1):
            instr.then_inc(self.sems[name], n)
            self.cnt[name] += n
            return self.cnt[name]

        def wait(self, eng, name, val=None):
            eng.wait_ge(self.sems[name], self.cnt[name] if val is None else val)

    ctx = Ctx()
    for s in ["ld", "cast", "st", "coll", "td", "sc", "ve", "gA", "gB", "gC",
              "gD", "gp", "sA", "sB", "sC", "sD"]:
        ctx.sem(s)

    blk = nc.Block()
    block = blk.__enter__()

    def T(f): block.tensor(f)
    def V(f): block.vector(f)
    def S(f): block.scalar(f)
    def G(f): block.gpsimd(f)
    def Y(f): block.sync(f)

    def barrier():
        names = ["td", "sc", "ve", "ld", "cast", "st", "coll"]
        for engname in ["tensor", "vector", "scalar", "gpsimd", "sync"]:
            def _(e, engname=engname):
                for snm in names:
                    ctx.wait(e, snm)
            getattr(block, engname)(_)

    tap_list = []

    def tap(name, src_ap, shape, dtype):
        if name not in taps:
            return
        tt = nc.dram_tensor("tap_" + name, list(shape), dtype,
                            kind="ExternalOutput").ap()
        tap_list.append(name)

        def _(s_):
            ctx.wait(s_, "td"); ctx.wait(s_, "sc"); ctx.wait(s_, "ve")
            ctx.inc(s_.dma_start(out=tt, in_=src_ap), "st", 16)
        Y(_)

        def _b(v):
            ctx.wait(v, "st")
        V(_b)

    def finish():
        def _(s_):
            ctx.wait(s_, "st")
            ctx.wait(s_, "coll")
        Y(_)
        blk.__exit__(None, None, None)
        for m in reversed(list(live.values())):
            m.__exit__(None, None, None)
        return nc

    wfree = {"A": 0, "B": 0, "C": 0, "D": 0}
    wb = {"A": wbufA, "B": wbufB}
    gsem_hw = {"A": "gA", "B": "gB", "C": "gC", "D": "gD"}
    gsem_sw = {"A": "sA", "B": "sB", "C": "sC", "D": "sD"}
    pair = {"A": "C", "B": "D"}

    pfree = {}

    def mm_stream(strips, n_oc, n_kc, rhs_fn, evac, strips2=None,
                  tensor_first_wait=None, pbanks=(0, 1), pbanks2=(2, 3),
                  cast_gate=None, bufs=("A", "B"), gq=False, steps=None,
                  fast=False):
        gsem = gsem_sw if gq else gsem_hw
        evfree = [None, None]

        def do_oc(oc):
            bi = bufs[0] if oc % 2 == 0 else bufs[1]
            bw = wfree[bi]
            bw2 = wfree[pair[bi]] if strips2 is not None else 0

            def _g(g, oc=oc, bi=bi, bw=bw, bw2=bw2):
                if oc == 0 and cast_gate is not None:
                    ctx.wait(g, "cast", cast_gate)
                if bw:
                    ctx.wait(g, "td", bw)
                ctx.inc(g.dma_start(out=wb[bi][:, 0:n_kc * 128],
                                    in_=strips[oc]), gsem[bi], 16)
                if strips2 is not None:
                    bj = pair[bi]
                    if bw2:
                        ctx.wait(g, "td", bw2)
                    ctx.inc(g.dma_start(out=wb[bj][:, 0:n_kc * 128],
                                        in_=strips2[oc]), gsem[bj], 16)
            (G if gq else Y)(_g)

            gw = ctx.cnt[gsem[bi]]
            gw2 = ctx.cnt[gsem[pair[bi]]] if strips2 is not None else 0
            evw = evfree[oc % 2]

            def _t(t, oc=oc, bi=bi, gw=gw, gw2=gw2, evw=evw):
                if oc == 0 and tensor_first_wait is not None:
                    tensor_first_wait(t)
                if oc < 2:
                    if fast:
                        pw = pfree.get(pbanks[oc])
                        if pw:
                            ctx.wait(t, pw[0], pw[1])
                        if strips2 is not None:
                            pw2 = pfree.get(pbanks2[oc])
                            if pw2:
                                ctx.wait(t, pw2[0], pw2[1])
                    else:
                        ctx.wait(t, "sc")
                        ctx.wait(t, "ve")
                ctx.wait(t, gsem[bi], gw)
                if strips2 is not None:
                    ctx.wait(t, gsem[pair[bi]], gw2)
                if evw is not None:
                    ctx.wait(t, evw[0], evw[1])
                pm = ps[pbanks[oc % 2]]
                pm2 = ps[pbanks2[oc % 2]] if strips2 is not None else None
                for kc in range(n_kc):
                    mm = t.matmul(pm[:, :], wb[bi][:, kc * 128:(kc + 1) * 128],
                                  rhs_fn(kc), start=(kc == 0),
                                  stop=(kc == n_kc - 1))
                    if strips2 is not None:
                        mm = t.matmul(pm2[:, :],
                                      wb[pair[bi]][:, kc * 128:(kc + 1) * 128],
                                      rhs_fn(kc), start=(kc == 0),
                                      stop=(kc == n_kc - 1))
                ctx.inc(mm, "td")
            T(_t)
            wfree[bi] = ctx.cnt["td"]
            if strips2 is not None:
                wfree[pair[bi]] = ctx.cnt["td"]
            evfree[oc % 2] = evac(oc, pbanks[oc % 2],
                                  pbanks2[oc % 2] if strips2 is not None else None)
            pfree[pbanks[oc % 2]] = evfree[oc % 2]
            if strips2 is not None:
                pfree[pbanks2[oc % 2]] = evfree[oc % 2]

        for oc in range(n_oc):
            if steps is None:
                do_oc(oc)
            else:
                steps.append(lambda oc=oc: do_oc(oc))

    def _ph0(s_):
        ctx.inc(s_.dma_start(out=flag0[:, :], in_=ins["flag0"]), "ld", 16)
        ctx.inc(s_.dma_start(out=flag1[:, :], in_=ins["flag1"]), "ld", 16)
        ctx.inc(s_.dma_start(out=scal[:, 0:32], in_=ins["s1"]), "ld", 16)
        ctx.inc(s_.dma_start(out=scal[:, 32:64], in_=ins["s2"]), "ld", 16)
        ctx.inc(s_.dma_start(out=scal[:, 64:96], in_=ins["sh1"]), "ld", 16)
        ctx.inc(s_.dma_start(out=scal[:, 96:128], in_=ins["sh2"]), "ld", 16)
        ctx.inc(s_.dma_start(out=scal[:, 128:136], in_=ins["s3"]), "ld", 16)
        ctx.inc(s_.dma_start(out=scal[:, 136:144], in_=ins["dtb"]), "ld", 16)
        ctx.inc(s_.dma_start(out=scal[:, 144:152], in_=ins["convb"]), "ld", 16)
        ctx.inc(s_.dma_start(out=scal[:, 152:160], in_=ins["dvec"]), "ld", 16)
        ctx.inc(s_.dma_start(out=alog_sb[:, :], in_=ins["alog"]), "ld", 16)
        ctx.inc(s_.dma_start(out=convw_sb[:, :], in_=ins["convw"]), "ld", 16)
        ctx.inc(s_.dma_start(out=actc[:, :], in_=ins["actc"]), "ld", 16)
    Y(_ph0)

    def _ph0g(g):
        ctx.inc(g.dma_start(out=ident[:, :], in_=ins["ident"]), "cast", 16)
        ctx.inc(g.dma_start(out=identf[:, :], in_=ins["ident"]), "cast", 16)
        ctx.inc(g.dma_start(out=ones1[:, :], in_=ins["ones_row"]), "cast", 16)
        ctx.inc(g.dma_start(out=exp2[:, :], in_=ins["expand2"]), "cast", 16)
        ctx.inc(g.dma_start(out=sel8[:, :], in_=ins["sel8"]), "cast", 16)
        ctx.inc(g.dma_start(out=sel16[:, :], in_=ins["sel16"]), "cast", 16)
        for k in range(8):
            ctx.inc(g.dma_start(out=gate_sb[:, k * 8:(k + 1) * 8],
                                in_=ins["gate"][k]), "cast", 16)
            ctx.inc(g.dma_start(out=bcw[:, k * 16:(k + 1) * 16],
                                in_=ins["bproj"][k]), "cast", 16)
            ctx.inc(g.dma_start(out=bcw[:, 128 + k * 16:128 + (k + 1) * 16],
                                in_=ins["cproj"][k]), "cast", 16)
    G(_ph0g)

    def _ph0s(sc):
        ctx.wait(sc, "ld")
        ctx.inc(sc.activation(a_sb[:, :], alog_sb[:, :], AF.Exp), "sc")
    S(_ph0s)

    def _ph0v(v):
        ctx.wait(v, "sc")
        ctx.wait(v, "cast")
        ctx.inc(v.tensor_scalar_mul(a_sb[:, :], a_sb[:, :], -1.0), "ve")
        ctx.inc(v.memset(onescol[:, :], 1.0), "ve")
        ctx.inc(v.memset(state[:, :], 0.0), "ve")
        ctx.inc(v.memset(kv2d[:, :], 0.0), "ve")
        ctx.inc(v.memset(ks2[:, :], 0.0), "ve")
    V(_ph0v)

    if stop_after == "ph0":
        return finish(), tap_list

    def emit_ln(src_fn, dst_fn, nt):
        for k in range(8):
            def _s(sc, k=k):
                ctx.wait(sc, "ld")
                ctx.wait(sc, "td")
                if k == 0:
                    ctx.wait(sc, "ve")
                ctx.inc(sc.activation(tmp1[:, 0:nt], src_fn(k), AF.Square),
                        "sc")
            S(_s)

            def _t(t, k=k):
                ctx.wait(t, "sc")
                ctx.wait(t, "ve")
                t.matmul(ps[0][0:1, 0:nt], onescol[:, :], src_fn(k),
                         start=(k == 0), stop=(k == 7))
                mm = t.matmul(ps[1][0:1, 0:nt], onescol[:, :], tmp1[:, 0:nt],
                              start=(k == 0), stop=(k == 7))
                ctx.inc(mm, "td")
            T(_t)

        def _v(v):
            ctx.wait(v, "td")
            ctx.inc(v.tensor_scalar_mul(tmp2[0:1, 0:nt], ps[0][0:1, 0:nt],
                                        1.0 / D), "ve")
            ctx.inc(v.tensor_scalar_mul(tmp2[0:1, nt:2 * nt],
                                        ps[1][0:1, 0:nt], 1.0 / D), "ve")
            ctx.inc(v.tensor_mul(tmp3[0:1, 0:nt], tmp2[0:1, 0:nt],
                                 tmp2[0:1, 0:nt]), "ve")
            ctx.inc(v.tensor_sub(tmp3[0:1, nt:2 * nt], tmp2[0:1, nt:2 * nt],
                                 tmp3[0:1, 0:nt]), "ve")
        V(_v)

        def _s2(sc):
            ctx.wait(sc, "ve")
            ctx.inc(sc.activation(tmp3[0:1, 0:nt], tmp3[0:1, nt:2 * nt],
                                  AF.Ln, bias=actc[0:1, 0:1]), "sc")
            ctx.inc(sc.activation(tmp3[0:1, nt:2 * nt], tmp3[0:1, 0:nt],
                                  AF.Exp, scale=actc[0:1, 1:2]), "sc")
        S(_s2)

        def _v2(v):
            ctx.wait(v, "sc")
            ctx.inc(v.tensor_mul(tmp1[0:1, 0:nt], tmp2[0:1, 0:nt],
                                 tmp3[0:1, nt:2 * nt]), "ve")
            ctx.inc(v.tensor_copy(bfrow[0:1, 0:nt], tmp3[0:1, nt:2 * nt]),
                    "ve")
            ctx.inc(v.tensor_scalar_mul(bfrow[0:1, nt:2 * nt],
                                        tmp1[0:1, 0:nt], -1.0), "ve")
        V(_v2)

        def _t2(t):
            ctx.wait(t, "ve")
            t.matmul(ps[2][:, 0:nt], ones1[0:1, :], bfrow[0:1, 0:nt],
                     start=True, stop=True)
            mm = t.matmul(ps[3][:, 0:nt], ones1[0:1, :],
                          bfrow[0:1, nt:2 * nt], start=True, stop=True)
            ctx.inc(mm, "td")
        T(_t2)

        for k in range(8):
            def _v3(v, k=k):
                if k == 0:
                    ctx.wait(v, "td")
                ctx.inc(v.tensor_mul(tmp1[:, 0:nt], src_fn(k), ps[2][:, 0:nt]),
                        "ve")
                ctx.inc(v.tensor_add(dst_fn(k), tmp1[:, 0:nt], ps[3][:, 0:nt]),
                        "ve")
            V(_v3)

    kftm = sbuf("kftm", [128, NTS * 1024], BF16)
    vext = sbuf("vext", [128, NTS * 1040], BF16)
    xn0 = sbuf("xn0", [128, 8 * TOK], BF16)
    for c in range(NCH):
        def _lx(s_, c=c):
            ctx.wait(s_, "ve")
            for k in range(8):
                ctx.inc(s_.dma_start(out=xchunk[:, k * CH:(k + 1) * CH],
                                     in_=ins["x_fm"][k][:, c * CH:(c + 1) * CH]),
                        "ld", 16)
        Y(_lx)
        emit_ln(lambda k, c=c: xchunk[:, k * CH:(k + 1) * CH],
                lambda k, c=c: xn0[:, k * TOK + c * CH:k * TOK + (c + 1) * CH],
                CH)

    def _pk0(s_):
        ctx.wait(s_, "ve")
        for k in range(8):
            ctx.inc(s_.dma_start(out=xn0_park[k],
                                 in_=xn0[:, k * TOK:(k + 1) * TOK]), "st", 16)
    Y(_pk0)

    tap("xn0", xn0[:, :], [128, 8 * TOK], BF16)
    if stop_after == "ph1":
        return finish(), tap_list

    rope_sb = sbuf("rope_sb", [128, 2 * 2048], BF16)
    qtm = sbuf("qtm", [128, 1024], F32)
    q2c = sbuf("q2c", [128, 8 * 128], BF16)
    wqq = sbuf("wqq", [128, 8 * 1024], BF16)

    def _gq(g):
        for k in range(8):
            ctx.inc(g.dma_start(out=wqq[:, k * 1024:(k + 1) * 1024],
                                in_=ins["wqkv"][k][:, 0:1024]), "cast", 16)
    G(_gq)

    def emit_rope_feat(ts, psq, dst_ap, sc_pos, sc_neg, td_w=None, ld_w=None):
        rb = (ts % 2) * 2048

        def _v(v, ts=ts, rb=rb, td_w=td_w, ld_w=ld_w):
            ctx.wait(v, "td", td_w)
            ctx.wait(v, "ld", ld_w)
            ctx.wait(v, "sc")
            for half in range(2):
                qp = ps[psq + half][:, :]
                sl = slice(half * 512, (half + 1) * 512)
                ctx.inc(v.tensor_mul(tmp1[:, sl], qp,
                                     rope_sb[:, rb + half * 512:
                                             rb + (half + 1) * 512]), "ve")
                qv = qp.rearrange("p (h two d) -> p h two d", two=2, d=32)
                t2v = tmp2[:, sl].rearrange("p (h two d) -> p h two d",
                                            two=2, d=32)
                ctx.inc(v.tensor_scalar_mul(t2v[:, :, 0, :], qv[:, :, 1, :],
                                            -1.0), "ve")
                ctx.inc(v.tensor_copy(t2v[:, :, 1, :], qv[:, :, 0, :]), "ve")
                ctx.inc(v.tensor_mul(tmp2[:, sl], tmp2[:, sl],
                                     rope_sb[:, rb + 1024 + half * 512:
                                             rb + 1024 + (half + 1) * 512]),
                        "ve")
                ctx.inc(v.tensor_add(tmp1[:, sl], tmp1[:, sl], tmp2[:, sl]),
                        "ve")
        V(_v)

        def _s2(sc, ts=ts):
            ctx.wait(sc, "ve")
            ctx.inc(sc.activation(tmp2[:, 0:1024], tmp1[:, 0:1024], AF.Relu,
                                  scale=sc_neg), "sc")
            ctx.inc(sc.activation(tmp2[:, 0:1024], tmp2[:, 0:1024], AF.Exp,
                                  scale=actc[:, 2:3]), "sc")
            ctx.inc(sc.activation(tmp1[:, 0:1024], tmp1[:, 0:1024], AF.Relu,
                                  scale=sc_pos), "sc")
        S(_s2)

        def _v2(v, ts=ts):
            ctx.wait(v, "sc")
            ctx.inc(v.tensor_add(dst_ap, tmp1[:, 0:1024], tmp2[:, 0:1024]),
                    "ve")
        V(_v2)

    q_ve_done = {}
    q_ld_done = {}
    for ts in range(NTS):
        def _lr(s_, ts=ts):
            if ts >= 2:
                ctx.wait(s_, "ve", q_ve_done[ts - 2])
            ctx.inc(s_.dma_start(out=rope_sb[:, (ts % 2) * 2048:
                                             (ts % 2) * 2048 + 1024],
                                 in_=ins["rope_cos"][ts]), "ld", 16)
            ctx.inc(s_.dma_start(out=rope_sb[:, (ts % 2) * 2048 + 1024:
                                             (ts % 2) * 2048 + 2048],
                                 in_=ins["rope_sin"][ts]), "ld", 16)
        Y(_lr)
        q_ld_done[ts] = ctx.cnt["ld"]

        def _t(t, ts=ts):
            ctx.wait(t, "cast")
            if ts == 0:
                ctx.wait(t, "ve")
                ctx.wait(t, "sc")
            if ts >= 2:
                ctx.wait(t, "ve", q_ve_done[ts - 2])
            pq = 4 if ts % 2 == 0 else 6
            for k in range(8):
                lhs = xn0[:, k * TOK + ts * 128:k * TOK + (ts + 1) * 128]
                for half in range(2):
                    mm = t.matmul(ps[pq + half][:, :], lhs,
                                  wqq[:, k * 1024 + half * 512:
                                      k * 1024 + (half + 1) * 512],
                                  start=(k == 0), stop=(k == 7))
            ctx.inc(mm, "td")
        T(_t)
        td_q = ctx.cnt["td"]

        emit_rope_feat(ts, 4 if ts % 2 == 0 else 6, qtm[:, 0:1024],
                       actc[:, 3:4], actc[:, 4:5], td_w=td_q,
                       ld_w=q_ld_done[ts])
        q_ve_done[ts] = ctx.cnt["ve"]

        for pr in range(8):
            def _tt(t, ts=ts, pr=pr):
                ctx.wait(t, "ve")
                if pr == 0:
                    ctx.wait(t, "st")
                mm = t.transpose(ps[2][:, 0:128],
                                 qtm[:, pr * 128:(pr + 1) * 128], identf[:, :])
                ctx.inc(mm, "td")
            T(_tt)

            def _se(v, ts=ts, pr=pr):
                ctx.wait(v, "td")
                ctx.inc(v.tensor_copy(q2c[:, pr * 128:(pr + 1) * 128],
                                      ps[2][:, 0:128]), "ve")
            V(_se)

        def _pq(s_, ts=ts):
            ctx.wait(s_, "ve")
            for pr in range(8):
                ctx.inc(s_.dma_start(
                    out=q2_park[pr][:, ts * 128:(ts + 1) * 128],
                    in_=q2c[:, pr * 128:(pr + 1) * 128]), "st", 16)
        Y(_pq)

    barrier()
    sfree("wqq")

    if stop_after == "ph1q":
        return finish(), tap_list

    wqkv2 = sbuf("wqkv2", [128, 8 * 2048], BF16)

    cast_wk = [0]

    def _gk(g):
        for k in range(8):
            ctx.inc(g.dma_start(out=wqkv2[:, k * 2048:(k + 1) * 2048],
                                in_=ins["wqkv"][k][:, 1024:3072]), "cast", 16)
        cast_wk[0] = ctx.cnt["cast"]
    G(_gk)

    def _vo(v):
        vv = vext[:, :].rearrange("p (t h c) -> p t h c", t=NTS, h=HEADS)
        ctx.inc(v.memset(vv[:, :, :, 64:65], 1.0), "ve")
    V(_vo)

    k_ve_done = {}
    k_ld_done = {}
    for ts in range(NTS):
        def _lr(s_, ts=ts):
            if ts >= 2:
                ctx.wait(s_, "ve", k_ve_done[ts - 2])
            ctx.inc(s_.dma_start(out=rope_sb[:, (ts % 2) * 2048:
                                             (ts % 2) * 2048 + 1024],
                                 in_=ins["rope_cos"][ts]), "ld", 16)
            ctx.inc(s_.dma_start(out=rope_sb[:, (ts % 2) * 2048 + 1024:
                                             (ts % 2) * 2048 + 2048],
                                 in_=ins["rope_sin"][ts]), "ld", 16)
        Y(_lr)
        k_ld_done[ts] = ctx.cnt["ld"]

        def _t(t, ts=ts):
            ctx.wait(t, "cast", cast_wk[0])
            if ts == 0:
                ctx.wait(t, "ve")
                ctx.wait(t, "sc")
            if ts >= 2:
                ctx.wait(t, "ve", k_ve_done[ts - 2])
            pk, pv = (4, 6) if ts % 2 == 0 else (0, 2)
            for k in range(8):
                lhs = xn0[:, k * TOK + ts * 128:k * TOK + (ts + 1) * 128]
                for half in range(2):
                    t.matmul(ps[pk + half][:, :], lhs,
                             wqkv2[:, k * 2048 + half * 512:
                                   k * 2048 + (half + 1) * 512],
                             start=(k == 0), stop=(k == 7))
                for half in range(2):
                    mm = t.matmul(ps[pv + half][:, :], lhs,
                                  wqkv2[:, k * 2048 + 1024 + half * 512:
                                        k * 2048 + 1024 + (half + 1) * 512],
                                  start=(k == 0), stop=(k == 7))
            ctx.inc(mm, "td")
        T(_t)
        td_k = ctx.cnt["td"]

        def _sv(v, ts=ts):
            ctx.wait(v, "td", td_k)
            pv = 6 if ts % 2 == 0 else 2
            vv = vext[:, ts * 1040:(ts + 1) * 1040].rearrange(
                "p (h c) -> p h c", h=HEADS)
            for half in range(2):
                src = ps[pv + half][:, :].rearrange("p (h c) -> p h c", h=8)
                ctx.inc(v.tensor_copy(vv[:, half * 8:(half + 1) * 8, 0:64],
                                      src), "ve")
        V(_sv)

        emit_rope_feat(ts, 4 if ts % 2 == 0 else 0,
                       kftm[:, ts * 1024:(ts + 1) * 1024],
                       actc[:, 5:6], actc[:, 2:3], td_w=td_k,
                       ld_w=k_ld_done[ts])
        k_ve_done[ts] = ctx.cnt["ve"]

    barrier()
    sfree("wqkv2", "q2c", "qtm", "rope_sb", "xn0")
    tap("kftm", kftm[:, :], [128, NTS * 1024], BF16)
    tap("vext", vext[:, :], [128, NTS * 1040], BF16)
    if stop_after == "ph1k":
        return finish(), tap_list

    for hd_ in range(HEADS):
        bank = hd_ % 2
        po = (hd_ % 2) * 64

        def _t(t, hd_=hd_, bank=bank, po=po):
            ctx.wait(t, "ve")
            for tsl in range(NTS):
                mm = t.matmul(
                    ps[bank][po:po + 64, 0:65],
                    kftm[:, tsl * 1024 + hd_ * 64:tsl * 1024 + (hd_ + 1) * 64],
                    vext[:, tsl * 1040 + hd_ * 65:tsl * 1040 + (hd_ + 1) * 65],
                    start=(tsl == 0), stop=(tsl == NTS - 1),
                    tile_position=(0, po))
            ctx.inc(mm, "td")
        T(_t)

        def _s(v, hd_=hd_, bank=bank, po=po):
            ctx.wait(v, "td")
            ctx.inc(v.tensor_copy(
                kvsb[po:po + 64, (hd_ // 2) * 65:(hd_ // 2 + 1) * 65],
                ps[bank][po:po + 64, 0:65]), "ve")
        V(_s)

    def _st2(s_):
        ctx.wait(s_, "ve")
        ctx.inc(s_.dma_start(out=kv_in, in_=kvsb[:, 0:520]), "st", 16)
    Y(_st2)

    def _cc(g):
        ctx.wait(g, "st")
        ctx.inc(g.collective_compute("AllGather", ALU.bypass,
                                     replica_groups=rg, ins=[kv_in],
                                     outs=[kv_out]), "coll")
        for k in ["wgate", "wup", "wdown"]:
            for e in range(NE):
                ctx.inc(g.dma_start(out=wbf[k][e], in_=ins[k][e]),
                        "cast", 16)
            cast_done[k] = ctx.cnt["cast"]
        for k in ["xproj", "dtproj", "oproj"]:
            ctx.inc(g.dma_start(out=wbf[k], in_=ins[k]), "cast", 16)
            cast_done[k] = ctx.cnt["cast"]
    G(_cc)

    def _rb2(s_):
        ctx.wait(s_, "coll")
        ctx.inc(s_.dma_start(out=tmp1[:, 0:520], in_=kv_out[0]), "ld", 16)
        ctx.inc(s_.dma_start(out=tmp2[:, 0:520], in_=kv_out[1]), "ld", 16)
    Y(_rb2)

    def _vc(v):
        ctx.wait(v, "ld")
        ctx.inc(v.tensor_add(kvsb[:, 0:520], tmp1[:, 0:520], tmp2[:, 0:520]),
                "ve")
        for pr in range(8):
            ctx.inc(v.tensor_copy(kv2d[0:64, pr * 128:pr * 128 + 64],
                                  kvsb[0:64, pr * 65:pr * 65 + 64]), "ve")
            ctx.inc(v.tensor_copy(kv2d[64:128, pr * 128 + 64:pr * 128 + 128],
                                  kvsb[64:128, pr * 65:pr * 65 + 64]), "ve")
            ctx.inc(v.tensor_copy(ks2[0:64, pr * 2:pr * 2 + 1],
                                  kvsb[0:64, pr * 65 + 64:pr * 65 + 65]), "ve")
            ctx.inc(v.tensor_copy(ks2[64:128, pr * 2 + 1:pr * 2 + 2],
                                  kvsb[64:128, pr * 65 + 64:pr * 65 + 65]),
                    "ve")
    V(_vc)

    barrier()
    sfree("vext", "kftm")
    tap("kvsb", kvsb[:, :], [128, 8 * 65], F32)
    if stop_after == "ph2":
        return finish(), tap_list

    wb["C"] = sbuf("wbufC", [128, 32 * 128], BF16)
    wb["D"] = sbuf("wbufD", [128, 32 * 128], BF16)
    xn0c = sbuf("xn0c", [128, 8 * CH], BF16)
    q2cc = sbuf("q2cc", [128, 8 * CH], BF16)
    xn1c = sbuf("xn1c", [128, 8 * CH], BF16)
    ha = sbuf("ha", [128, 32 * CH], BF16)
    hb = sbuf("hb", [128, 32 * CH], BF16)

    for c in range(NCH):
        def _lc(s_, c=c):
            ctx.wait(s_, "ve")
            ctx.wait(s_, "td")
            for k in range(8):
                ctx.inc(s_.dma_start(out=xn0c[:, k * CH:(k + 1) * CH],
                                     in_=xn0_park[k][:, c * CH:(c + 1) * CH]),
                        "ld", 16)
                ctx.inc(s_.dma_start(out=q2cc[:, k * CH:(k + 1) * CH],
                                     in_=q2_park[k][:, c * CH:(c + 1) * CH]),
                        "ld", 16)
        Y(_lc)

        for pr in range(NPAIR):
            def _t(t, c=c, pr=pr):
                ctx.wait(t, "ve")
                ctx.wait(t, "sc")
                if pr == 0:
                    ctx.wait(t, "ld")
                t.matmul(ps[2][0:2, 0:CH], ks2[:, pr * 2:(pr + 1) * 2],
                         q2cc[:, pr * CH:(pr + 1) * CH], start=True, stop=True)
                mm = t.matmul(ps[3][:, 0:CH], kv2d[:, pr * 128:(pr + 1) * 128],
                              q2cc[:, pr * CH:(pr + 1) * CH], start=True,
                              stop=True)
                ctx.inc(mm, "td")
            T(_t)

            def _v(v, c=c, pr=pr):
                ctx.wait(v, "td")
                ctx.inc(v.tensor_scalar_max(tmp1[0:2, 0:CH], ps[2][0:2, 0:CH],
                                            1e-6), "ve")
                ctx.inc(v.reciprocal(tmp1[0:2, CH:2 * CH], tmp1[0:2, 0:CH]),
                        "ve")
                ctx.inc(v.tensor_copy(bfrow[0:2, 0:CH], tmp1[0:2, CH:2 * CH]),
                        "ve")
            V(_v)

            def _t2(t, c=c, pr=pr):
                ctx.wait(t, "ve")
                mm = t.matmul(ps[4][:, 0:CH], exp2[:, :], bfrow[0:2, 0:CH],
                              start=True, stop=True)
                ctx.inc(mm, "td")
            T(_t2)

            def _v2(v, c=c, pr=pr):
                ctx.wait(v, "td")
                ctx.inc(v.tensor_copy(tmp1[:, 0:CH], ps[4][:, 0:CH]), "ve")
                ctx.inc(v.tensor_mul(hb[:, pr * CH:(pr + 1) * CH],
                                     ps[3][:, 0:CH], tmp1[:, 0:CH]), "ve")
            V(_v2)

        def _lx(s_, c=c):
            ctx.wait(s_, "ve")
            ctx.wait(s_, "st")
            for k in range(8):
                ctx.inc(s_.dma_start(out=xchunk[:, k * CH:(k + 1) * CH],
                                     in_=ins["x_fm"][k][:, c * CH:(c + 1) * CH]),
                        "ld", 16)
        Y(_lx)

        def ev_wout(oc, pb, pb2):
            def _v(v, oc=oc, pb=pb):
                ctx.wait(v, "td")
                ctx.wait(v, "ld")
                ctx.inc(v.scalar_tensor_tensor(
                    xchunk[:, oc * CH:(oc + 1) * CH], ps[pb][:, 0:CH], 1.0,
                    xchunk[:, oc * CH:(oc + 1) * CH], ALU.bypass, ALU.add),
                    "ve")
            V(_v)
            return ("ve", ctx.cnt["ve"])

        mm_stream(ins["wout"], 8, 8,
                  lambda kc: hb[:, kc * CH:(kc + 1) * CH], ev_wout,
                  pbanks=(5, 6), gq=True)
        if c == 3:
            tap("attnfm3", hb[:, 0:8 * CH], [128, 8 * CH], BF16)
            tap("accA", xchunk[:, :], [128, 8 * CH], F32)

        def ev_w12(oc, pb, pb2, c=c):
            def _s(sc, oc=oc, pb=pb):
                ctx.wait(sc, "td")
                ctx.wait(sc, "ve")
                ctx.inc(sc.activation(silb[:, 0:CH], ps[pb][:, 0:CH], AF.Silu,
                                      scale=scal[:, oc:oc + 1]), "sc")
            S(_s)

            def _v(v, oc=oc, pb2=pb2):
                ctx.wait(v, "sc")
                ctx.inc(v.tensor_scalar_mul(tmp1[:, 0:CH], ps[pb2][:, 0:CH],
                                            scal[:, 32 + oc:33 + oc]), "ve")
                ctx.inc(v.tensor_mul(ha[:, oc * CH:(oc + 1) * CH],
                                     silb[:, 0:CH], tmp1[:, 0:CH]), "ve")
            V(_v)
            return ("ve", ctx.cnt["ve"])

        def fw(t):
            ctx.wait(t, "ve")
        mm_stream(ins["w1"], 32, 8,
                  lambda kc: xn0c[:, kc * CH:(kc + 1) * CH], ev_w12,
                  strips2=ins["w2"], tensor_first_wait=fw,
                  pbanks=(0, 1), pbanks2=(2, 3), gq=True)

        def ev_h1(oc, pb, pb2):
            def _s(sc, oc=oc, pb=pb):
                ctx.wait(sc, "td")
                ctx.inc(sc.activation(hb[:, oc * CH:(oc + 1) * CH],
                                      ps[pb][:, 0:CH], AF.Gelu,
                                      scale=scal[:, 64 + oc:65 + oc]), "sc")
            S(_s)
            return ("sc", ctx.cnt["sc"])

        if c == 3:
            tap("ha3", ha[:, :], [128, 32 * CH], BF16)

        def fw1(t):
            ctx.wait(t, "ve")
        mm_stream(ins["h1"], 32, 32,
                  lambda kc: ha[:, kc * CH:(kc + 1) * CH], ev_h1,
                  tensor_first_wait=fw1, pbanks=(0, 1),
                  bufs=("C", "D"), gq=True)

        def ev_h2(oc, pb, pb2):
            def _s(sc, oc=oc, pb=pb):
                ctx.wait(sc, "td")
                ctx.inc(sc.activation(ha[:, oc * CH:(oc + 1) * CH],
                                      ps[pb][:, 0:CH], AF.Gelu,
                                      scale=scal[:, 96 + oc:97 + oc]), "sc")
            S(_s)
            return ("sc", ctx.cnt["sc"])

        def fw2(t):
            ctx.wait(t, "sc")
        mm_stream(ins["h2"], 32, 32,
                  lambda kc: hb[:, kc * CH:(kc + 1) * CH], ev_h2,
                  tensor_first_wait=fw2, pbanks=(0, 1),
                  bufs=("C", "D"), gq=True)

        def ev_w3(oc, pb, pb2):
            def _v(v, oc=oc, pb=pb):
                ctx.wait(v, "td")
                ctx.inc(v.scalar_tensor_tensor(
                    xchunk[:, oc * CH:(oc + 1) * CH], ps[pb][:, 0:CH],
                    scal[:, 128 + oc:129 + oc],
                    xchunk[:, oc * CH:(oc + 1) * CH], ALU.mult, ALU.add), "ve")
            V(_v)
            return ("ve", ctx.cnt["ve"])

        def fw3(t):
            ctx.wait(t, "sc")
        mm_stream(ins["w3"], 8, 32,
                  lambda kc: ha[:, kc * CH:(kc + 1) * CH], ev_w3,
                  tensor_first_wait=fw3, pbanks=(0, 1),
                  bufs=("C", "D"), gq=True)

        if c == 3:
            tap("accB", xchunk[:, :], [128, 8 * CH], F32)

        emit_ln(lambda k: xchunk[:, k * CH:(k + 1) * CH],
                lambda k: xn1c[:, k * CH:(k + 1) * CH], CH)

        def _pk(s_, c=c):
            ctx.wait(s_, "ve")
            for k in range(8):
                ctx.inc(s_.dma_start(out=acc_park[c][k],
                                     in_=xchunk[:, k * CH:(k + 1) * CH]),
                        "st", 16)
                ctx.inc(s_.dma_start(out=ag_in[c][k],
                                     in_=xn1c[:, k * CH:(k + 1) * CH]),
                        "st", 16)
        Y(_pk)

        def _cg(g, c=c):
            ctx.wait(g, "st")
            ctx.inc(g.collective_compute("AllGather", ALU.bypass,
                                         replica_groups=rg, ins=[ag_in[c]],
                                         outs=[ag_out[c]]), "coll")
        G(_cg)

    barrier()
    sfree("hb", "ha", "xn1c", "q2cc", "xn0c", "wbufD", "wbufC",
          "bfrow", "ks2", "kv2d", "kvsb")
    del wb["C"], wb["D"]
    tap("acc3", xchunk[:, :], [128, 8 * CH], F32)
    tap("xn1c", xn1c[:, :], [128, 8 * CH], BF16)
    if stop_after == "ph3":
        return finish(), tap_list

    xn1g = sbuf("xn1g", [128, 8 * 1024], BF16)
    dtbuf = sbuf("dtbuf", [128, 8 * 1024], F32)
    xcbuf = sbuf("xcbuf", [128, 8 * 1024], F32)
    xiy = sbuf("xiy", [128, 8 * 1027], F32)
    yown = sbuf("yown", [128, 8 * 1024], BF16)
    tmpa = sbuf("tmpa", [128, 1024], F32)
    brow = sbuf("brow", [16, 2048], BF16)
    mscratch = sbuf("mscratch", [128, 8 * 1024], BF16)
    xn1cR = mscratch[:, 0:8 * CH]
    pe_sb = mscratch[:, 8 * CH:12 * CH]
    cbc = mscratch[:, 12 * CH:13 * CH]
    gsc = sbuf("gsc", [128, CH], F32)
    wb["C"] = sbuf("wbufC2", [128, 8 * 128], BF16)
    wb["D"] = sbuf("wbufD2", [128, 8 * 128], BF16)
    wfree["C"] = 0
    wfree["D"] = 0

    xinner = xiy
    ybuf = xiy

    pending = []

    def make_tail(j):
        steps = []
        yagc = 6 + j
        marks = {}

        def st_ry():
            def _ry(s_, j=j, yagc=yagc):
                ctx.wait(s_, "coll", yagc)
                ctx.wait(s_, "ve")
                for k in range(8):
                    ctx.inc(s_.dma_start(
                        out=mscratch[:, k * 1024:(k + 1) * 1024],
                        in_=yag_out[j][0][k]), "ld", 16)
                    ctx.inc(s_.dma_start(
                        out=xn1g[:, k * 1024:(k + 1) * 1024],
                        in_=yag_out[j][1][k]), "ld", 16)
            Y(_ry)
        steps.append(st_ry)

        for k in range(8):
            def st_vy(k=k):
                def _vy(v, j=j, k=k):
                    ctx.wait(v, "ld")
                    ctx.inc(v.tensor_scalar_mul(
                        tmp3[:, 0:1024], xn1g[:, k * 1024:(k + 1) * 1024],
                        flag1[:, 0:1]), "ve")
                    ctx.inc(v.scalar_tensor_tensor(
                        xn1g[:, k * 1024:(k + 1) * 1024],
                        mscratch[:, k * 1024:(k + 1) * 1024], flag0[:, 0:1],
                        tmp3[:, 0:1024], ALU.mult, ALU.add), "ve")
                V(_vy)
                if k == 7:
                    marks["vy"] = ctx.cnt["ve"]
            steps.append(st_vy)

        def st_ua():
            def _ua(s_, j=j):
                ctx.wait(s_, "ve")
                ctx.wait(s_, "st")
                for k in range(8):
                    ctx.inc(s_.dma_start(out=xchunk[:, k * CH:(k + 1) * CH],
                                         in_=acc_park[j][k]), "ld", 16)
                for k in range(8):
                    ctx.inc(s_.dma_start(out=xn1cR[:, k * CH:(k + 1) * CH],
                                         in_=ag_in[j][k]), "ld", 16)
            Y(_ua)
        steps.append(st_ua)

        def st_gate1():
            ld_moe = ctx.cnt["ld"]
            for t4 in range(4):
                def _t5(t, j=j, t4=t4, ld_moe=ld_moe):
                    ctx.wait(t, "ld", ld_moe)
                    pw = pfree.get(t4 % 2)
                    if pw:
                        ctx.wait(t, pw[0], pw[1])
                    pb = t4 % 2
                    for kc in range(8):
                        mm = t.matmul(ps[pb][:, 0:8],
                                      xn1cR[:, kc * CH + t4 * 128:
                                            kc * CH + (t4 + 1) * 128],
                                      gate_sb[:, kc * 8:(kc + 1) * 8],
                                      start=(kc == 0), stop=(kc == 7))
                    ctx.inc(mm, "td")
                T(_t5)

                def _v5(v, t4=t4):
                    ctx.wait(v, "td")
                    pb = t4 % 2
                    ctx.inc(v.reduce_max(logit[:, 32 + t4:33 + t4],
                                         ps[pb][:, 0:8],
                                         axis=mybir.AxisListType.X), "ve")
                    ctx.wait(v, "ve")
                    ctx.inc(v.tensor_scalar(logit[:, t4 * 8:(t4 + 1) * 8],
                                            ps[pb][:, 0:8],
                                            logit[:, 32 + t4:33 + t4], None,
                                            ALU.is_ge), "ve")
                V(_v5)
                pfree[t4 % 2] = ("ve", ctx.cnt["ve"])
        steps.append(st_gate1)

        def st_gate2():
            def _s6(sc, j=j):
                ctx.wait(sc, "ve")
                ctx.inc(sc.activation(logit[:, 36:40], logit[:, 32:36],
                                      AF.Sigmoid), "sc")
            S(_s6)

            def _v6(v, j=j):
                ctx.wait(v, "sc")
                for t4 in range(4):
                    ctx.inc(v.tensor_scalar_mul(
                        logitb[:, t4 * 8:(t4 + 1) * 8],
                        logit[:, t4 * 8:(t4 + 1) * 8],
                        logit[:, 36 + t4:37 + t4]), "ve")
            V(_v6)

            for t4 in range(4):
                def _t6(t, j=j, t4=t4):
                    ctx.wait(t, "ve")
                    pw = pfree.get(2)
                    if pw:
                        ctx.wait(t, pw[0], pw[1])
                    mm = t.matmul(ps[2][0:8, 0:128],
                                  logitb[:, t4 * 8:(t4 + 1) * 8],
                                  ident[:, :], start=True, stop=True)
                    ctx.inc(mm, "td")
                T(_t6)

                def _s7(v, j=j, t4=t4):
                    ctx.wait(v, "td")
                    ctx.inc(v.tensor_copy(cfm[0:8, t4 * 128:(t4 + 1) * 128],
                                          ps[2][0:8, 0:128]), "ve")
                V(_s7)
                pfree[2] = ("ve", ctx.cnt["ve"])
        steps.append(st_gate2)

        for e in range(NE):
            def st_cbc(e=e, j=j):
                def _t7(t, e=e):
                    ctx.wait(t, "ve")
                    pw = pfree.get(3)
                    if pw:
                        ctx.wait(t, pw[0], pw[1])
                    mm = t.matmul(ps[3][:, 0:CH],
                                  sel8[:, e * 128:(e + 1) * 128],
                                  cfm[0:8, 0:CH], start=True, stop=True)
                    ctx.inc(mm, "td")
                T(_t7)

                def _s8(v, e=e):
                    ctx.wait(v, "td")
                    ctx.inc(v.tensor_copy(cbc[:, 0:CH], ps[3][:, 0:CH]),
                            "ve")
                V(_s8)
                pfree[3] = ("ve", ctx.cnt["ve"])
            steps.append(st_cbc)

            def ev_gu(oc, pb, pb2, e=e):
                def _v(v, oc=oc, pb=pb, pb2=pb2):
                    ctx.wait(v, "td")
                    ctx.inc(v.tensor_mul(gsc[:, 0:CH], ps[pb][:, 0:CH],
                                         cbc[:, 0:CH]), "ve")
                    ctx.inc(v.tensor_mul(tmp2[:, 0:CH], ps[pb2][:, 0:CH],
                                         cbc[:, 0:CH]), "ve")
                V(_v)

                def _s(sc, oc=oc):
                    ctx.wait(sc, "ve")
                    ctx.inc(sc.activation(silb[:, 0:CH], gsc[:, 0:CH],
                                          AF.Silu), "sc")
                S(_s)

                def _v2(v, oc=oc):
                    ctx.wait(v, "sc")
                    ctx.inc(v.tensor_mul(pe_sb[:, oc * CH:(oc + 1) * CH],
                                         silb[:, 0:CH], tmp2[:, 0:CH]),
                            "ve")
                V(_v2)
                if oc == 3:
                    marks["gu", e] = ctx.cnt["ve"]
                return ("ve", ctx.cnt["ve"])

            def fwg(t):
                ctx.wait(t, "ld")
            mm_stream(wbf["wgate"][e], 4, 8,
                      lambda kc: xn1cR[:, kc * CH:(kc + 1) * CH], ev_gu,
                      strips2=wbf["wup"][e], tensor_first_wait=fwg,
                      pbanks=(0, 1), pbanks2=(2, 3),
                      cast_gate=cast_done["wup"] if (j == 0 and e == 0)
                      else None, steps=steps, fast=True)

            def ev_dn(oc, pb, pb2):
                def _v(v, oc=oc, pb=pb):
                    ctx.wait(v, "td")
                    ctx.inc(v.scalar_tensor_tensor(
                        xchunk[:, oc * CH:(oc + 1) * CH], ps[pb][:, 0:CH],
                        1.0, xchunk[:, oc * CH:(oc + 1) * CH],
                        ALU.bypass, ALU.add), "ve")
                V(_v)
                return ("ve", ctx.cnt["ve"])

            def fwd(t, e=e):
                ctx.wait(t, "ve", marks["gu", e])
            mm_stream(wbf["wdown"][e], 8, 4,
                      lambda kc: pe_sb[:, kc * CH:(kc + 1) * CH], ev_dn,
                      tensor_first_wait=fwd, pbanks=(0, 1),
                      cast_gate=cast_done["wdown"] if (j == 0 and e == 0)
                      else None, steps=steps, fast=True)

        def yrhs(kc, g_):
            if kc < 8:
                return yown[:, kc * 1024 + g_ * CH:kc * 1024 + (g_ + 1) * CH]
            return xn1g[:, (kc - 8) * 1024 + g_ * CH:
                        (kc - 8) * 1024 + (g_ + 1) * CH]

        for g_ in range(2):
            sel = flag1 if g_ == 0 else flag0

            def ev_op(oc, pb, pb2, j=j, g_=g_, sel=sel):
                def _v(v, oc=oc, pb=pb, sel=sel):
                    ctx.wait(v, "td")
                    ctx.inc(v.tensor_scalar_mul(
                        tmp3[:, 0:CH], ps[pb][:, 0:CH], sel[:, 0:1]),
                        "ve")
                    ctx.inc(v.scalar_tensor_tensor(
                        xchunk[:, oc * CH:(oc + 1) * CH], tmp3[:, 0:CH],
                        1.0, xchunk[:, oc * CH:(oc + 1) * CH],
                        ALU.bypass, ALU.add), "ve")
                V(_v)
                return ("ve", ctx.cnt["ve"])

            def fwo(t):
                ctx.wait(t, "ve", marks["vy"])
            mm_stream(wbf["oproj"], 8, 16,
                      lambda kc, g_=g_: yrhs(kc, g_), ev_op,
                      tensor_first_wait=fwo, pbanks=(0, 1),
                      cast_gate=cast_done["oproj"] if (j == 0 and g_ == 0)
                      else None, steps=steps, fast=True)

        def st_so():
            def _so(s_, j=j):
                ctx.wait(s_, "ve")
                for k in range(8):
                    ctx.inc(s_.dma_start(out=out[k][:, j * CH:(j + 1) * CH],
                                         in_=xchunk[:, k * CH:(k + 1) * CH]),
                            "st", 16)
            Y(_so)
        steps.append(st_so)
        return steps

    for j in range(NCH):
        agc = 2 + j

        def _rb(s_, j=j, agc=agc):
            ctx.wait(s_, "coll", agc)
            ctx.wait(s_, "ve")
            ctx.wait(s_, "td")
            for k in range(8):
                ctx.inc(s_.dma_start(out=xn1g[:, k * 1024:k * 1024 + CH],
                                     in_=ag_out[j][0][k]), "ld", 16)
                ctx.inc(s_.dma_start(out=xn1g[:, k * 1024 + CH:(k + 1) * 1024],
                                     in_=ag_out[j][1][k]), "ld", 16)
        Y(_rb)

        def _vh(v, j=j):
            ctx.wait(v, "sc")
            if j > 0:
                for k in range(8):
                    ctx.inc(v.tensor_copy(
                        halo[:, k * 3:(k + 1) * 3],
                        xinner[:, k * 1027 + 1024:k * 1027 + 1027]), "ve")
            else:
                ctx.inc(v.memset(halo[:, :], 0.0), "ve")
            for k in range(8):
                ctx.inc(v.tensor_copy(xinner[:, k * 1027:k * 1027 + 3],
                                      halo[:, k * 3:(k + 1) * 3]), "ve")
        V(_vh)

        for g_ in range(2):
            def ev_xpg(oc, pb, pb2, j=j, g_=g_):
                def _s(v, oc=oc, pb=pb):
                    ctx.wait(v, "td")
                    ctx.inc(v.tensor_copy(
                        xinner[:, oc * 1027 + 3 + g_ * CH:
                               oc * 1027 + 3 + (g_ + 1) * CH],
                        ps[pb][:, 0:CH]), "ve")
                V(_s)
                return ("ve", ctx.cnt["ve"])

            def fwx(t, j=j, g_=g_):
                ctx.wait(t, "ld")
            mm_stream(wbf["xproj"], 8, 8,
                      lambda kc, g_=g_: xn1g[:, kc * 1024 + g_ * CH:
                                             kc * 1024 + (g_ + 1) * CH],
                      ev_xpg, tensor_first_wait=fwx, pbanks=(0, 1),
                      cast_gate=cast_done["xproj"] if (j == 0 and g_ == 0)
                      else None, fast=True)

        for k in range(8):
            def _v(v, j=j, k=k):
                ctx.wait(v, "sc")
                ctx.inc(v.tensor_scalar_mul(
                    xcbuf[:, k * 1024:(k + 1) * 1024],
                    xinner[:, k * 1027 + 0:k * 1027 + 1024],
                    convw_sb[:, k * 4:k * 4 + 1]), "ve")
                for kk in range(1, 4):
                    ctx.inc(v.scalar_tensor_tensor(
                        xcbuf[:, k * 1024:(k + 1) * 1024],
                        xinner[:, k * 1027 + kk:k * 1027 + 1024 + kk],
                        convw_sb[:, k * 4 + kk:k * 4 + kk + 1],
                        xcbuf[:, k * 1024:(k + 1) * 1024],
                        ALU.mult, ALU.add), "ve")
            V(_v)

            def _s(sc, j=j, k=k):
                ctx.wait(sc, "ve")
                ctx.inc(sc.activation(xcbuf[:, k * 1024:(k + 1) * 1024],
                                      xcbuf[:, k * 1024:(k + 1) * 1024],
                                      AF.Silu, bias=scal[:, 144 + k:145 + k]),
                        "sc")
            S(_s)

        for g_ in range(2):
            def ev_dt(oc, pb, pb2, j=j, g_=g_):
                def _s(sc, oc=oc, pb=pb):
                    ctx.wait(sc, "td")
                    dst = dtbuf[:, oc * 1024 + g_ * CH:
                                oc * 1024 + (g_ + 1) * CH]
                    ctx.inc(sc.activation(dst, ps[pb][:, 0:CH], AF.Exp,
                                          bias=scal[:, 136 + oc:137 + oc]),
                            "sc")
                S(_s)
                return ("sc", ctx.cnt["sc"])

            def fwd2(t):
                ctx.wait(t, "ld")
            mm_stream(wbf["dtproj"], 8, 8,
                      lambda kc, g_=g_: xn1g[:, kc * 1024 + g_ * CH:
                                             kc * 1024 + (g_ + 1) * CH],
                      ev_dt, tensor_first_wait=fwd2, pbanks=(0, 1),
                      cast_gate=cast_done["dtproj"] if (j == 0 and g_ == 0)
                      else None, fast=True)

        def _sp2(sc, j=j):
            for oc in range(8):
                for gg in range(2):
                    dst = dtbuf[:, oc * 1024 + gg * CH:
                                oc * 1024 + (gg + 1) * CH]
                    ctx.inc(sc.activation(dst, dst, AF.Ln,
                                          bias=actc[:, 5:6]), "sc")
        S(_sp2)
        sc_dt = ctx.cnt["sc"]

        def _tbc(t, j=j):
            ctx.wait(t, "cast")
            ctx.wait(t, "sc")
            for g_ in range(2):
                for kc in range(8):
                    t.matmul(ps[4 + g_][0:16, 0:CH],
                             bcw[:, kc * 16:(kc + 1) * 16],
                             xn1g[:, kc * 1024 + g_ * CH:
                                  kc * 1024 + (g_ + 1) * CH],
                             start=(kc == 0), stop=(kc == 7))
                for kc in range(8):
                    mm = t.matmul(ps[6 + g_][0:16, 0:CH],
                                  bcw[:, 128 + kc * 16:128 + (kc + 1) * 16],
                                  xn1g[:, kc * 1024 + g_ * CH:
                                       kc * 1024 + (g_ + 1) * CH],
                                  start=(kc == 0), stop=(kc == 7))
            ctx.inc(mm, "td")
        T(_tbc)

        def _sbc(v, j=j):
            ctx.wait(v, "td")
            for g_ in range(2):
                ctx.inc(v.tensor_copy(brow[0:16, g_ * CH:(g_ + 1) * CH],
                                      ps[4 + g_][0:16, 0:CH]), "ve")
                ctx.inc(v.tensor_copy(
                    brow[0:16, 1024 + g_ * CH:1024 + (g_ + 1) * CH],
                    ps[6 + g_][0:16, 0:CH]), "ve")
        V(_sbc)

        for k in range(8):
            def _v(v, j=j, k=k):
                ctx.wait(v, "sc")
                ctx.inc(v.tensor_scalar_mul(
                    ybuf[:, k * 1027:k * 1027 + 1024],
                    xcbuf[:, k * 1024:(k + 1) * 1024],
                    scal[:, 152 + k:153 + k]), "ve")
                ctx.inc(v.tensor_mul(xcbuf[:, k * 1024:(k + 1) * 1024],
                                     xcbuf[:, k * 1024:(k + 1) * 1024],
                                     dtbuf[:, k * 1024:(k + 1) * 1024]), "ve")
            V(_v)

        ve_after = {}
        sc_after = {}
        abufs = [tmp1, tmpa]
        n_steps = len(pending)
        done_steps = 0

        def emit_sa(s_i, k0):
            i0 = s_i * 8 + k0
            ab0 = abufs[i0 % 2]

            def _sa(sc, s_i=s_i, k0=k0, i0=i0, ab0=ab0):
                if i0 >= 2:
                    ctx.wait(sc, "ve", ve_after[i0 - 2])
                ctx.inc(sc.activation(
                    ab0[:, 0:1024], dtbuf[:, k0 * 1024:(k0 + 1) * 1024],
                    AF.Exp,
                    scale=a_sb[:, k0 * 16 + s_i:k0 * 16 + s_i + 1]), "sc")
                sc_after[i0] = ctx.cnt["sc"]
            S(_sa)

        for s_i in range(NS):
            ve_bc = ve_after.get(s_i * 8 - 1)

            def _tb(t, j=j, s_i=s_i, ve_bc=ve_bc):
                if ve_bc is None:
                    ctx.wait(t, "ve")
                else:
                    ctx.wait(t, "ve", ve_bc)
                for g_ in range(2):
                    t.matmul(psW0[:, g_ * CH:(g_ + 1) * CH],
                             sel16[:, s_i * 128:(s_i + 1) * 128],
                             brow[0:16, g_ * CH:(g_ + 1) * CH],
                             start=True, stop=True)
                    mm = t.matmul(psW1[:, g_ * CH:(g_ + 1) * CH],
                                  sel16[:, s_i * 128:(s_i + 1) * 128],
                                  brow[0:16, 1024 + g_ * CH:
                                       1024 + (g_ + 1) * CH],
                                  start=True, stop=True)
                ctx.inc(mm, "td")
            T(_tb)
            td_tb = ctx.cnt["td"]

            emit_sa(s_i, 0)
            emit_sa(s_i, 1)

            for k in range(8):
                col = s_i * 8 + k
                i = s_i * 8 + k
                ab = abufs[i % 2]

                def _vs(v, j=j, s_i=s_i, k=k, col=col, i=i, ab=ab,
                        td_tb=td_tb):
                    ctx.wait(v, "sc", sc_after[i])
                    if k == 0:
                        if s_i == NS - 1:
                            ctx.wait(v, "td")
                        else:
                            ctx.wait(v, "td", td_tb)
                    ctx.inc(v.tensor_mul(tmp2[:, 0:1024],
                                         xcbuf[:, k * 1024:(k + 1) * 1024],
                                         psW0[:, 0:1024]), "ve")
                    ctx.inc(v.tensor_tensor_scan(tmp3[:, 0:1024],
                                                 ab[:, 0:1024],
                                                 tmp2[:, 0:1024],
                                                 state[:, col:col + 1],
                                                 ALU.mult, ALU.add), "ve")
                    ctx.wait(v, "ve")
                    ctx.inc(v.tensor_copy(state[:, col:col + 1],
                                          tmp3[:, 1023:1024]), "ve")
                    ctx.inc(v.tensor_mul(tmp2[:, 0:1024], tmp3[:, 0:1024],
                                         psW1[:, 0:1024]), "ve")
                    if s_i < NS - 1:
                        ctx.inc(v.tensor_add(
                            ybuf[:, k * 1027:k * 1027 + 1024],
                            ybuf[:, k * 1027:k * 1027 + 1024],
                            tmp2[:, 0:1024]), "ve")
                    else:
                        ctx.inc(v.tensor_add(
                            yown[:, k * 1024:(k + 1) * 1024],
                            ybuf[:, k * 1027:k * 1027 + 1024],
                            tmp2[:, 0:1024]), "ve")
                V(_vs)
                ve_after[i] = ctx.cnt["ve"]

                if k + 2 < 8:
                    emit_sa(s_i, k + 2)

                slot = s_i * 8 + k
                while pending and done_steps * 120 < n_steps * (slot + 1):
                    pending.pop(0)()
                    done_steps += 1

        while pending:
            pending.pop(0)()

        def _sy(s_, j=j):
            ctx.wait(s_, "ve")
            for k in range(8):
                ctx.inc(s_.dma_start(out=yag_in[j][k],
                                     in_=yown[:, k * 1024:(k + 1) * 1024]),
                        "st", 16)
        Y(_sy)

        def _cy(g, j=j):
            ctx.wait(g, "st")
            ctx.inc(g.collective_compute("AllGather", ALU.bypass,
                                         replica_groups=rg, ins=[yag_in[j]],
                                         outs=[yag_out[j]]), "coll")
        G(_cy)

        pending = make_tail(j)

    while pending:
        pending.pop(0)()

    tap("dtbuf", dtbuf[:, :], [128, 8 * 1024], F32)
    tap("xcbuf", xcbuf[:, :], [128, 8 * 1024], F32)
    tap("yown", yown[:, :], [128, 8 * 1024], BF16)
    return finish(), tap_list


def kernel(**inputs):
    nc, _ = build(8)
    in_maps = [prep_core_inputs(c, inputs) for c in range(8)]
    res = run_bass_kernel_spmd(nc, in_maps, core_ids=list(range(8)))
    full = np.zeros((B, T, D), dtype=np.float32)
    for core in range(8):
        b, h = core // 2, core % 2
        o = np.asarray(res.results[core]["out"]).reshape(D, NCH, CH)
        for c in range(NCH):
            g = 2 * c + h
            full[b, g * CH:(g + 1) * CH, :] = o[:, c, :].T
    return full


# version 22
# speedup vs baseline: 1.0787x; 1.0106x over previous
import numpy as np
import ml_dtypes
import concourse.bass as bass
import concourse.mybir as mybir
from concourse.bass_utils import run_bass_kernel_spmd

F32 = mybir.dt.float32
BF16 = mybir.dt.bfloat16
AF = mybir.ActivationFunctionType
ALU = mybir.AluOpType

B, T, D = 4, 4096, 1024
TOK, NCH, CH, NTS = 2048, 4, 512, 16
HEADS, HD, NPAIR = 16, 64, 8
DH = 4096
DI, DIH, NS = 2048, 1024, 16
NE, IE = 8, 512
SCALE = HD ** (-0.5)
EPS = 1e-5


def _lhsT_tiles(wT):
    K, M = wT.shape
    return np.ascontiguousarray(
        wT.reshape(K // 128, 128, M // 128, 128).transpose(2, 1, 0, 3)
        .reshape(M // 128, 128, (K // 128) * 128))


def _rhs_tiles(w):
    K, N = w.shape
    return np.ascontiguousarray(w.reshape(K // 128, 128, N))


def prep_core_inputs(core, inp):
    b, h = core // 2, core % 2
    own_g = [2 * c + h for c in range(NCH)]
    d = {}
    x = np.asarray(inp["L0_attn_x"])[b]
    xo = np.concatenate([x[g * CH:(g + 1) * CH] for g in own_g], 0)
    d["x_fm"] = np.ascontiguousarray(xo.T).reshape(D // 128, 128, TOK)

    pos = np.concatenate([np.arange(g * CH, (g + 1) * CH) for g in own_g])
    inv = 1.0 / (500000.0 ** (np.arange(0, HD, 2) / HD))
    fr = pos[:, None] * inv[None, :]
    emb = np.concatenate([fr, fr], axis=1)
    d["rope_cos"] = np.tile(np.cos(emb), (1, HEADS)).astype(
        ml_dtypes.bfloat16).reshape(NTS, 128, HEADS * HD)
    d["rope_sin"] = np.tile(np.sin(emb), (1, HEADS)).astype(
        ml_dtypes.bfloat16).reshape(NTS, 128, HEADS * HD)

    d["ident"] = np.eye(128, dtype=np.float32)
    d["ones_row"] = np.ones((1, 128), dtype=np.float32)
    ex2 = np.zeros((2, 128), dtype=np.float32)
    ex2[0, :64] = 1.0
    ex2[1, 64:] = 1.0
    d["expand2"] = ex2
    d["flag0"] = np.full((128, 1), 1.0 if h == 1 else 0.0, np.float32)
    d["flag1"] = np.full((128, 1), 1.0 if h == 0 else 0.0, np.float32)
    ac = np.zeros((128, 8), np.float32)
    ac[:, 0], ac[:, 1], ac[:, 2] = EPS, -0.5, -1.0
    ac[:, 3], ac[:, 4], ac[:, 5] = SCALE, -SCALE, 1.0
    d["actc"] = ac
    s2b = np.zeros((2, 128), np.float32); s2b[1, :] = 1.0
    d["sel2b"] = s2b
    s8 = np.zeros((8, 8 * 128), np.float32)
    for e in range(8):
        s8[e, e * 128:(e + 1) * 128] = 1.0
    d["sel8"] = s8
    s16 = np.zeros((16, 16 * 128), np.float32)
    for si in range(16):
        s16[si, si * 128:(si + 1) * 128] = 1.0
    d["sel16"] = s16

    d["wqkv"] = _rhs_tiles(np.asarray(inp["L0_attn_W_qkv"]))
    d["wout"] = _lhsT_tiles(np.asarray(inp["L0_attn_W_out"]))

    d["w1"] = _lhsT_tiles(np.asarray(inp["L0_ffn_w1_w"]).T)
    d["w2"] = _lhsT_tiles(np.asarray(inp["L0_ffn_w2_w"]).T)
    d["h1"] = _lhsT_tiles(np.asarray(inp["L0_ffn_h1_w"]).T)
    d["h2"] = _lhsT_tiles(np.asarray(inp["L0_ffn_h2_w"]).T)
    d["w3"] = _lhsT_tiles(np.asarray(inp["L0_ffn_w3_w"]).T)
    d["s1"] = np.asarray(inp["L0_ffn_w1_s"]).reshape(32, 128).T.copy()
    d["s2"] = np.asarray(inp["L0_ffn_w2_s"]).reshape(32, 128).T.copy()
    d["sh1"] = np.asarray(inp["L0_ffn_h1_s"]).reshape(32, 128).T.copy()
    d["sh2"] = np.asarray(inp["L0_ffn_h2_s"]).reshape(32, 128).T.copy()
    d["s3"] = np.asarray(inp["L0_ffn_w3_s"]).reshape(8, 128).T.copy()

    rows = slice(h * DIH, (h + 1) * DIH)
    d["xproj"] = _lhsT_tiles(np.asarray(inp["L1_attn_x_proj_w"])[rows].T)
    d["dtproj"] = _lhsT_tiles(np.asarray(inp["L1_attn_dt_proj_w"])[rows].T)
    d["dtb"] = np.asarray(inp["L1_attn_dt_proj_b"])[rows].reshape(8, 128).T.copy()
    d["alog"] = np.ascontiguousarray(
        np.asarray(inp["L1_attn_A_log"])[rows].reshape(8, 128, NS)
        .transpose(1, 0, 2).reshape(128, 8 * NS))
    d["dvec"] = np.asarray(inp["L1_attn_D"])[rows].reshape(8, 128).T.copy()
    d["convw"] = np.ascontiguousarray(
        np.asarray(inp["L1_attn_conv_w"])[rows, 0].reshape(8, 128, 4)
        .transpose(1, 0, 2).reshape(128, 32))
    d["convb"] = np.asarray(inp["L1_attn_conv_b"])[rows].reshape(8, 128).T.copy()
    d["bproj"] = _rhs_tiles(np.asarray(inp["L1_attn_B_proj_w"]).T)
    d["cproj"] = _rhs_tiles(np.asarray(inp["L1_attn_C_proj_w"]).T)
    opw = np.asarray(inp["L1_attn_out_proj_w"]).T
    mine, other = opw[rows], opw[slice((1 - h) * DIH, (2 - h) * DIH)]
    d["oproj"] = _lhsT_tiles(np.concatenate([mine, other], 0))

    d["gate"] = _rhs_tiles(np.asarray(inp["L1_ffn_gate_w"]).T)
    wg, wu, wd = (np.asarray(inp["L1_ffn_w_gate"]),
                  np.asarray(inp["L1_ffn_w_up"]),
                  np.asarray(inp["L1_ffn_w_down"]))
    d["wgate"] = np.stack([_lhsT_tiles(wg[e].T) for e in range(NE)])
    d["wup"] = np.stack([_lhsT_tiles(wu[e].T) for e in range(NE)])
    d["wdown"] = np.stack([_lhsT_tiles(wd[e].T) for e in range(NE)])
    return d


INPUT_SPECS = {
    "x_fm": ([8, 128, TOK], F32),
    "rope_cos": ([NTS, 128, HEADS * HD], BF16),
    "rope_sin": ([NTS, 128, HEADS * HD], BF16),
    "ident": ([128, 128], F32),
    "ones_row": ([1, 128], F32),
    "expand2": ([2, 128], F32),
    "flag0": ([128, 1], F32),
    "flag1": ([128, 1], F32),
    "actc": ([128, 8], F32),
    "sel2b": ([2, 128], F32),
    "sel8": ([8, 8 * 128], F32),
    "sel16": ([16, 16 * 128], F32),
    "wqkv": ([8, 128, 3072], F32),
    "wout": ([8, 128, 8 * 128], F32),
    "w1": ([32, 128, 8 * 128], F32),
    "w2": ([32, 128, 8 * 128], F32),
    "h1": ([32, 128, 32 * 128], F32),
    "h2": ([32, 128, 32 * 128], F32),
    "w3": ([8, 128, 32 * 128], F32),
    "s1": ([128, 32], F32),
    "s2": ([128, 32], F32),
    "sh1": ([128, 32], F32),
    "sh2": ([128, 32], F32),
    "s3": ([128, 8], F32),
    "xproj": ([8, 128, 8 * 128], F32),
    "dtproj": ([8, 128, 8 * 128], F32),
    "dtb": ([128, 8], F32),
    "alog": ([128, 8 * NS], F32),
    "dvec": ([128, 8], F32),
    "convw": ([128, 32], F32),
    "convb": ([128, 8], F32),
    "bproj": ([8, 128, 16], F32),
    "cproj": ([8, 128, 16], F32),
    "oproj": ([8, 128, 16 * 128], F32),
    "gate": ([8, 128, 8], F32),
    "wgate": ([NE, 4, 128, 8 * 128], F32),
    "wup": ([NE, 4, 128, 8 * 128], F32),
    "wdown": ([NE, 8, 128, 4 * 128], F32),
}


def build(n_cores=8, stop_after=None, taps=()):
    nc = bass.Bass("TRN2", target_bir_lowering=False, debug=False,
                   detect_race_conditions=False)
    rg = [[i, i + 1] for i in range(0, n_cores, 2)]
    taps = list(taps)

    ins = {k: nc.dram_tensor(k, list(shp), dt, kind="ExternalInput").ap()
           for k, (shp, dt) in INPUT_SPECS.items()}
    out = nc.dram_tensor("out", [8, 128, TOK], F32, kind="ExternalOutput").ap()

    acc_park = nc.dram_tensor("acc_park", [NCH, 8, 128, CH], F32,
                              kind="Internal").ap()
    STREAMW = {"wout": [8, 128, 8 * 128], "w1": [32, 128, 8 * 128],
               "w2": [32, 128, 8 * 128], "h1": [32, 128, 32 * 128],
               "h2": [32, 128, 32 * 128], "w3": [8, 128, 32 * 128],
               "xproj": [8, 128, 8 * 128], "dtproj": [8, 128, 8 * 128],
               "oproj": [8, 128, 16 * 128]}
    wbf = {k: nc.dram_tensor("bf_" + k, shp, BF16, kind="Internal").ap()
           for k, shp in STREAMW.items()}
    MOEW = {"wgate": [NE, 4, 128, 8 * 128], "wup": [NE, 4, 128, 8 * 128],
            "wdown": [NE, 8, 128, 4 * 128]}
    for k, shp in MOEW.items():
        wbf[k] = nc.dram_tensor("bf_" + k, shp, BF16, kind="Internal").ap()
    cast_done = {}
    xn0_park = nc.dram_tensor("xn0_park", [8, 128, TOK], BF16,
                              kind="Internal").ap()
    q2_park = nc.dram_tensor("q2_park", [8, 128, TOK], BF16,
                             kind="Internal").ap()
    kv_in = nc.dram_tensor("kv_in", [128, 520], F32, kind="Internal").ap()
    kv_out = nc.dram_tensor("kv_out", [2, 128, 520], F32, kind="Internal").ap()
    ag_in = [nc.dram_tensor(f"ag_in{c}", [8, 128, CH], BF16, kind="Internal").ap()
             for c in range(NCH)]
    ag_out = [nc.dram_tensor(f"ag_out{c}", [2, 8, 128, CH], BF16,
                             kind="Internal").ap() for c in range(NCH)]
    yag_in = [nc.dram_tensor(f"yag_in{c}", [8, 128, 1024], BF16,
                             kind="Internal").ap() for c in range(NCH)]
    yag_out = [nc.dram_tensor(f"yag_out{c}", [2, 8, 128, 1024], BF16,
                              kind="Internal").ap() for c in range(NCH)]

    live = {}

    def sbuf(name, shape, dtype):
        m = nc.sbuf_tensor(name, list(shape), dtype)
        hdl = m.__enter__()
        live[name] = m
        return hdl

    def sfree(*names):
        for n in names:
            live.pop(n).__exit__(None, None, None)

    def psum_(name, shape):
        m = nc.psum_tensor(name, list(shape), F32)
        hdl = m.__enter__()
        live["psum_" + name] = m
        return hdl

    farena = sbuf("farena", [128, 787], F32)
    scal = farena[:, 0:160]
    onescol = farena[:, 160:161]
    flag0 = farena[:, 161:162]
    flag1 = farena[:, 162:163]
    actc = farena[:, 163:171]
    convw_sb = farena[:, 171:203]
    a_sb = farena[:, 203:331]
    alog_sb = farena[:, 331:459]
    state = farena[:, 459:587]
    halo = farena[:, 587:611]
    logit = farena[:, 611:659]
    identf = farena[:, 659:787]
    barena = sbuf("barena", [128, 480], BF16)
    ident = barena[:, 0:128]
    ones1 = barena[0:1, 128:256]
    exp2 = barena[0:2, 256:384]
    gate_sb = barena[:, 384:448]
    logitb = barena[:, 448:480]
    sel8 = sbuf("sel8s", [8, 8 * 128], BF16)
    sel16 = sbuf("sel16s", [16, 16 * 128], BF16)
    bcw = sbuf("bcw", [128, 256], BF16)
    xchunk = sbuf("xchunk", [128, 8 * CH], F32)
    tmp1 = sbuf("tmp1", [128, 1024], F32)
    tmp2 = sbuf("tmp2", [128, 1024], F32)
    tmp3 = sbuf("tmp3", [128, 1024], F32)
    wbufA = sbuf("wbufA", [128, 16 * 128], BF16)
    wbufB = sbuf("wbufB", [128, 16 * 128], BF16)
    cfm = sbuf("cfm", [8, CH], BF16)
    silb = sbuf("silb", [128, CH], BF16)
    kvsb = sbuf("kvsb", [128, 8 * 65], F32)
    kv2d = sbuf("kv2d", [128, 8 * 128], BF16)
    ks2 = sbuf("ks2", [128, 16], BF16)
    bfrow = sbuf("bfrow", [2, 1024], BF16)

    ps = [psum_(f"psb{i}", [128, 512]) for i in range(4)]
    psW0 = psum_("psW0", [128, 1024])
    psW1 = psum_("psW1", [128, 1024])
    ps.append(psW0[:, 0:512])
    ps.append(psW0[:, 512:1024])
    ps.append(psW1[:, 0:512])
    ps.append(psW1[:, 512:1024])

    class Ctx:
        def __init__(self):
            self.sems, self.cnt = {}, {}

        def sem(self, name):
            m = nc.semaphore(name)
            self.sems[name] = m.__enter__()
            live["sem_" + name] = m
            self.cnt[name] = 0

        def inc(self, instr, name, n=1):
            instr.then_inc(self.sems[name], n)
            self.cnt[name] += n
            return self.cnt[name]

        def wait(self, eng, name, val=None):
            eng.wait_ge(self.sems[name], self.cnt[name] if val is None else val)

    ctx = Ctx()
    for s in ["ld", "cast", "st", "coll", "td", "sc", "ve", "gA", "gB", "gC",
              "gD", "gp", "sA", "sB", "sC", "sD"]:
        ctx.sem(s)

    blk = nc.Block()
    block = blk.__enter__()

    def T(f): block.tensor(f)
    def V(f): block.vector(f)
    def S(f): block.scalar(f)
    def G(f): block.gpsimd(f)
    def Y(f): block.sync(f)

    def barrier():
        names = ["td", "sc", "ve", "ld", "cast", "st", "coll"]
        for engname in ["tensor", "vector", "scalar", "gpsimd", "sync"]:
            def _(e, engname=engname):
                for snm in names:
                    ctx.wait(e, snm)
            getattr(block, engname)(_)

    tap_list = []

    def tap(name, src_ap, shape, dtype):
        if name not in taps:
            return
        tt = nc.dram_tensor("tap_" + name, list(shape), dtype,
                            kind="ExternalOutput").ap()
        tap_list.append(name)

        def _(s_):
            ctx.wait(s_, "td"); ctx.wait(s_, "sc"); ctx.wait(s_, "ve")
            ctx.inc(s_.dma_start(out=tt, in_=src_ap), "st", 16)
        Y(_)

        def _b(v):
            ctx.wait(v, "st")
        V(_b)

    def finish():
        def _(s_):
            ctx.wait(s_, "st")
            ctx.wait(s_, "coll")
        Y(_)
        blk.__exit__(None, None, None)
        for m in reversed(list(live.values())):
            m.__exit__(None, None, None)
        return nc

    wfree = {"A": 0, "B": 0, "C": 0, "D": 0}
    wb = {"A": wbufA, "B": wbufB}
    gsem_hw = {"A": "gA", "B": "gB", "C": "gC", "D": "gD"}
    gsem_sw = {"A": "sA", "B": "sB", "C": "sC", "D": "sD"}
    pair = {"A": "C", "B": "D"}

    pfree = {}

    def mm_stream(strips, n_oc, n_kc, rhs_fn, evac, strips2=None,
                  tensor_first_wait=None, pbanks=(0, 1), pbanks2=(2, 3),
                  cast_gate=None, bufs=("A", "B"), gq=False, steps=None,
                  fast=False):
        gsem = gsem_sw if gq else gsem_hw
        evfree = [None, None]

        def do_oc(oc):
            bi = bufs[0] if oc % 2 == 0 else bufs[1]
            bw = wfree[bi]
            bw2 = wfree[pair[bi]] if strips2 is not None else 0

            def _g(g, oc=oc, bi=bi, bw=bw, bw2=bw2):
                if oc == 0 and cast_gate is not None:
                    ctx.wait(g, "cast", cast_gate)
                if bw:
                    ctx.wait(g, "td", bw)
                ctx.inc(g.dma_start(out=wb[bi][:, 0:n_kc * 128],
                                    in_=strips[oc]), gsem[bi], 16)
                if strips2 is not None:
                    bj = pair[bi]
                    if bw2:
                        ctx.wait(g, "td", bw2)
                    ctx.inc(g.dma_start(out=wb[bj][:, 0:n_kc * 128],
                                        in_=strips2[oc]), gsem[bj], 16)
            (G if gq else Y)(_g)

            gw = ctx.cnt[gsem[bi]]
            gw2 = ctx.cnt[gsem[pair[bi]]] if strips2 is not None else 0
            evw = evfree[oc % 2]

            def _t(t, oc=oc, bi=bi, gw=gw, gw2=gw2, evw=evw):
                if oc == 0 and tensor_first_wait is not None:
                    tensor_first_wait(t)
                if oc < 2:
                    if fast:
                        pw = pfree.get(pbanks[oc])
                        if pw:
                            ctx.wait(t, pw[0], pw[1])
                        if strips2 is not None:
                            pw2 = pfree.get(pbanks2[oc])
                            if pw2:
                                ctx.wait(t, pw2[0], pw2[1])
                    else:
                        ctx.wait(t, "sc")
                        ctx.wait(t, "ve")
                ctx.wait(t, gsem[bi], gw)
                if strips2 is not None:
                    ctx.wait(t, gsem[pair[bi]], gw2)
                if evw is not None:
                    ctx.wait(t, evw[0], evw[1])
                pm = ps[pbanks[oc % 2]]
                pm2 = ps[pbanks2[oc % 2]] if strips2 is not None else None
                for kc in range(n_kc):
                    mm = t.matmul(pm[:, :], wb[bi][:, kc * 128:(kc + 1) * 128],
                                  rhs_fn(kc), start=(kc == 0),
                                  stop=(kc == n_kc - 1))
                    if strips2 is not None:
                        mm = t.matmul(pm2[:, :],
                                      wb[pair[bi]][:, kc * 128:(kc + 1) * 128],
                                      rhs_fn(kc), start=(kc == 0),
                                      stop=(kc == n_kc - 1))
                ctx.inc(mm, "td")
            T(_t)
            wfree[bi] = ctx.cnt["td"]
            if strips2 is not None:
                wfree[pair[bi]] = ctx.cnt["td"]
            evfree[oc % 2] = evac(oc, pbanks[oc % 2],
                                  pbanks2[oc % 2] if strips2 is not None else None)
            pfree[pbanks[oc % 2]] = evfree[oc % 2]
            if strips2 is not None:
                pfree[pbanks2[oc % 2]] = evfree[oc % 2]

        for oc in range(n_oc):
            if steps is None:
                do_oc(oc)
            else:
                steps.append(lambda oc=oc: do_oc(oc))

    def _ph0(s_):
        ctx.inc(s_.dma_start(out=flag0[:, :], in_=ins["flag0"]), "ld", 16)
        ctx.inc(s_.dma_start(out=flag1[:, :], in_=ins["flag1"]), "ld", 16)
        ctx.inc(s_.dma_start(out=scal[:, 0:32], in_=ins["s1"]), "ld", 16)
        ctx.inc(s_.dma_start(out=scal[:, 32:64], in_=ins["s2"]), "ld", 16)
        ctx.inc(s_.dma_start(out=scal[:, 64:96], in_=ins["sh1"]), "ld", 16)
        ctx.inc(s_.dma_start(out=scal[:, 96:128], in_=ins["sh2"]), "ld", 16)
        ctx.inc(s_.dma_start(out=scal[:, 128:136], in_=ins["s3"]), "ld", 16)
        ctx.inc(s_.dma_start(out=scal[:, 136:144], in_=ins["dtb"]), "ld", 16)
        ctx.inc(s_.dma_start(out=scal[:, 144:152], in_=ins["convb"]), "ld", 16)
        ctx.inc(s_.dma_start(out=scal[:, 152:160], in_=ins["dvec"]), "ld", 16)
        ctx.inc(s_.dma_start(out=alog_sb[:, :], in_=ins["alog"]), "ld", 16)
        ctx.inc(s_.dma_start(out=convw_sb[:, :], in_=ins["convw"]), "ld", 16)
        ctx.inc(s_.dma_start(out=actc[:, :], in_=ins["actc"]), "ld", 16)
    Y(_ph0)

    def _ph0g(g):
        ctx.inc(g.dma_start(out=ident[:, :], in_=ins["ident"]), "cast", 16)
        ctx.inc(g.dma_start(out=identf[:, :], in_=ins["ident"]), "cast", 16)
        ctx.inc(g.dma_start(out=ones1[:, :], in_=ins["ones_row"]), "cast", 16)
        ctx.inc(g.dma_start(out=exp2[:, :], in_=ins["expand2"]), "cast", 16)
        ctx.inc(g.dma_start(out=sel8[:, :], in_=ins["sel8"]), "cast", 16)
        ctx.inc(g.dma_start(out=sel16[:, :], in_=ins["sel16"]), "cast", 16)
        for k in range(8):
            ctx.inc(g.dma_start(out=gate_sb[:, k * 8:(k + 1) * 8],
                                in_=ins["gate"][k]), "cast", 16)
            ctx.inc(g.dma_start(out=bcw[:, k * 16:(k + 1) * 16],
                                in_=ins["bproj"][k]), "cast", 16)
            ctx.inc(g.dma_start(out=bcw[:, 128 + k * 16:128 + (k + 1) * 16],
                                in_=ins["cproj"][k]), "cast", 16)
    G(_ph0g)

    def _ph0s(sc):
        ctx.wait(sc, "ld")
        ctx.inc(sc.activation(a_sb[:, :], alog_sb[:, :], AF.Exp), "sc")
    S(_ph0s)

    def _ph0v(v):
        ctx.wait(v, "sc")
        ctx.wait(v, "cast")
        ctx.inc(v.tensor_scalar_mul(a_sb[:, :], a_sb[:, :], -1.0), "ve")
        ctx.inc(v.memset(onescol[:, :], 1.0), "ve")
        ctx.inc(v.memset(state[:, :], 0.0), "ve")
        ctx.inc(v.memset(kv2d[:, :], 0.0), "ve")
        ctx.inc(v.memset(ks2[:, :], 0.0), "ve")
    V(_ph0v)

    if stop_after == "ph0":
        return finish(), tap_list

    def emit_ln(src_fn, dst_fn, nt):
        for k in range(8):
            def _s(sc, k=k):
                ctx.wait(sc, "ld")
                ctx.wait(sc, "td")
                if k == 0:
                    ctx.wait(sc, "ve")
                ctx.inc(sc.activation(tmp1[:, 0:nt], src_fn(k), AF.Square),
                        "sc")
            S(_s)

            def _t(t, k=k):
                ctx.wait(t, "sc")
                ctx.wait(t, "ve")
                t.matmul(ps[0][0:1, 0:nt], onescol[:, :], src_fn(k),
                         start=(k == 0), stop=(k == 7))
                mm = t.matmul(ps[1][0:1, 0:nt], onescol[:, :], tmp1[:, 0:nt],
                              start=(k == 0), stop=(k == 7))
                ctx.inc(mm, "td")
            T(_t)

        def _v(v):
            ctx.wait(v, "td")
            ctx.inc(v.tensor_scalar_mul(tmp2[0:1, 0:nt], ps[0][0:1, 0:nt],
                                        1.0 / D), "ve")
            ctx.inc(v.tensor_scalar_mul(tmp2[0:1, nt:2 * nt],
                                        ps[1][0:1, 0:nt], 1.0 / D), "ve")
            ctx.inc(v.tensor_mul(tmp3[0:1, 0:nt], tmp2[0:1, 0:nt],
                                 tmp2[0:1, 0:nt]), "ve")
            ctx.inc(v.tensor_sub(tmp3[0:1, nt:2 * nt], tmp2[0:1, nt:2 * nt],
                                 tmp3[0:1, 0:nt]), "ve")
        V(_v)

        def _s2(sc):
            ctx.wait(sc, "ve")
            ctx.inc(sc.activation(tmp3[0:1, 0:nt], tmp3[0:1, nt:2 * nt],
                                  AF.Ln, bias=actc[0:1, 0:1]), "sc")
            ctx.inc(sc.activation(tmp3[0:1, nt:2 * nt], tmp3[0:1, 0:nt],
                                  AF.Exp, scale=actc[0:1, 1:2]), "sc")
        S(_s2)

        def _v2(v):
            ctx.wait(v, "sc")
            ctx.inc(v.tensor_mul(tmp1[0:1, 0:nt], tmp2[0:1, 0:nt],
                                 tmp3[0:1, nt:2 * nt]), "ve")
            ctx.inc(v.tensor_copy(bfrow[0:1, 0:nt], tmp3[0:1, nt:2 * nt]),
                    "ve")
            ctx.inc(v.tensor_scalar_mul(bfrow[0:1, nt:2 * nt],
                                        tmp1[0:1, 0:nt], -1.0), "ve")
        V(_v2)

        def _t2(t):
            ctx.wait(t, "ve")
            t.matmul(ps[2][:, 0:nt], ones1[0:1, :], bfrow[0:1, 0:nt],
                     start=True, stop=True)
            mm = t.matmul(ps[3][:, 0:nt], ones1[0:1, :],
                          bfrow[0:1, nt:2 * nt], start=True, stop=True)
            ctx.inc(mm, "td")
        T(_t2)

        for k in range(8):
            def _v3(v, k=k):
                if k == 0:
                    ctx.wait(v, "td")
                ctx.inc(v.tensor_mul(tmp1[:, 0:nt], src_fn(k), ps[2][:, 0:nt]),
                        "ve")
                ctx.inc(v.tensor_add(dst_fn(k), tmp1[:, 0:nt], ps[3][:, 0:nt]),
                        "ve")
            V(_v3)

    kftm = sbuf("kftm", [128, NTS * 1024], BF16)
    vext = sbuf("vext", [128, NTS * 1040], BF16)
    xn0 = sbuf("xn0", [128, 8 * TOK], BF16)
    for c in range(NCH):
        def _lx(s_, c=c):
            ctx.wait(s_, "ve")
            for k in range(8):
                ctx.inc(s_.dma_start(out=xchunk[:, k * CH:(k + 1) * CH],
                                     in_=ins["x_fm"][k][:, c * CH:(c + 1) * CH]),
                        "ld", 16)
        Y(_lx)
        emit_ln(lambda k, c=c: xchunk[:, k * CH:(k + 1) * CH],
                lambda k, c=c: xn0[:, k * TOK + c * CH:k * TOK + (c + 1) * CH],
                CH)

    def _pk0(s_):
        ctx.wait(s_, "ve")
        for k in range(8):
            ctx.inc(s_.dma_start(out=xn0_park[k],
                                 in_=xn0[:, k * TOK:(k + 1) * TOK]), "st", 16)
    Y(_pk0)

    tap("xn0", xn0[:, :], [128, 8 * TOK], BF16)
    if stop_after == "ph1":
        return finish(), tap_list

    rope_sb = sbuf("rope_sb", [128, 2 * 2048], BF16)
    qtm = sbuf("qtm", [128, 1024], F32)
    q2c = sbuf("q2c", [128, 8 * 128], BF16)
    wqq = sbuf("wqq", [128, 8 * 1024], BF16)

    def _gq(g):
        for k in range(8):
            ctx.inc(g.dma_start(out=wqq[:, k * 1024:(k + 1) * 1024],
                                in_=ins["wqkv"][k][:, 0:1024]), "cast", 16)
    G(_gq)

    def emit_rope_feat(ts, psq, dst_ap, sc_pos, sc_neg, td_w=None, ld_w=None):
        rb = (ts % 2) * 2048

        def _v(v, ts=ts, rb=rb, td_w=td_w, ld_w=ld_w):
            ctx.wait(v, "td", td_w)
            ctx.wait(v, "ld", ld_w)
            ctx.wait(v, "sc")
            for half in range(2):
                qp = ps[psq + half][:, :]
                sl = slice(half * 512, (half + 1) * 512)
                ctx.inc(v.tensor_mul(tmp1[:, sl], qp,
                                     rope_sb[:, rb + half * 512:
                                             rb + (half + 1) * 512]), "ve")
                qv = qp.rearrange("p (h two d) -> p h two d", two=2, d=32)
                t2v = tmp2[:, sl].rearrange("p (h two d) -> p h two d",
                                            two=2, d=32)
                ctx.inc(v.tensor_scalar_mul(t2v[:, :, 0, :], qv[:, :, 1, :],
                                            -1.0), "ve")
                ctx.inc(v.tensor_copy(t2v[:, :, 1, :], qv[:, :, 0, :]), "ve")
                ctx.inc(v.tensor_mul(tmp2[:, sl], tmp2[:, sl],
                                     rope_sb[:, rb + 1024 + half * 512:
                                             rb + 1024 + (half + 1) * 512]),
                        "ve")
                ctx.inc(v.tensor_add(tmp1[:, sl], tmp1[:, sl], tmp2[:, sl]),
                        "ve")
        V(_v)

        def _s2(sc, ts=ts):
            ctx.wait(sc, "ve")
            ctx.inc(sc.activation(tmp2[:, 0:1024], tmp1[:, 0:1024], AF.Relu,
                                  scale=sc_neg), "sc")
            ctx.inc(sc.activation(tmp2[:, 0:1024], tmp2[:, 0:1024], AF.Exp,
                                  scale=actc[:, 2:3]), "sc")
            ctx.inc(sc.activation(tmp1[:, 0:1024], tmp1[:, 0:1024], AF.Relu,
                                  scale=sc_pos), "sc")
        S(_s2)

        def _v2(v, ts=ts):
            ctx.wait(v, "sc")
            ctx.inc(v.tensor_add(dst_ap, tmp1[:, 0:1024], tmp2[:, 0:1024]),
                    "ve")
        V(_v2)

    q_ve_done = {}
    q_ld_done = {}
    for ts in range(NTS):
        def _lr(s_, ts=ts):
            if ts >= 2:
                ctx.wait(s_, "ve", q_ve_done[ts - 2])
            ctx.inc(s_.dma_start(out=rope_sb[:, (ts % 2) * 2048:
                                             (ts % 2) * 2048 + 1024],
                                 in_=ins["rope_cos"][ts]), "ld", 16)
            ctx.inc(s_.dma_start(out=rope_sb[:, (ts % 2) * 2048 + 1024:
                                             (ts % 2) * 2048 + 2048],
                                 in_=ins["rope_sin"][ts]), "ld", 16)
        Y(_lr)
        q_ld_done[ts] = ctx.cnt["ld"]

        def _t(t, ts=ts):
            ctx.wait(t, "cast")
            if ts == 0:
                ctx.wait(t, "ve")
                ctx.wait(t, "sc")
            if ts >= 2:
                ctx.wait(t, "ve", q_ve_done[ts - 2])
            pq = 4 if ts % 2 == 0 else 6
            for k in range(8):
                lhs = xn0[:, k * TOK + ts * 128:k * TOK + (ts + 1) * 128]
                for half in range(2):
                    mm = t.matmul(ps[pq + half][:, :], lhs,
                                  wqq[:, k * 1024 + half * 512:
                                      k * 1024 + (half + 1) * 512],
                                  start=(k == 0), stop=(k == 7))
            ctx.inc(mm, "td")
        T(_t)
        td_q = ctx.cnt["td"]

        emit_rope_feat(ts, 4 if ts % 2 == 0 else 6, qtm[:, 0:1024],
                       actc[:, 3:4], actc[:, 4:5], td_w=td_q,
                       ld_w=q_ld_done[ts])
        q_ve_done[ts] = ctx.cnt["ve"]

        for pr in range(8):
            def _tt(t, ts=ts, pr=pr):
                ctx.wait(t, "ve")
                if pr == 0:
                    ctx.wait(t, "st")
                mm = t.transpose(ps[2][:, 0:128],
                                 qtm[:, pr * 128:(pr + 1) * 128], identf[:, :])
                ctx.inc(mm, "td")
            T(_tt)

            def _se(v, ts=ts, pr=pr):
                ctx.wait(v, "td")
                ctx.inc(v.tensor_copy(q2c[:, pr * 128:(pr + 1) * 128],
                                      ps[2][:, 0:128]), "ve")
            V(_se)

        def _pq(s_, ts=ts):
            ctx.wait(s_, "ve")
            for pr in range(8):
                ctx.inc(s_.dma_start(
                    out=q2_park[pr][:, ts * 128:(ts + 1) * 128],
                    in_=q2c[:, pr * 128:(pr + 1) * 128]), "st", 16)
        Y(_pq)

    barrier()
    sfree("wqq")

    if stop_after == "ph1q":
        return finish(), tap_list

    wqkv2 = sbuf("wqkv2", [128, 8 * 2048], BF16)

    cast_wk = [0]

    def _gk(g):
        for k in range(8):
            ctx.inc(g.dma_start(out=wqkv2[:, k * 2048:(k + 1) * 2048],
                                in_=ins["wqkv"][k][:, 1024:3072]), "cast", 16)
        cast_wk[0] = ctx.cnt["cast"]
    G(_gk)

    def _vo(v):
        vv = vext[:, :].rearrange("p (t h c) -> p t h c", t=NTS, h=HEADS)
        ctx.inc(v.memset(vv[:, :, :, 64:65], 1.0), "ve")
    V(_vo)

    k_ve_done = {}
    k_ld_done = {}
    for ts in range(NTS):
        def _lr(s_, ts=ts):
            if ts >= 2:
                ctx.wait(s_, "ve", k_ve_done[ts - 2])
            ctx.inc(s_.dma_start(out=rope_sb[:, (ts % 2) * 2048:
                                             (ts % 2) * 2048 + 1024],
                                 in_=ins["rope_cos"][ts]), "ld", 16)
            ctx.inc(s_.dma_start(out=rope_sb[:, (ts % 2) * 2048 + 1024:
                                             (ts % 2) * 2048 + 2048],
                                 in_=ins["rope_sin"][ts]), "ld", 16)
        Y(_lr)
        k_ld_done[ts] = ctx.cnt["ld"]

        def _t(t, ts=ts):
            ctx.wait(t, "cast", cast_wk[0])
            if ts == 0:
                ctx.wait(t, "ve")
                ctx.wait(t, "sc")
            if ts >= 2:
                ctx.wait(t, "ve", k_ve_done[ts - 2])
            pk, pv = (4, 6) if ts % 2 == 0 else (0, 2)
            for k in range(8):
                lhs = xn0[:, k * TOK + ts * 128:k * TOK + (ts + 1) * 128]
                for half in range(2):
                    t.matmul(ps[pk + half][:, :], lhs,
                             wqkv2[:, k * 2048 + half * 512:
                                   k * 2048 + (half + 1) * 512],
                             start=(k == 0), stop=(k == 7))
                for half in range(2):
                    mm = t.matmul(ps[pv + half][:, :], lhs,
                                  wqkv2[:, k * 2048 + 1024 + half * 512:
                                        k * 2048 + 1024 + (half + 1) * 512],
                                  start=(k == 0), stop=(k == 7))
            ctx.inc(mm, "td")
        T(_t)
        td_k = ctx.cnt["td"]

        def _sv(v, ts=ts):
            ctx.wait(v, "td", td_k)
            pv = 6 if ts % 2 == 0 else 2
            vv = vext[:, ts * 1040:(ts + 1) * 1040].rearrange(
                "p (h c) -> p h c", h=HEADS)
            for half in range(2):
                src = ps[pv + half][:, :].rearrange("p (h c) -> p h c", h=8)
                ctx.inc(v.tensor_copy(vv[:, half * 8:(half + 1) * 8, 0:64],
                                      src), "ve")
        V(_sv)

        emit_rope_feat(ts, 4 if ts % 2 == 0 else 0,
                       kftm[:, ts * 1024:(ts + 1) * 1024],
                       actc[:, 5:6], actc[:, 2:3], td_w=td_k,
                       ld_w=k_ld_done[ts])
        k_ve_done[ts] = ctx.cnt["ve"]

    barrier()
    sfree("wqkv2", "q2c", "qtm", "rope_sb", "xn0")
    tap("kftm", kftm[:, :], [128, NTS * 1024], BF16)
    tap("vext", vext[:, :], [128, NTS * 1040], BF16)
    if stop_after == "ph1k":
        return finish(), tap_list

    for hd_ in range(HEADS):
        bank = hd_ % 2
        po = (hd_ % 2) * 64

        def _t(t, hd_=hd_, bank=bank, po=po):
            ctx.wait(t, "ve")
            for tsl in range(NTS):
                mm = t.matmul(
                    ps[bank][po:po + 64, 0:65],
                    kftm[:, tsl * 1024 + hd_ * 64:tsl * 1024 + (hd_ + 1) * 64],
                    vext[:, tsl * 1040 + hd_ * 65:tsl * 1040 + (hd_ + 1) * 65],
                    start=(tsl == 0), stop=(tsl == NTS - 1),
                    tile_position=(0, po))
            ctx.inc(mm, "td")
        T(_t)

        def _s(v, hd_=hd_, bank=bank, po=po):
            ctx.wait(v, "td")
            ctx.inc(v.tensor_copy(
                kvsb[po:po + 64, (hd_ // 2) * 65:(hd_ // 2 + 1) * 65],
                ps[bank][po:po + 64, 0:65]), "ve")
        V(_s)

    def _st2(s_):
        ctx.wait(s_, "ve")
        ctx.inc(s_.dma_start(out=kv_in, in_=kvsb[:, 0:520]), "st", 16)
    Y(_st2)

    def _cc(g):
        ctx.wait(g, "st")
        ctx.inc(g.collective_compute("AllGather", ALU.bypass,
                                     replica_groups=rg, ins=[kv_in],
                                     outs=[kv_out]), "coll")
        for k in ["wgate", "wup", "wdown"]:
            for e in range(NE):
                ctx.inc(g.dma_start(out=wbf[k][e], in_=ins[k][e]),
                        "cast", 16)
            cast_done[k] = ctx.cnt["cast"]
        for k in ["xproj", "dtproj", "oproj"]:
            ctx.inc(g.dma_start(out=wbf[k], in_=ins[k]), "cast", 16)
            cast_done[k] = ctx.cnt["cast"]
    G(_cc)

    def _rb2(s_):
        ctx.wait(s_, "coll")
        ctx.inc(s_.dma_start(out=tmp1[:, 0:520], in_=kv_out[0]), "ld", 16)
        ctx.inc(s_.dma_start(out=tmp2[:, 0:520], in_=kv_out[1]), "ld", 16)
    Y(_rb2)

    def _vc(v):
        ctx.wait(v, "ld")
        ctx.inc(v.tensor_add(kvsb[:, 0:520], tmp1[:, 0:520], tmp2[:, 0:520]),
                "ve")
        for pr in range(8):
            ctx.inc(v.tensor_copy(kv2d[0:64, pr * 128:pr * 128 + 64],
                                  kvsb[0:64, pr * 65:pr * 65 + 64]), "ve")
            ctx.inc(v.tensor_copy(kv2d[64:128, pr * 128 + 64:pr * 128 + 128],
                                  kvsb[64:128, pr * 65:pr * 65 + 64]), "ve")
            ctx.inc(v.tensor_copy(ks2[0:64, pr * 2:pr * 2 + 1],
                                  kvsb[0:64, pr * 65 + 64:pr * 65 + 65]), "ve")
            ctx.inc(v.tensor_copy(ks2[64:128, pr * 2 + 1:pr * 2 + 2],
                                  kvsb[64:128, pr * 65 + 64:pr * 65 + 65]),
                    "ve")
    V(_vc)

    barrier()
    sfree("vext", "kftm")
    tap("kvsb", kvsb[:, :], [128, 8 * 65], F32)
    if stop_after == "ph2":
        return finish(), tap_list

    wb["C"] = sbuf("wbufC", [128, 32 * 128], BF16)
    wb["D"] = sbuf("wbufD", [128, 32 * 128], BF16)
    xn0c = sbuf("xn0c", [128, 8 * CH], BF16)
    q2cc = sbuf("q2cc", [128, 8 * CH], BF16)
    xn1c = sbuf("xn1c", [128, 8 * CH], BF16)
    ha = sbuf("ha", [128, 32 * CH], BF16)
    hb = sbuf("hb", [128, 32 * CH], BF16)

    for c in range(NCH):
        def _lc(s_, c=c):
            ctx.wait(s_, "ve")
            ctx.wait(s_, "td")
            for k in range(8):
                ctx.inc(s_.dma_start(out=xn0c[:, k * CH:(k + 1) * CH],
                                     in_=xn0_park[k][:, c * CH:(c + 1) * CH]),
                        "ld", 16)
                ctx.inc(s_.dma_start(out=q2cc[:, k * CH:(k + 1) * CH],
                                     in_=q2_park[k][:, c * CH:(c + 1) * CH]),
                        "ld", 16)
        Y(_lc)

        for pr in range(NPAIR):
            def _t(t, c=c, pr=pr):
                ctx.wait(t, "ve")
                ctx.wait(t, "sc")
                if pr == 0:
                    ctx.wait(t, "ld")
                t.matmul(ps[2][0:2, 0:CH], ks2[:, pr * 2:(pr + 1) * 2],
                         q2cc[:, pr * CH:(pr + 1) * CH], start=True, stop=True)
                mm = t.matmul(ps[3][:, 0:CH], kv2d[:, pr * 128:(pr + 1) * 128],
                              q2cc[:, pr * CH:(pr + 1) * CH], start=True,
                              stop=True)
                ctx.inc(mm, "td")
            T(_t)

            def _v(v, c=c, pr=pr):
                ctx.wait(v, "td")
                ctx.inc(v.tensor_scalar_max(tmp1[0:2, 0:CH], ps[2][0:2, 0:CH],
                                            1e-6), "ve")
                ctx.inc(v.reciprocal(tmp1[0:2, CH:2 * CH], tmp1[0:2, 0:CH]),
                        "ve")
                ctx.inc(v.tensor_copy(bfrow[0:2, 0:CH], tmp1[0:2, CH:2 * CH]),
                        "ve")
            V(_v)

            def _t2(t, c=c, pr=pr):
                ctx.wait(t, "ve")
                mm = t.matmul(ps[4][:, 0:CH], exp2[:, :], bfrow[0:2, 0:CH],
                              start=True, stop=True)
                ctx.inc(mm, "td")
            T(_t2)

            def _v2(v, c=c, pr=pr):
                ctx.wait(v, "td")
                ctx.inc(v.tensor_copy(tmp1[:, 0:CH], ps[4][:, 0:CH]), "ve")
                ctx.inc(v.tensor_mul(hb[:, pr * CH:(pr + 1) * CH],
                                     ps[3][:, 0:CH], tmp1[:, 0:CH]), "ve")
            V(_v2)

        def _lx(s_, c=c):
            ctx.wait(s_, "ve")
            ctx.wait(s_, "st")
            for k in range(8):
                ctx.inc(s_.dma_start(out=xchunk[:, k * CH:(k + 1) * CH],
                                     in_=ins["x_fm"][k][:, c * CH:(c + 1) * CH]),
                        "ld", 16)
        Y(_lx)

        def ev_wout(oc, pb, pb2):
            def _v(v, oc=oc, pb=pb):
                ctx.wait(v, "td")
                ctx.wait(v, "ld")
                ctx.inc(v.scalar_tensor_tensor(
                    xchunk[:, oc * CH:(oc + 1) * CH], ps[pb][:, 0:CH], 1.0,
                    xchunk[:, oc * CH:(oc + 1) * CH], ALU.bypass, ALU.add),
                    "ve")
            V(_v)
            return ("ve", ctx.cnt["ve"])

        mm_stream(ins["wout"], 8, 8,
                  lambda kc: hb[:, kc * CH:(kc + 1) * CH], ev_wout,
                  pbanks=(5, 6), gq=True)
        if c == 3:
            tap("attnfm3", hb[:, 0:8 * CH], [128, 8 * CH], BF16)
            tap("accA", xchunk[:, :], [128, 8 * CH], F32)

        def ev_w12(oc, pb, pb2, c=c):
            def _s(sc, oc=oc, pb=pb):
                ctx.wait(sc, "td")
                ctx.wait(sc, "ve")
                ctx.inc(sc.activation(silb[:, 0:CH], ps[pb][:, 0:CH], AF.Silu,
                                      scale=scal[:, oc:oc + 1]), "sc")
            S(_s)

            def _v(v, oc=oc, pb2=pb2):
                ctx.wait(v, "sc")
                ctx.inc(v.tensor_scalar_mul(tmp1[:, 0:CH], ps[pb2][:, 0:CH],
                                            scal[:, 32 + oc:33 + oc]), "ve")
                ctx.inc(v.tensor_mul(ha[:, oc * CH:(oc + 1) * CH],
                                     silb[:, 0:CH], tmp1[:, 0:CH]), "ve")
            V(_v)
            return ("ve", ctx.cnt["ve"])

        def fw(t):
            ctx.wait(t, "ve")
        mm_stream(ins["w1"], 32, 8,
                  lambda kc: xn0c[:, kc * CH:(kc + 1) * CH], ev_w12,
                  strips2=ins["w2"], tensor_first_wait=fw,
                  pbanks=(0, 1), pbanks2=(2, 3), gq=True)

        def ev_h1(oc, pb, pb2):
            def _s(sc, oc=oc, pb=pb):
                ctx.wait(sc, "td")
                ctx.inc(sc.activation(hb[:, oc * CH:(oc + 1) * CH],
                                      ps[pb][:, 0:CH], AF.Gelu,
                                      scale=scal[:, 64 + oc:65 + oc]), "sc")
            S(_s)
            return ("sc", ctx.cnt["sc"])

        if c == 3:
            tap("ha3", ha[:, :], [128, 32 * CH], BF16)

        def fw1(t):
            ctx.wait(t, "ve")
        mm_stream(ins["h1"], 32, 32,
                  lambda kc: ha[:, kc * CH:(kc + 1) * CH], ev_h1,
                  tensor_first_wait=fw1, pbanks=(0, 1),
                  bufs=("C", "D"), gq=True)

        def ev_h2(oc, pb, pb2):
            def _s(sc, oc=oc, pb=pb):
                ctx.wait(sc, "td")
                ctx.inc(sc.activation(ha[:, oc * CH:(oc + 1) * CH],
                                      ps[pb][:, 0:CH], AF.Gelu,
                                      scale=scal[:, 96 + oc:97 + oc]), "sc")
            S(_s)
            return ("sc", ctx.cnt["sc"])

        def fw2(t):
            ctx.wait(t, "sc")
        mm_stream(ins["h2"], 32, 32,
                  lambda kc: hb[:, kc * CH:(kc + 1) * CH], ev_h2,
                  tensor_first_wait=fw2, pbanks=(0, 1),
                  bufs=("C", "D"), gq=True)

        def ev_w3(oc, pb, pb2):
            def _v(v, oc=oc, pb=pb):
                ctx.wait(v, "td")
                ctx.inc(v.scalar_tensor_tensor(
                    xchunk[:, oc * CH:(oc + 1) * CH], ps[pb][:, 0:CH],
                    scal[:, 128 + oc:129 + oc],
                    xchunk[:, oc * CH:(oc + 1) * CH], ALU.mult, ALU.add), "ve")
            V(_v)
            return ("ve", ctx.cnt["ve"])

        def fw3(t):
            ctx.wait(t, "sc")
        mm_stream(ins["w3"], 8, 32,
                  lambda kc: ha[:, kc * CH:(kc + 1) * CH], ev_w3,
                  tensor_first_wait=fw3, pbanks=(0, 1),
                  bufs=("C", "D"), gq=True)

        if c == 3:
            tap("accB", xchunk[:, :], [128, 8 * CH], F32)

        emit_ln(lambda k: xchunk[:, k * CH:(k + 1) * CH],
                lambda k: xn1c[:, k * CH:(k + 1) * CH], CH)

        def _pk(s_, c=c):
            ctx.wait(s_, "ve")
            for k in range(8):
                ctx.inc(s_.dma_start(out=acc_park[c][k],
                                     in_=xchunk[:, k * CH:(k + 1) * CH]),
                        "st", 16)
                ctx.inc(s_.dma_start(out=ag_in[c][k],
                                     in_=xn1c[:, k * CH:(k + 1) * CH]),
                        "st", 16)
        Y(_pk)

        def _cg(g, c=c):
            ctx.wait(g, "st")
            ctx.inc(g.collective_compute("AllGather", ALU.bypass,
                                         replica_groups=rg, ins=[ag_in[c]],
                                         outs=[ag_out[c]]), "coll")
        G(_cg)

    barrier()
    sfree("hb", "ha", "xn1c", "q2cc", "xn0c", "wbufD", "wbufC",
          "bfrow", "ks2", "kv2d", "kvsb")
    del wb["C"], wb["D"]
    tap("acc3", xchunk[:, :], [128, 8 * CH], F32)
    tap("xn1c", xn1c[:, :], [128, 8 * CH], BF16)
    if stop_after == "ph3":
        return finish(), tap_list

    xn1g = sbuf("xn1g", [128, 8 * 1024], BF16)
    dtbuf = sbuf("dtbuf", [128, 8 * 1024], F32)
    xcbuf = sbuf("xcbuf", [128, 8 * 1024], F32)
    xiy = sbuf("xiy", [128, 8 * 1027], F32)
    yown = sbuf("yown", [128, 8 * 1024], BF16)
    tmpa = sbuf("tmpa", [128, 1024], F32)
    brow = sbuf("brow", [16, 2048], BF16)
    mscratch = sbuf("mscratch", [128, 8 * 1024], BF16)
    xn1cR = mscratch[:, 0:8 * CH]
    pe_sb = mscratch[:, 8 * CH:12 * CH]
    cbc = mscratch[:, 12 * CH:13 * CH]
    ucb = mscratch[:, 13 * CH:14 * CH]
    gsc = sbuf("gsc", [128, CH], F32)
    wb["C"] = sbuf("wbufC2", [128, 8 * 128], BF16)
    wb["D"] = sbuf("wbufD2", [128, 8 * 128], BF16)
    wfree["C"] = 0
    wfree["D"] = 0

    xinner = xiy
    ybuf = xiy

    pending = []
    defer_q = []

    def make_tail(j):
        steps = []
        yagc = 6 + j
        marks = {}

        def st_ry():
            def _ry(s_, j=j, yagc=yagc):
                ctx.wait(s_, "coll", yagc)
                ctx.wait(s_, "ve")
                for k in range(8):
                    ctx.inc(s_.dma_start(
                        out=mscratch[:, k * 1024:(k + 1) * 1024],
                        in_=yag_out[j][0][k]), "ld", 16)
                    ctx.inc(s_.dma_start(
                        out=xn1g[:, k * 1024:(k + 1) * 1024],
                        in_=yag_out[j][1][k]), "ld", 16)
            Y(_ry)
        steps.append(st_ry)

        for k in range(8):
            def st_vy(k=k):
                def _vy(v, j=j, k=k):
                    ctx.wait(v, "ld")
                    ctx.inc(v.tensor_scalar_mul(
                        tmp3[:, 0:1024], xn1g[:, k * 1024:(k + 1) * 1024],
                        flag1[:, 0:1]), "ve")
                    ctx.inc(v.scalar_tensor_tensor(
                        xn1g[:, k * 1024:(k + 1) * 1024],
                        mscratch[:, k * 1024:(k + 1) * 1024], flag0[:, 0:1],
                        tmp3[:, 0:1024], ALU.mult, ALU.add), "ve")
                V(_vy)
                if k == 7:
                    marks["vy"] = ctx.cnt["ve"]
            steps.append(st_vy)

        def st_ua():
            def _ua(s_, j=j):
                ctx.wait(s_, "ve")
                ctx.wait(s_, "st")
                for k in range(8):
                    ctx.inc(s_.dma_start(out=xchunk[:, k * CH:(k + 1) * CH],
                                         in_=acc_park[j][k]), "ld", 16)
                for k in range(8):
                    ctx.inc(s_.dma_start(out=xn1cR[:, k * CH:(k + 1) * CH],
                                         in_=ag_in[j][k]), "ld", 16)
            Y(_ua)
        steps.append(st_ua)

        def st_gate1():
            ld_moe = ctx.cnt["ld"]
            for t4 in range(4):
                def _t5(t, j=j, t4=t4, ld_moe=ld_moe):
                    ctx.wait(t, "ld", ld_moe)
                    pw = pfree.get(t4 % 2)
                    if pw:
                        ctx.wait(t, pw[0], pw[1])
                    pb = t4 % 2
                    for kc in range(8):
                        mm = t.matmul(ps[pb][:, 0:8],
                                      xn1cR[:, kc * CH + t4 * 128:
                                            kc * CH + (t4 + 1) * 128],
                                      gate_sb[:, kc * 8:(kc + 1) * 8],
                                      start=(kc == 0), stop=(kc == 7))
                    ctx.inc(mm, "td")
                T(_t5)

                def _v5(v, t4=t4):
                    ctx.wait(v, "td")
                    pb = t4 % 2
                    ctx.inc(v.reduce_max(logit[:, 32 + t4:33 + t4],
                                         ps[pb][:, 0:8],
                                         axis=mybir.AxisListType.X), "ve")
                    ctx.wait(v, "ve")
                    ctx.inc(v.tensor_scalar(logit[:, t4 * 8:(t4 + 1) * 8],
                                            ps[pb][:, 0:8],
                                            logit[:, 32 + t4:33 + t4], None,
                                            ALU.is_ge), "ve")
                V(_v5)
                pfree[t4 % 2] = ("ve", ctx.cnt["ve"])
        steps.append(st_gate1)

        def st_gate2():
            def _s6(sc, j=j):
                ctx.wait(sc, "ve")
                ctx.inc(sc.activation(logit[:, 36:40], logit[:, 32:36],
                                      AF.Sigmoid), "sc")
            S(_s6)

            def _v6(v, j=j):
                ctx.wait(v, "sc")
                for t4 in range(4):
                    ctx.inc(v.tensor_scalar_mul(
                        logitb[:, t4 * 8:(t4 + 1) * 8],
                        logit[:, t4 * 8:(t4 + 1) * 8],
                        logit[:, 36 + t4:37 + t4]), "ve")
            V(_v6)

            for t4 in range(4):
                def _t6(t, j=j, t4=t4):
                    ctx.wait(t, "ve")
                    pw = pfree.get(2)
                    if pw:
                        ctx.wait(t, pw[0], pw[1])
                    mm = t.matmul(ps[2][0:8, 0:128],
                                  logitb[:, t4 * 8:(t4 + 1) * 8],
                                  ident[:, :], start=True, stop=True)
                    ctx.inc(mm, "td")
                T(_t6)

                def _s7(v, j=j, t4=t4):
                    ctx.wait(v, "td")
                    ctx.inc(v.tensor_copy(cfm[0:8, t4 * 128:(t4 + 1) * 128],
                                          ps[2][0:8, 0:128]), "ve")
                V(_s7)
                pfree[2] = ("ve", ctx.cnt["ve"])
        steps.append(st_gate2)

        for e in range(NE):
            def st_cbc(e=e, j=j):
                def _t7(t, e=e):
                    ctx.wait(t, "ve")
                    pw = pfree.get(3)
                    if pw:
                        ctx.wait(t, pw[0], pw[1])
                    mm = t.matmul(ps[3][:, 0:CH],
                                  sel8[:, e * 128:(e + 1) * 128],
                                  cfm[0:8, 0:CH], start=True, stop=True)
                    ctx.inc(mm, "td")
                T(_t7)

                def _s8(v, e=e):
                    ctx.wait(v, "td")
                    ctx.inc(v.tensor_copy(cbc[:, 0:CH], ps[3][:, 0:CH]),
                            "ve")
                V(_s8)
                pfree[3] = ("ve", ctx.cnt["ve"])
            steps.append(st_cbc)

            def ev_gu(oc, pb, pb2, e=e):
                def _v(v, oc=oc, pb=pb, pb2=pb2):
                    ctx.wait(v, "td")
                    ctx.inc(v.tensor_mul(gsc[:, 0:CH], ps[pb][:, 0:CH],
                                         cbc[:, 0:CH]), "ve")
                    ctx.inc(v.tensor_mul(ucb[:, 0:CH], ps[pb2][:, 0:CH],
                                         cbc[:, 0:CH]), "ve")
                V(_v)
                ret = ("ve", ctx.cnt["ve"])

                def _s(sc, oc=oc):
                    ctx.wait(sc, "ve")
                    ctx.inc(sc.activation(silb[:, 0:CH], gsc[:, 0:CH],
                                          AF.Silu), "sc")
                S(_s)

                def dv2(oc=oc, e=e):
                    def _v2(v, oc=oc):
                        ctx.wait(v, "sc")
                        ctx.inc(v.tensor_mul(pe_sb[:, oc * CH:(oc + 1) * CH],
                                             silb[:, 0:CH], ucb[:, 0:CH]),
                                "ve")
                    V(_v2)
                    if oc == 3:
                        marks["gu", e] = ctx.cnt["ve"]
                defer_q.append(dv2)
                return ret

            def fwg(t):
                ctx.wait(t, "ld")
            mm_stream(wbf["wgate"][e], 4, 8,
                      lambda kc: xn1cR[:, kc * CH:(kc + 1) * CH], ev_gu,
                      strips2=wbf["wup"][e], tensor_first_wait=fwg,
                      pbanks=(0, 1), pbanks2=(2, 3),
                      cast_gate=cast_done["wup"] if (j == 0 and e == 0)
                      else None, steps=steps, fast=True)

            def ev_dn(oc, pb, pb2):
                def _v(v, oc=oc, pb=pb):
                    ctx.wait(v, "td")
                    ctx.inc(v.scalar_tensor_tensor(
                        xchunk[:, oc * CH:(oc + 1) * CH], ps[pb][:, 0:CH],
                        1.0, xchunk[:, oc * CH:(oc + 1) * CH],
                        ALU.bypass, ALU.add), "ve")
                V(_v)
                return ("ve", ctx.cnt["ve"])

            def fwd(t, e=e):
                ctx.wait(t, "ve", marks["gu", e])
            mm_stream(wbf["wdown"][e], 8, 4,
                      lambda kc: pe_sb[:, kc * CH:(kc + 1) * CH], ev_dn,
                      tensor_first_wait=fwd, pbanks=(0, 1),
                      cast_gate=cast_done["wdown"] if (j == 0 and e == 0)
                      else None, steps=steps, fast=True)

        def yrhs(kc, g_):
            if kc < 8:
                return yown[:, kc * 1024 + g_ * CH:kc * 1024 + (g_ + 1) * CH]
            return xn1g[:, (kc - 8) * 1024 + g_ * CH:
                        (kc - 8) * 1024 + (g_ + 1) * CH]

        for g_ in range(2):
            sel = flag1 if g_ == 0 else flag0

            def ev_op(oc, pb, pb2, j=j, g_=g_, sel=sel):
                def _v(v, oc=oc, pb=pb, sel=sel):
                    ctx.wait(v, "td")
                    ctx.inc(v.tensor_scalar_mul(
                        tmp3[:, 0:CH], ps[pb][:, 0:CH], sel[:, 0:1]),
                        "ve")
                    ctx.inc(v.scalar_tensor_tensor(
                        xchunk[:, oc * CH:(oc + 1) * CH], tmp3[:, 0:CH],
                        1.0, xchunk[:, oc * CH:(oc + 1) * CH],
                        ALU.bypass, ALU.add), "ve")
                V(_v)
                return ("ve", ctx.cnt["ve"])

            def fwo(t):
                ctx.wait(t, "ve", marks["vy"])
            mm_stream(wbf["oproj"], 8, 16,
                      lambda kc, g_=g_: yrhs(kc, g_), ev_op,
                      tensor_first_wait=fwo, pbanks=(0, 1),
                      cast_gate=cast_done["oproj"] if (j == 0 and g_ == 0)
                      else None, steps=steps, fast=True)

        def st_so():
            def _so(s_, j=j):
                ctx.wait(s_, "ve")
                for k in range(8):
                    ctx.inc(s_.dma_start(out=out[k][:, j * CH:(j + 1) * CH],
                                         in_=xchunk[:, k * CH:(k + 1) * CH]),
                            "st", 16)
            Y(_so)
        steps.append(st_so)
        return steps

    for j in range(NCH):
        agc = 2 + j

        def _rb(s_, j=j, agc=agc):
            ctx.wait(s_, "coll", agc)
            ctx.wait(s_, "ve")
            ctx.wait(s_, "td")
            for k in range(8):
                ctx.inc(s_.dma_start(out=xn1g[:, k * 1024:k * 1024 + CH],
                                     in_=ag_out[j][0][k]), "ld", 16)
                ctx.inc(s_.dma_start(out=xn1g[:, k * 1024 + CH:(k + 1) * 1024],
                                     in_=ag_out[j][1][k]), "ld", 16)
        Y(_rb)

        def _vh(v, j=j):
            ctx.wait(v, "sc")
            if j > 0:
                for k in range(8):
                    ctx.inc(v.tensor_copy(
                        halo[:, k * 3:(k + 1) * 3],
                        xinner[:, k * 1027 + 1024:k * 1027 + 1027]), "ve")
            else:
                ctx.inc(v.memset(halo[:, :], 0.0), "ve")
            for k in range(8):
                ctx.inc(v.tensor_copy(xinner[:, k * 1027:k * 1027 + 3],
                                      halo[:, k * 3:(k + 1) * 3]), "ve")
        V(_vh)

        for g_ in range(2):
            def ev_xpg(oc, pb, pb2, j=j, g_=g_):
                def _s(v, oc=oc, pb=pb):
                    ctx.wait(v, "td")
                    ctx.inc(v.tensor_copy(
                        xinner[:, oc * 1027 + 3 + g_ * CH:
                               oc * 1027 + 3 + (g_ + 1) * CH],
                        ps[pb][:, 0:CH]), "ve")
                V(_s)
                return ("ve", ctx.cnt["ve"])

            def fwx(t, j=j, g_=g_):
                ctx.wait(t, "ld")
            mm_stream(wbf["xproj"], 8, 8,
                      lambda kc, g_=g_: xn1g[:, kc * 1024 + g_ * CH:
                                             kc * 1024 + (g_ + 1) * CH],
                      ev_xpg, tensor_first_wait=fwx, pbanks=(0, 1),
                      cast_gate=cast_done["xproj"] if (j == 0 and g_ == 0)
                      else None, fast=True)

        for k in range(8):
            def _v(v, j=j, k=k):
                ctx.wait(v, "sc")
                ctx.inc(v.tensor_scalar_mul(
                    xcbuf[:, k * 1024:(k + 1) * 1024],
                    xinner[:, k * 1027 + 0:k * 1027 + 1024],
                    convw_sb[:, k * 4:k * 4 + 1]), "ve")
                for kk in range(1, 4):
                    ctx.inc(v.scalar_tensor_tensor(
                        xcbuf[:, k * 1024:(k + 1) * 1024],
                        xinner[:, k * 1027 + kk:k * 1027 + 1024 + kk],
                        convw_sb[:, k * 4 + kk:k * 4 + kk + 1],
                        xcbuf[:, k * 1024:(k + 1) * 1024],
                        ALU.mult, ALU.add), "ve")
            V(_v)

            def _s(sc, j=j, k=k):
                ctx.wait(sc, "ve")
                ctx.inc(sc.activation(xcbuf[:, k * 1024:(k + 1) * 1024],
                                      xcbuf[:, k * 1024:(k + 1) * 1024],
                                      AF.Silu, bias=scal[:, 144 + k:145 + k]),
                        "sc")
            S(_s)

        for g_ in range(2):
            def ev_dt(oc, pb, pb2, j=j, g_=g_):
                def _s(sc, oc=oc, pb=pb):
                    ctx.wait(sc, "td")
                    dst = dtbuf[:, oc * 1024 + g_ * CH:
                                oc * 1024 + (g_ + 1) * CH]
                    ctx.inc(sc.activation(dst, ps[pb][:, 0:CH], AF.Exp,
                                          bias=scal[:, 136 + oc:137 + oc]),
                            "sc")
                S(_s)
                return ("sc", ctx.cnt["sc"])

            def fwd2(t):
                ctx.wait(t, "ld")
            mm_stream(wbf["dtproj"], 8, 8,
                      lambda kc, g_=g_: xn1g[:, kc * 1024 + g_ * CH:
                                             kc * 1024 + (g_ + 1) * CH],
                      ev_dt, tensor_first_wait=fwd2, pbanks=(0, 1),
                      cast_gate=cast_done["dtproj"] if (j == 0 and g_ == 0)
                      else None, fast=True)

        def _sp2(sc, j=j):
            for oc in range(8):
                for gg in range(2):
                    dst = dtbuf[:, oc * 1024 + gg * CH:
                                oc * 1024 + (gg + 1) * CH]
                    ctx.inc(sc.activation(dst, dst, AF.Ln,
                                          bias=actc[:, 5:6]), "sc")
        S(_sp2)
        sc_dt = ctx.cnt["sc"]

        def _tbc(t, j=j):
            ctx.wait(t, "cast")
            ctx.wait(t, "sc")
            for g_ in range(2):
                for kc in range(8):
                    t.matmul(ps[4 + g_][0:16, 0:CH],
                             bcw[:, kc * 16:(kc + 1) * 16],
                             xn1g[:, kc * 1024 + g_ * CH:
                                  kc * 1024 + (g_ + 1) * CH],
                             start=(kc == 0), stop=(kc == 7))
                for kc in range(8):
                    mm = t.matmul(ps[6 + g_][0:16, 0:CH],
                                  bcw[:, 128 + kc * 16:128 + (kc + 1) * 16],
                                  xn1g[:, kc * 1024 + g_ * CH:
                                       kc * 1024 + (g_ + 1) * CH],
                                  start=(kc == 0), stop=(kc == 7))
            ctx.inc(mm, "td")
        T(_tbc)

        def _sbc(v, j=j):
            ctx.wait(v, "td")
            for g_ in range(2):
                ctx.inc(v.tensor_copy(brow[0:16, g_ * CH:(g_ + 1) * CH],
                                      ps[4 + g_][0:16, 0:CH]), "ve")
                ctx.inc(v.tensor_copy(
                    brow[0:16, 1024 + g_ * CH:1024 + (g_ + 1) * CH],
                    ps[6 + g_][0:16, 0:CH]), "ve")
        V(_sbc)

        for k in range(8):
            def _v(v, j=j, k=k):
                ctx.wait(v, "sc")
                ctx.inc(v.tensor_scalar_mul(
                    ybuf[:, k * 1027:k * 1027 + 1024],
                    xcbuf[:, k * 1024:(k + 1) * 1024],
                    scal[:, 152 + k:153 + k]), "ve")
                ctx.inc(v.tensor_mul(xcbuf[:, k * 1024:(k + 1) * 1024],
                                     xcbuf[:, k * 1024:(k + 1) * 1024],
                                     dtbuf[:, k * 1024:(k + 1) * 1024]), "ve")
            V(_v)

        ve_after = {}
        sc_after = {}
        abufs = [tmp1, tmpa]
        n_steps = len(pending)
        done_steps = 0

        def emit_sa(s_i, k0):
            i0 = s_i * 8 + k0
            ab0 = abufs[i0 % 2]

            def _sa(sc, s_i=s_i, k0=k0, i0=i0, ab0=ab0):
                if i0 >= 2:
                    ctx.wait(sc, "ve", ve_after[i0 - 2])
                ctx.inc(sc.activation(
                    ab0[:, 0:1024], dtbuf[:, k0 * 1024:(k0 + 1) * 1024],
                    AF.Exp,
                    scale=a_sb[:, k0 * 16 + s_i:k0 * 16 + s_i + 1]), "sc")
                sc_after[i0] = ctx.cnt["sc"]
            S(_sa)

        for s_i in range(NS):
            ve_bc = ve_after.get(s_i * 8 - 1)

            def _tb(t, j=j, s_i=s_i, ve_bc=ve_bc):
                if ve_bc is None:
                    ctx.wait(t, "ve")
                else:
                    ctx.wait(t, "ve", ve_bc)
                for g_ in range(2):
                    t.matmul(psW0[:, g_ * CH:(g_ + 1) * CH],
                             sel16[:, s_i * 128:(s_i + 1) * 128],
                             brow[0:16, g_ * CH:(g_ + 1) * CH],
                             start=True, stop=True)
                    mm = t.matmul(psW1[:, g_ * CH:(g_ + 1) * CH],
                                  sel16[:, s_i * 128:(s_i + 1) * 128],
                                  brow[0:16, 1024 + g_ * CH:
                                       1024 + (g_ + 1) * CH],
                                  start=True, stop=True)
                ctx.inc(mm, "td")
            T(_tb)
            td_tb = ctx.cnt["td"]

            emit_sa(s_i, 0)
            emit_sa(s_i, 1)

            for k in range(8):
                col = s_i * 8 + k
                i = s_i * 8 + k
                ab = abufs[i % 2]

                def _vs(v, j=j, s_i=s_i, k=k, col=col, i=i, ab=ab,
                        td_tb=td_tb):
                    ctx.wait(v, "sc", sc_after[i])
                    if k == 0:
                        if s_i == NS - 1:
                            ctx.wait(v, "td")
                        else:
                            ctx.wait(v, "td", td_tb)
                    ctx.inc(v.tensor_mul(tmp2[:, 0:1024],
                                         xcbuf[:, k * 1024:(k + 1) * 1024],
                                         psW0[:, 0:1024]), "ve")
                    ctx.inc(v.tensor_tensor_scan(tmp3[:, 0:1024],
                                                 ab[:, 0:1024],
                                                 tmp2[:, 0:1024],
                                                 state[:, col:col + 1],
                                                 ALU.mult, ALU.add), "ve")
                    ctx.wait(v, "ve")
                    ctx.inc(v.tensor_copy(state[:, col:col + 1],
                                          tmp3[:, 1023:1024]), "ve")
                    ctx.inc(v.tensor_mul(tmp2[:, 0:1024], tmp3[:, 0:1024],
                                         psW1[:, 0:1024]), "ve")
                    if s_i < NS - 1:
                        ctx.inc(v.tensor_add(
                            ybuf[:, k * 1027:k * 1027 + 1024],
                            ybuf[:, k * 1027:k * 1027 + 1024],
                            tmp2[:, 0:1024]), "ve")
                    else:
                        ctx.inc(v.tensor_add(
                            yown[:, k * 1024:(k + 1) * 1024],
                            ybuf[:, k * 1027:k * 1027 + 1024],
                            tmp2[:, 0:1024]), "ve")
                V(_vs)
                ve_after[i] = ctx.cnt["ve"]

                if k + 2 < 8:
                    emit_sa(s_i, k + 2)

                slot = s_i * 8 + k
                while pending and done_steps * 120 < n_steps * (slot + 1):
                    while defer_q:
                        defer_q.pop(0)()
                    pending.pop(0)()
                    done_steps += 1

        while pending:
            while defer_q:
                defer_q.pop(0)()
            pending.pop(0)()
        while defer_q:
            defer_q.pop(0)()

        def _sy(s_, j=j):
            ctx.wait(s_, "ve")
            for k in range(8):
                ctx.inc(s_.dma_start(out=yag_in[j][k],
                                     in_=yown[:, k * 1024:(k + 1) * 1024]),
                        "st", 16)
        Y(_sy)

        def _cy(g, j=j):
            ctx.wait(g, "st")
            ctx.inc(g.collective_compute("AllGather", ALU.bypass,
                                         replica_groups=rg, ins=[yag_in[j]],
                                         outs=[yag_out[j]]), "coll")
        G(_cy)

        pending = make_tail(j)

    while pending:
        while defer_q:
            defer_q.pop(0)()
        pending.pop(0)()
    while defer_q:
        defer_q.pop(0)()

    tap("dtbuf", dtbuf[:, :], [128, 8 * 1024], F32)
    tap("xcbuf", xcbuf[:, :], [128, 8 * 1024], F32)
    tap("yown", yown[:, :], [128, 8 * 1024], BF16)
    return finish(), tap_list


def kernel(**inputs):
    nc, _ = build(8)
    in_maps = [prep_core_inputs(c, inputs) for c in range(8)]
    res = run_bass_kernel_spmd(nc, in_maps, core_ids=list(range(8)))
    full = np.zeros((B, T, D), dtype=np.float32)
    for core in range(8):
        b, h = core // 2, core % 2
        o = np.asarray(res.results[core]["out"]).reshape(D, NCH, CH)
        for c in range(NCH):
            g = 2 * c + h
            full[b, g * CH:(g + 1) * CH, :] = o[:, c, :].T
    return full


# version 23
# speedup vs baseline: 1.1493x; 1.0654x over previous
import numpy as np
import ml_dtypes
import concourse.bass as bass
import concourse.mybir as mybir
from concourse.bass_utils import run_bass_kernel_spmd

F32 = mybir.dt.float32
BF16 = mybir.dt.bfloat16
AF = mybir.ActivationFunctionType
ALU = mybir.AluOpType

B, T, D = 4, 4096, 1024
TOK, NCH, CH, NTS = 2048, 4, 512, 16
HEADS, HD, NPAIR = 16, 64, 8
DH = 4096
DI, DIH, NS = 2048, 1024, 16
NE, IE = 8, 512
SCALE = HD ** (-0.5)
EPS = 1e-5


def _lhsT_tiles(wT):
    K, M = wT.shape
    return np.ascontiguousarray(
        wT.reshape(K // 128, 128, M // 128, 128).transpose(2, 1, 0, 3)
        .reshape(M // 128, 128, (K // 128) * 128))


def _rhs_tiles(w):
    K, N = w.shape
    return np.ascontiguousarray(w.reshape(K // 128, 128, N))


def prep_core_inputs(core, inp):
    b, h = core // 2, core % 2
    own_g = [2 * c + h for c in range(NCH)]
    d = {}
    x = np.asarray(inp["L0_attn_x"])[b]
    xo = np.concatenate([x[g * CH:(g + 1) * CH] for g in own_g], 0)
    d["x_fm"] = np.ascontiguousarray(xo.T).reshape(D // 128, 128, TOK)

    pos = np.concatenate([np.arange(g * CH, (g + 1) * CH) for g in own_g])
    inv = 1.0 / (500000.0 ** (np.arange(0, HD, 2) / HD))
    fr = pos[:, None] * inv[None, :]
    emb = np.concatenate([fr, fr], axis=1)
    d["rope_cos"] = np.tile(np.cos(emb), (1, HEADS)).astype(
        ml_dtypes.bfloat16).reshape(NTS, 128, HEADS * HD)
    d["rope_sin"] = np.tile(np.sin(emb), (1, HEADS)).astype(
        ml_dtypes.bfloat16).reshape(NTS, 128, HEADS * HD)

    d["ident"] = np.eye(128, dtype=np.float32)
    d["ones_row"] = np.ones((1, 128), dtype=np.float32)
    ex2 = np.zeros((2, 128), dtype=np.float32)
    ex2[0, :64] = 1.0
    ex2[1, 64:] = 1.0
    d["expand2"] = ex2
    d["flag0"] = np.full((128, 1), 1.0 if h == 1 else 0.0, np.float32)
    d["flag1"] = np.full((128, 1), 1.0 if h == 0 else 0.0, np.float32)
    ac = np.zeros((128, 8), np.float32)
    ac[:, 0], ac[:, 1], ac[:, 2] = EPS, -0.5, -1.0
    ac[:, 3], ac[:, 4], ac[:, 5] = SCALE, -SCALE, 1.0
    d["actc"] = ac
    s2b = np.zeros((2, 128), np.float32); s2b[1, :] = 1.0
    d["sel2b"] = s2b
    s8 = np.zeros((8, 8 * 128), np.float32)
    for e in range(8):
        s8[e, e * 128:(e + 1) * 128] = 1.0
    d["sel8"] = s8
    s16 = np.zeros((16, 16 * 128), np.float32)
    for si in range(16):
        s16[si, si * 128:(si + 1) * 128] = 1.0
    d["sel16"] = s16

    d["wqkv"] = _rhs_tiles(np.asarray(inp["L0_attn_W_qkv"]))
    bf = ml_dtypes.bfloat16
    d["wout"] = _lhsT_tiles(np.asarray(inp["L0_attn_W_out"])).astype(bf)

    d["w1"] = _lhsT_tiles(np.asarray(inp["L0_ffn_w1_w"]).T).astype(bf)
    d["w2"] = _lhsT_tiles(np.asarray(inp["L0_ffn_w2_w"]).T).astype(bf)
    d["h1"] = _lhsT_tiles(np.asarray(inp["L0_ffn_h1_w"]).T).astype(bf)
    d["h2"] = _lhsT_tiles(np.asarray(inp["L0_ffn_h2_w"]).T).astype(bf)
    d["w3"] = _lhsT_tiles(np.asarray(inp["L0_ffn_w3_w"]).T).astype(bf)
    d["s1"] = np.asarray(inp["L0_ffn_w1_s"]).reshape(32, 128).T.copy()
    d["s2"] = np.asarray(inp["L0_ffn_w2_s"]).reshape(32, 128).T.copy()
    d["sh1"] = np.asarray(inp["L0_ffn_h1_s"]).reshape(32, 128).T.copy()
    d["sh2"] = np.asarray(inp["L0_ffn_h2_s"]).reshape(32, 128).T.copy()
    d["s3"] = np.asarray(inp["L0_ffn_w3_s"]).reshape(8, 128).T.copy()

    rows = slice(h * DIH, (h + 1) * DIH)
    d["xproj"] = _lhsT_tiles(np.asarray(inp["L1_attn_x_proj_w"])[rows].T)
    d["dtproj"] = _lhsT_tiles(np.asarray(inp["L1_attn_dt_proj_w"])[rows].T)
    d["dtb"] = np.asarray(inp["L1_attn_dt_proj_b"])[rows].reshape(8, 128).T.copy()
    d["alog"] = np.ascontiguousarray(
        np.asarray(inp["L1_attn_A_log"])[rows].reshape(8, 128, NS)
        .transpose(1, 0, 2).reshape(128, 8 * NS))
    d["dvec"] = np.asarray(inp["L1_attn_D"])[rows].reshape(8, 128).T.copy()
    d["convw"] = np.ascontiguousarray(
        np.asarray(inp["L1_attn_conv_w"])[rows, 0].reshape(8, 128, 4)
        .transpose(1, 0, 2).reshape(128, 32))
    d["convb"] = np.asarray(inp["L1_attn_conv_b"])[rows].reshape(8, 128).T.copy()
    d["bproj"] = _rhs_tiles(np.asarray(inp["L1_attn_B_proj_w"]).T)
    d["cproj"] = _rhs_tiles(np.asarray(inp["L1_attn_C_proj_w"]).T)
    opw = np.asarray(inp["L1_attn_out_proj_w"]).T
    mine, other = opw[rows], opw[slice((1 - h) * DIH, (2 - h) * DIH)]
    d["oproj"] = _lhsT_tiles(np.concatenate([mine, other], 0))

    d["gate"] = _rhs_tiles(np.asarray(inp["L1_ffn_gate_w"]).T)
    wg, wu, wd = (np.asarray(inp["L1_ffn_w_gate"]),
                  np.asarray(inp["L1_ffn_w_up"]),
                  np.asarray(inp["L1_ffn_w_down"]))
    d["wgate"] = np.stack([_lhsT_tiles(wg[e].T) for e in range(NE)])
    d["wup"] = np.stack([_lhsT_tiles(wu[e].T) for e in range(NE)])
    d["wdown"] = np.stack([_lhsT_tiles(wd[e].T) for e in range(NE)])
    return d


INPUT_SPECS = {
    "x_fm": ([8, 128, TOK], F32),
    "rope_cos": ([NTS, 128, HEADS * HD], BF16),
    "rope_sin": ([NTS, 128, HEADS * HD], BF16),
    "ident": ([128, 128], F32),
    "ones_row": ([1, 128], F32),
    "expand2": ([2, 128], F32),
    "flag0": ([128, 1], F32),
    "flag1": ([128, 1], F32),
    "actc": ([128, 8], F32),
    "sel2b": ([2, 128], F32),
    "sel8": ([8, 8 * 128], F32),
    "sel16": ([16, 16 * 128], F32),
    "wqkv": ([8, 128, 3072], F32),
    "wout": ([8, 128, 8 * 128], BF16),
    "w1": ([32, 128, 8 * 128], BF16),
    "w2": ([32, 128, 8 * 128], BF16),
    "h1": ([32, 128, 32 * 128], BF16),
    "h2": ([32, 128, 32 * 128], BF16),
    "w3": ([8, 128, 32 * 128], BF16),
    "s1": ([128, 32], F32),
    "s2": ([128, 32], F32),
    "sh1": ([128, 32], F32),
    "sh2": ([128, 32], F32),
    "s3": ([128, 8], F32),
    "xproj": ([8, 128, 8 * 128], F32),
    "dtproj": ([8, 128, 8 * 128], F32),
    "dtb": ([128, 8], F32),
    "alog": ([128, 8 * NS], F32),
    "dvec": ([128, 8], F32),
    "convw": ([128, 32], F32),
    "convb": ([128, 8], F32),
    "bproj": ([8, 128, 16], F32),
    "cproj": ([8, 128, 16], F32),
    "oproj": ([8, 128, 16 * 128], F32),
    "gate": ([8, 128, 8], F32),
    "wgate": ([NE, 4, 128, 8 * 128], F32),
    "wup": ([NE, 4, 128, 8 * 128], F32),
    "wdown": ([NE, 8, 128, 4 * 128], F32),
}


def build(n_cores=8, stop_after=None, taps=()):
    nc = bass.Bass("TRN2", target_bir_lowering=False, debug=False,
                   detect_race_conditions=False)
    rg = [[i, i + 1] for i in range(0, n_cores, 2)]
    taps = list(taps)

    ins = {k: nc.dram_tensor(k, list(shp), dt, kind="ExternalInput").ap()
           for k, (shp, dt) in INPUT_SPECS.items()}
    out = nc.dram_tensor("out", [8, 128, TOK], F32, kind="ExternalOutput").ap()

    acc_park = nc.dram_tensor("acc_park", [NCH, 8, 128, CH], F32,
                              kind="Internal").ap()
    STREAMW = {"wout": [8, 128, 8 * 128], "w1": [32, 128, 8 * 128],
               "w2": [32, 128, 8 * 128], "h1": [32, 128, 32 * 128],
               "h2": [32, 128, 32 * 128], "w3": [8, 128, 32 * 128],
               "xproj": [8, 128, 8 * 128], "dtproj": [8, 128, 8 * 128],
               "oproj": [8, 128, 16 * 128]}
    wbf = {k: nc.dram_tensor("bf_" + k, shp, BF16, kind="Internal").ap()
           for k, shp in STREAMW.items()}
    MOEW = {"wgate": [NE, 4, 128, 8 * 128], "wup": [NE, 4, 128, 8 * 128],
            "wdown": [NE, 8, 128, 4 * 128]}
    for k, shp in MOEW.items():
        wbf[k] = nc.dram_tensor("bf_" + k, shp, BF16, kind="Internal").ap()
    cast_done = {}
    xn0_park = nc.dram_tensor("xn0_park", [8, 128, TOK], BF16,
                              kind="Internal").ap()
    q2_park = nc.dram_tensor("q2_park", [8, 128, TOK], BF16,
                             kind="Internal").ap()
    kv_in = nc.dram_tensor("kv_in", [128, 520], F32, kind="Internal").ap()
    kv_out = nc.dram_tensor("kv_out", [2, 128, 520], F32, kind="Internal").ap()
    ag_in = [nc.dram_tensor(f"ag_in{c}", [8, 128, CH], BF16, kind="Internal").ap()
             for c in range(NCH)]
    ag_out = [nc.dram_tensor(f"ag_out{c}", [2, 8, 128, CH], BF16,
                             kind="Internal").ap() for c in range(NCH)]
    yag_in = [nc.dram_tensor(f"yag_in{c}", [8, 128, 1024], BF16,
                             kind="Internal").ap() for c in range(NCH)]
    yag_out = [nc.dram_tensor(f"yag_out{c}", [2, 8, 128, 1024], BF16,
                              kind="Internal").ap() for c in range(NCH)]

    live = {}

    def sbuf(name, shape, dtype):
        m = nc.sbuf_tensor(name, list(shape), dtype)
        hdl = m.__enter__()
        live[name] = m
        return hdl

    def sfree(*names):
        for n in names:
            live.pop(n).__exit__(None, None, None)

    def psum_(name, shape):
        m = nc.psum_tensor(name, list(shape), F32)
        hdl = m.__enter__()
        live["psum_" + name] = m
        return hdl

    farena = sbuf("farena", [128, 787], F32)
    scal = farena[:, 0:160]
    onescol = farena[:, 160:161]
    flag0 = farena[:, 161:162]
    flag1 = farena[:, 162:163]
    actc = farena[:, 163:171]
    convw_sb = farena[:, 171:203]
    a_sb = farena[:, 203:331]
    alog_sb = farena[:, 331:459]
    state = farena[:, 459:587]
    halo = farena[:, 587:611]
    logit = farena[:, 611:659]
    identf = farena[:, 659:787]
    barena = sbuf("barena", [128, 480], BF16)
    ident = barena[:, 0:128]
    ones1 = barena[0:1, 128:256]
    exp2 = barena[0:2, 256:384]
    gate_sb = barena[:, 384:448]
    logitb = barena[:, 448:480]
    sel8 = sbuf("sel8s", [8, 8 * 128], BF16)
    sel16 = sbuf("sel16s", [16, 16 * 128], BF16)
    bcw = sbuf("bcw", [128, 256], BF16)
    xchunk = sbuf("xchunk", [128, 8 * CH], F32)
    tmp1 = sbuf("tmp1", [128, 1024], F32)
    tmp2 = sbuf("tmp2", [128, 1024], F32)
    tmp3 = sbuf("tmp3", [128, 1024], F32)
    wbufA = sbuf("wbufA", [128, 16 * 128], BF16)
    wbufB = sbuf("wbufB", [128, 16 * 128], BF16)
    cfm = sbuf("cfm", [8, CH], BF16)
    silb = sbuf("silb", [128, CH], BF16)
    kvsb = sbuf("kvsb", [128, 8 * 65], F32)
    kv2d = sbuf("kv2d", [128, 8 * 128], BF16)
    ks2 = sbuf("ks2", [128, 16], BF16)
    bfrow = sbuf("bfrow", [2, 1024], BF16)

    ps = [psum_(f"psb{i}", [128, 512]) for i in range(4)]
    psW0 = psum_("psW0", [128, 1024])
    psW1 = psum_("psW1", [128, 1024])
    ps.append(psW0[:, 0:512])
    ps.append(psW0[:, 512:1024])
    ps.append(psW1[:, 0:512])
    ps.append(psW1[:, 512:1024])

    class Ctx:
        def __init__(self):
            self.sems, self.cnt = {}, {}

        def sem(self, name):
            m = nc.semaphore(name)
            self.sems[name] = m.__enter__()
            live["sem_" + name] = m
            self.cnt[name] = 0

        def inc(self, instr, name, n=1):
            instr.then_inc(self.sems[name], n)
            self.cnt[name] += n
            return self.cnt[name]

        def wait(self, eng, name, val=None):
            eng.wait_ge(self.sems[name], self.cnt[name] if val is None else val)

    ctx = Ctx()
    for s in ["ld", "cast", "st", "coll", "td", "sc", "ve", "gA", "gB", "gC",
              "gD", "gp", "sA", "sB", "sC", "sD"]:
        ctx.sem(s)

    blk = nc.Block()
    block = blk.__enter__()

    def T(f): block.tensor(f)
    def V(f): block.vector(f)
    def S(f): block.scalar(f)
    def G(f): block.gpsimd(f)
    def Y(f): block.sync(f)

    def barrier():
        names = ["td", "sc", "ve", "ld", "cast", "st", "coll"]
        for engname in ["tensor", "vector", "scalar", "gpsimd", "sync"]:
            def _(e, engname=engname):
                for snm in names:
                    ctx.wait(e, snm)
            getattr(block, engname)(_)

    tap_list = []

    def tap(name, src_ap, shape, dtype):
        if name not in taps:
            return
        tt = nc.dram_tensor("tap_" + name, list(shape), dtype,
                            kind="ExternalOutput").ap()
        tap_list.append(name)

        def _(s_):
            ctx.wait(s_, "td"); ctx.wait(s_, "sc"); ctx.wait(s_, "ve")
            ctx.inc(s_.dma_start(out=tt, in_=src_ap), "st", 16)
        Y(_)

        def _b(v):
            ctx.wait(v, "st")
        V(_b)

    def finish():
        def _(s_):
            ctx.wait(s_, "st")
            ctx.wait(s_, "coll")
        Y(_)
        blk.__exit__(None, None, None)
        for m in reversed(list(live.values())):
            m.__exit__(None, None, None)
        return nc

    wfree = {"A": 0, "B": 0, "C": 0, "D": 0}
    wb = {"A": wbufA, "B": wbufB}
    gsem_hw = {"A": "gA", "B": "gB", "C": "gC", "D": "gD"}
    gsem_sw = {"A": "sA", "B": "sB", "C": "sC", "D": "sD"}
    pair = {"A": "C", "B": "D"}

    pfree = {}

    def mm_stream(strips, n_oc, n_kc, rhs_fn, evac, strips2=None,
                  tensor_first_wait=None, pbanks=(0, 1), pbanks2=(2, 3),
                  cast_gate=None, bufs=("A", "B"), gq=False, steps=None,
                  fast=False):
        gsem = gsem_sw if gq else gsem_hw
        evfree = [None, None]

        def do_oc(oc):
            bi = bufs[0] if oc % 2 == 0 else bufs[1]
            bw = wfree[bi]
            bw2 = wfree[pair[bi]] if strips2 is not None else 0

            def _g(g, oc=oc, bi=bi, bw=bw, bw2=bw2):
                if oc == 0 and cast_gate is not None:
                    ctx.wait(g, "cast", cast_gate)
                if bw:
                    ctx.wait(g, "td", bw)
                ctx.inc(g.dma_start(out=wb[bi][:, 0:n_kc * 128],
                                    in_=strips[oc]), gsem[bi], 16)
                if strips2 is not None:
                    bj = pair[bi]
                    if bw2:
                        ctx.wait(g, "td", bw2)
                    ctx.inc(g.dma_start(out=wb[bj][:, 0:n_kc * 128],
                                        in_=strips2[oc]), gsem[bj], 16)
            (G if gq else Y)(_g)

            gw = ctx.cnt[gsem[bi]]
            gw2 = ctx.cnt[gsem[pair[bi]]] if strips2 is not None else 0
            evw = evfree[oc % 2]

            def _t(t, oc=oc, bi=bi, gw=gw, gw2=gw2, evw=evw):
                if oc == 0 and tensor_first_wait is not None:
                    tensor_first_wait(t)
                if oc < 2:
                    if fast:
                        pw = pfree.get(pbanks[oc])
                        if pw:
                            ctx.wait(t, pw[0], pw[1])
                        if strips2 is not None:
                            pw2 = pfree.get(pbanks2[oc])
                            if pw2:
                                ctx.wait(t, pw2[0], pw2[1])
                    else:
                        ctx.wait(t, "sc")
                        ctx.wait(t, "ve")
                ctx.wait(t, gsem[bi], gw)
                if strips2 is not None:
                    ctx.wait(t, gsem[pair[bi]], gw2)
                if evw is not None:
                    ctx.wait(t, evw[0], evw[1])
                pm = ps[pbanks[oc % 2]]
                pm2 = ps[pbanks2[oc % 2]] if strips2 is not None else None
                for kc in range(n_kc):
                    mm = t.matmul(pm[:, :], wb[bi][:, kc * 128:(kc + 1) * 128],
                                  rhs_fn(kc), start=(kc == 0),
                                  stop=(kc == n_kc - 1))
                    if strips2 is not None:
                        mm = t.matmul(pm2[:, :],
                                      wb[pair[bi]][:, kc * 128:(kc + 1) * 128],
                                      rhs_fn(kc), start=(kc == 0),
                                      stop=(kc == n_kc - 1))
                ctx.inc(mm, "td")
            T(_t)
            wfree[bi] = ctx.cnt["td"]
            if strips2 is not None:
                wfree[pair[bi]] = ctx.cnt["td"]
            evfree[oc % 2] = evac(oc, pbanks[oc % 2],
                                  pbanks2[oc % 2] if strips2 is not None else None)
            pfree[pbanks[oc % 2]] = evfree[oc % 2]
            if strips2 is not None:
                pfree[pbanks2[oc % 2]] = evfree[oc % 2]

        for oc in range(n_oc):
            if steps is None:
                do_oc(oc)
            else:
                steps.append(lambda oc=oc: do_oc(oc))

    def _ph0(s_):
        ctx.inc(s_.dma_start(out=flag0[:, :], in_=ins["flag0"]), "ld", 16)
        ctx.inc(s_.dma_start(out=flag1[:, :], in_=ins["flag1"]), "ld", 16)
        ctx.inc(s_.dma_start(out=scal[:, 0:32], in_=ins["s1"]), "ld", 16)
        ctx.inc(s_.dma_start(out=scal[:, 32:64], in_=ins["s2"]), "ld", 16)
        ctx.inc(s_.dma_start(out=scal[:, 64:96], in_=ins["sh1"]), "ld", 16)
        ctx.inc(s_.dma_start(out=scal[:, 96:128], in_=ins["sh2"]), "ld", 16)
        ctx.inc(s_.dma_start(out=scal[:, 128:136], in_=ins["s3"]), "ld", 16)
        ctx.inc(s_.dma_start(out=scal[:, 136:144], in_=ins["dtb"]), "ld", 16)
        ctx.inc(s_.dma_start(out=scal[:, 144:152], in_=ins["convb"]), "ld", 16)
        ctx.inc(s_.dma_start(out=scal[:, 152:160], in_=ins["dvec"]), "ld", 16)
        ctx.inc(s_.dma_start(out=alog_sb[:, :], in_=ins["alog"]), "ld", 16)
        ctx.inc(s_.dma_start(out=convw_sb[:, :], in_=ins["convw"]), "ld", 16)
        ctx.inc(s_.dma_start(out=actc[:, :], in_=ins["actc"]), "ld", 16)
    Y(_ph0)

    def _ph0g(g):
        ctx.inc(g.dma_start(out=ident[:, :], in_=ins["ident"]), "cast", 16)
        ctx.inc(g.dma_start(out=identf[:, :], in_=ins["ident"]), "cast", 16)
        ctx.inc(g.dma_start(out=ones1[:, :], in_=ins["ones_row"]), "cast", 16)
        ctx.inc(g.dma_start(out=exp2[:, :], in_=ins["expand2"]), "cast", 16)
        ctx.inc(g.dma_start(out=sel8[:, :], in_=ins["sel8"]), "cast", 16)
        ctx.inc(g.dma_start(out=sel16[:, :], in_=ins["sel16"]), "cast", 16)
        for k in range(8):
            ctx.inc(g.dma_start(out=gate_sb[:, k * 8:(k + 1) * 8],
                                in_=ins["gate"][k]), "cast", 16)
            ctx.inc(g.dma_start(out=bcw[:, k * 16:(k + 1) * 16],
                                in_=ins["bproj"][k]), "cast", 16)
            ctx.inc(g.dma_start(out=bcw[:, 128 + k * 16:128 + (k + 1) * 16],
                                in_=ins["cproj"][k]), "cast", 16)
    G(_ph0g)

    def _ph0s(sc):
        ctx.wait(sc, "ld")
        ctx.inc(sc.activation(a_sb[:, :], alog_sb[:, :], AF.Exp), "sc")
    S(_ph0s)

    def _ph0v(v):
        ctx.wait(v, "sc")
        ctx.wait(v, "cast")
        ctx.inc(v.tensor_scalar_mul(a_sb[:, :], a_sb[:, :], -1.0), "ve")
        ctx.inc(v.memset(onescol[:, :], 1.0), "ve")
        ctx.inc(v.memset(state[:, :], 0.0), "ve")
        ctx.inc(v.memset(kv2d[:, :], 0.0), "ve")
        ctx.inc(v.memset(ks2[:, :], 0.0), "ve")
    V(_ph0v)

    if stop_after == "ph0":
        return finish(), tap_list

    def emit_ln(src_fn, dst_fn, nt):
        for k in range(8):
            def _s(sc, k=k):
                ctx.wait(sc, "ld")
                ctx.wait(sc, "td")
                if k == 0:
                    ctx.wait(sc, "ve")
                ctx.inc(sc.activation(tmp1[:, 0:nt], src_fn(k), AF.Square),
                        "sc")
            S(_s)

            def _t(t, k=k):
                ctx.wait(t, "sc")
                ctx.wait(t, "ve")
                t.matmul(ps[0][0:1, 0:nt], onescol[:, :], src_fn(k),
                         start=(k == 0), stop=(k == 7))
                mm = t.matmul(ps[1][0:1, 0:nt], onescol[:, :], tmp1[:, 0:nt],
                              start=(k == 0), stop=(k == 7))
                ctx.inc(mm, "td")
            T(_t)

        def _v(v):
            ctx.wait(v, "td")
            ctx.inc(v.tensor_scalar_mul(tmp2[0:1, 0:nt], ps[0][0:1, 0:nt],
                                        1.0 / D), "ve")
            ctx.inc(v.tensor_scalar_mul(tmp2[0:1, nt:2 * nt],
                                        ps[1][0:1, 0:nt], 1.0 / D), "ve")
            ctx.inc(v.tensor_mul(tmp3[0:1, 0:nt], tmp2[0:1, 0:nt],
                                 tmp2[0:1, 0:nt]), "ve")
            ctx.inc(v.tensor_sub(tmp3[0:1, nt:2 * nt], tmp2[0:1, nt:2 * nt],
                                 tmp3[0:1, 0:nt]), "ve")
        V(_v)

        def _s2(sc):
            ctx.wait(sc, "ve")
            ctx.inc(sc.activation(tmp3[0:1, 0:nt], tmp3[0:1, nt:2 * nt],
                                  AF.Ln, bias=actc[0:1, 0:1]), "sc")
            ctx.inc(sc.activation(tmp3[0:1, nt:2 * nt], tmp3[0:1, 0:nt],
                                  AF.Exp, scale=actc[0:1, 1:2]), "sc")
        S(_s2)

        def _v2(v):
            ctx.wait(v, "sc")
            ctx.inc(v.tensor_mul(tmp1[0:1, 0:nt], tmp2[0:1, 0:nt],
                                 tmp3[0:1, nt:2 * nt]), "ve")
            ctx.inc(v.tensor_copy(bfrow[0:1, 0:nt], tmp3[0:1, nt:2 * nt]),
                    "ve")
            ctx.inc(v.tensor_scalar_mul(bfrow[0:1, nt:2 * nt],
                                        tmp1[0:1, 0:nt], -1.0), "ve")
        V(_v2)

        def _t2(t):
            ctx.wait(t, "ve")
            t.matmul(ps[2][:, 0:nt], ones1[0:1, :], bfrow[0:1, 0:nt],
                     start=True, stop=True)
            mm = t.matmul(ps[3][:, 0:nt], ones1[0:1, :],
                          bfrow[0:1, nt:2 * nt], start=True, stop=True)
            ctx.inc(mm, "td")
        T(_t2)

        for k in range(8):
            def _v3(v, k=k):
                if k == 0:
                    ctx.wait(v, "td")
                ctx.inc(v.tensor_mul(tmp1[:, 0:nt], src_fn(k), ps[2][:, 0:nt]),
                        "ve")
                ctx.inc(v.tensor_add(dst_fn(k), tmp1[:, 0:nt], ps[3][:, 0:nt]),
                        "ve")
            V(_v3)

    kftm = sbuf("kftm", [128, NTS * 1024], BF16)
    vext = sbuf("vext", [128, NTS * 1040], BF16)
    xn0 = sbuf("xn0", [128, 8 * TOK], BF16)
    for c in range(NCH):
        def _lx(s_, c=c):
            ctx.wait(s_, "ve")
            for k in range(8):
                ctx.inc(s_.dma_start(out=xchunk[:, k * CH:(k + 1) * CH],
                                     in_=ins["x_fm"][k][:, c * CH:(c + 1) * CH]),
                        "ld", 16)
        Y(_lx)
        emit_ln(lambda k, c=c: xchunk[:, k * CH:(k + 1) * CH],
                lambda k, c=c: xn0[:, k * TOK + c * CH:k * TOK + (c + 1) * CH],
                CH)

    def _pk0(s_):
        ctx.wait(s_, "ve")
        for k in range(8):
            ctx.inc(s_.dma_start(out=xn0_park[k],
                                 in_=xn0[:, k * TOK:(k + 1) * TOK]), "st", 16)
    Y(_pk0)

    tap("xn0", xn0[:, :], [128, 8 * TOK], BF16)
    if stop_after == "ph1":
        return finish(), tap_list

    rope_sb = sbuf("rope_sb", [128, 2 * 2048], BF16)
    qtm = sbuf("qtm", [128, 1024], F32)
    q2c = sbuf("q2c", [128, 8 * 128], BF16)
    wqq = sbuf("wqq", [128, 8 * 1024], BF16)

    def _gq(g):
        for k in range(8):
            ctx.inc(g.dma_start(out=wqq[:, k * 1024:(k + 1) * 1024],
                                in_=ins["wqkv"][k][:, 0:1024]), "cast", 16)
    G(_gq)

    def emit_rope_feat(ts, psq, dst_ap, sc_pos, sc_neg, td_w=None, ld_w=None):
        rb = (ts % 2) * 2048

        def _v(v, ts=ts, rb=rb, td_w=td_w, ld_w=ld_w):
            ctx.wait(v, "td", td_w)
            ctx.wait(v, "ld", ld_w)
            ctx.wait(v, "sc")
            for half in range(2):
                qp = ps[psq + half][:, :]
                sl = slice(half * 512, (half + 1) * 512)
                ctx.inc(v.tensor_mul(tmp1[:, sl], qp,
                                     rope_sb[:, rb + half * 512:
                                             rb + (half + 1) * 512]), "ve")
                qv = qp.rearrange("p (h two d) -> p h two d", two=2, d=32)
                t2v = tmp2[:, sl].rearrange("p (h two d) -> p h two d",
                                            two=2, d=32)
                ctx.inc(v.tensor_scalar_mul(t2v[:, :, 0, :], qv[:, :, 1, :],
                                            -1.0), "ve")
                ctx.inc(v.tensor_copy(t2v[:, :, 1, :], qv[:, :, 0, :]), "ve")
                ctx.inc(v.tensor_mul(tmp2[:, sl], tmp2[:, sl],
                                     rope_sb[:, rb + 1024 + half * 512:
                                             rb + 1024 + (half + 1) * 512]),
                        "ve")
                ctx.inc(v.tensor_add(tmp1[:, sl], tmp1[:, sl], tmp2[:, sl]),
                        "ve")
        V(_v)

        def _s2(sc, ts=ts):
            ctx.wait(sc, "ve")
            ctx.inc(sc.activation(tmp2[:, 0:1024], tmp1[:, 0:1024], AF.Relu,
                                  scale=sc_neg), "sc")
            ctx.inc(sc.activation(tmp2[:, 0:1024], tmp2[:, 0:1024], AF.Exp,
                                  scale=actc[:, 2:3]), "sc")
            ctx.inc(sc.activation(tmp1[:, 0:1024], tmp1[:, 0:1024], AF.Relu,
                                  scale=sc_pos), "sc")
        S(_s2)

        def _v2(v, ts=ts):
            ctx.wait(v, "sc")
            ctx.inc(v.tensor_add(dst_ap, tmp1[:, 0:1024], tmp2[:, 0:1024]),
                    "ve")
        V(_v2)

    q_ve_done = {}
    q_ld_done = {}
    for ts in range(NTS):
        def _lr(s_, ts=ts):
            if ts >= 2:
                ctx.wait(s_, "ve", q_ve_done[ts - 2])
            ctx.inc(s_.dma_start(out=rope_sb[:, (ts % 2) * 2048:
                                             (ts % 2) * 2048 + 1024],
                                 in_=ins["rope_cos"][ts]), "ld", 16)
            ctx.inc(s_.dma_start(out=rope_sb[:, (ts % 2) * 2048 + 1024:
                                             (ts % 2) * 2048 + 2048],
                                 in_=ins["rope_sin"][ts]), "ld", 16)
        Y(_lr)
        q_ld_done[ts] = ctx.cnt["ld"]

        def _t(t, ts=ts):
            ctx.wait(t, "cast")
            if ts == 0:
                ctx.wait(t, "ve")
                ctx.wait(t, "sc")
            if ts >= 2:
                ctx.wait(t, "ve", q_ve_done[ts - 2])
            pq = 4 if ts % 2 == 0 else 6
            for k in range(8):
                lhs = xn0[:, k * TOK + ts * 128:k * TOK + (ts + 1) * 128]
                for half in range(2):
                    mm = t.matmul(ps[pq + half][:, :], lhs,
                                  wqq[:, k * 1024 + half * 512:
                                      k * 1024 + (half + 1) * 512],
                                  start=(k == 0), stop=(k == 7))
            ctx.inc(mm, "td")
        T(_t)
        td_q = ctx.cnt["td"]

        emit_rope_feat(ts, 4 if ts % 2 == 0 else 6, qtm[:, 0:1024],
                       actc[:, 3:4], actc[:, 4:5], td_w=td_q,
                       ld_w=q_ld_done[ts])
        q_ve_done[ts] = ctx.cnt["ve"]

        for pr in range(8):
            def _tt(t, ts=ts, pr=pr):
                ctx.wait(t, "ve")
                if pr == 0:
                    ctx.wait(t, "st")
                mm = t.transpose(ps[2][:, 0:128],
                                 qtm[:, pr * 128:(pr + 1) * 128], identf[:, :])
                ctx.inc(mm, "td")
            T(_tt)

            def _se(v, ts=ts, pr=pr):
                ctx.wait(v, "td")
                ctx.inc(v.tensor_copy(q2c[:, pr * 128:(pr + 1) * 128],
                                      ps[2][:, 0:128]), "ve")
            V(_se)

        def _pq(s_, ts=ts):
            ctx.wait(s_, "ve")
            for pr in range(8):
                ctx.inc(s_.dma_start(
                    out=q2_park[pr][:, ts * 128:(ts + 1) * 128],
                    in_=q2c[:, pr * 128:(pr + 1) * 128]), "st", 16)
        Y(_pq)

    barrier()
    sfree("wqq")

    if stop_after == "ph1q":
        return finish(), tap_list

    wqkv2 = sbuf("wqkv2", [128, 8 * 2048], BF16)

    cast_wk = [0]

    def _gk(g):
        for k in range(8):
            ctx.inc(g.dma_start(out=wqkv2[:, k * 2048:(k + 1) * 2048],
                                in_=ins["wqkv"][k][:, 1024:3072]), "cast", 16)
        cast_wk[0] = ctx.cnt["cast"]
    G(_gk)

    def _vo(v):
        vv = vext[:, :].rearrange("p (t h c) -> p t h c", t=NTS, h=HEADS)
        ctx.inc(v.memset(vv[:, :, :, 64:65], 1.0), "ve")
    V(_vo)

    k_ve_done = {}
    k_ld_done = {}
    for ts in range(NTS):
        def _lr(s_, ts=ts):
            if ts >= 2:
                ctx.wait(s_, "ve", k_ve_done[ts - 2])
            ctx.inc(s_.dma_start(out=rope_sb[:, (ts % 2) * 2048:
                                             (ts % 2) * 2048 + 1024],
                                 in_=ins["rope_cos"][ts]), "ld", 16)
            ctx.inc(s_.dma_start(out=rope_sb[:, (ts % 2) * 2048 + 1024:
                                             (ts % 2) * 2048 + 2048],
                                 in_=ins["rope_sin"][ts]), "ld", 16)
        Y(_lr)
        k_ld_done[ts] = ctx.cnt["ld"]

        def _t(t, ts=ts):
            ctx.wait(t, "cast", cast_wk[0])
            if ts == 0:
                ctx.wait(t, "ve")
                ctx.wait(t, "sc")
            if ts >= 2:
                ctx.wait(t, "ve", k_ve_done[ts - 2])
            pk, pv = (4, 6) if ts % 2 == 0 else (0, 2)
            for k in range(8):
                lhs = xn0[:, k * TOK + ts * 128:k * TOK + (ts + 1) * 128]
                for half in range(2):
                    t.matmul(ps[pk + half][:, :], lhs,
                             wqkv2[:, k * 2048 + half * 512:
                                   k * 2048 + (half + 1) * 512],
                             start=(k == 0), stop=(k == 7))
                for half in range(2):
                    mm = t.matmul(ps[pv + half][:, :], lhs,
                                  wqkv2[:, k * 2048 + 1024 + half * 512:
                                        k * 2048 + 1024 + (half + 1) * 512],
                                  start=(k == 0), stop=(k == 7))
            ctx.inc(mm, "td")
        T(_t)
        td_k = ctx.cnt["td"]

        def _sv(v, ts=ts):
            ctx.wait(v, "td", td_k)
            pv = 6 if ts % 2 == 0 else 2
            vv = vext[:, ts * 1040:(ts + 1) * 1040].rearrange(
                "p (h c) -> p h c", h=HEADS)
            for half in range(2):
                src = ps[pv + half][:, :].rearrange("p (h c) -> p h c", h=8)
                ctx.inc(v.tensor_copy(vv[:, half * 8:(half + 1) * 8, 0:64],
                                      src), "ve")
        V(_sv)

        emit_rope_feat(ts, 4 if ts % 2 == 0 else 0,
                       kftm[:, ts * 1024:(ts + 1) * 1024],
                       actc[:, 5:6], actc[:, 2:3], td_w=td_k,
                       ld_w=k_ld_done[ts])
        k_ve_done[ts] = ctx.cnt["ve"]

    barrier()
    sfree("wqkv2", "q2c", "qtm", "rope_sb", "xn0")
    tap("kftm", kftm[:, :], [128, NTS * 1024], BF16)
    tap("vext", vext[:, :], [128, NTS * 1040], BF16)
    if stop_after == "ph1k":
        return finish(), tap_list

    for hd_ in range(HEADS):
        bank = hd_ % 2
        po = (hd_ % 2) * 64

        def _t(t, hd_=hd_, bank=bank, po=po):
            ctx.wait(t, "ve")
            for tsl in range(NTS):
                mm = t.matmul(
                    ps[bank][po:po + 64, 0:65],
                    kftm[:, tsl * 1024 + hd_ * 64:tsl * 1024 + (hd_ + 1) * 64],
                    vext[:, tsl * 1040 + hd_ * 65:tsl * 1040 + (hd_ + 1) * 65],
                    start=(tsl == 0), stop=(tsl == NTS - 1),
                    tile_position=(0, po))
            ctx.inc(mm, "td")
        T(_t)

        def _s(v, hd_=hd_, bank=bank, po=po):
            ctx.wait(v, "td")
            ctx.inc(v.tensor_copy(
                kvsb[po:po + 64, (hd_ // 2) * 65:(hd_ // 2 + 1) * 65],
                ps[bank][po:po + 64, 0:65]), "ve")
        V(_s)

    def _st2(s_):
        ctx.wait(s_, "ve")
        ctx.inc(s_.dma_start(out=kv_in, in_=kvsb[:, 0:520]), "st", 16)
    Y(_st2)

    def _cc(g):
        ctx.wait(g, "st")
        ctx.inc(g.collective_compute("AllGather", ALU.bypass,
                                     replica_groups=rg, ins=[kv_in],
                                     outs=[kv_out]), "coll")
        for k in ["wgate", "wup", "wdown"]:
            for e in range(NE):
                ctx.inc(g.dma_start(out=wbf[k][e], in_=ins[k][e]),
                        "cast", 16)
            cast_done[k] = ctx.cnt["cast"]
        for k in ["xproj", "dtproj", "oproj"]:
            ctx.inc(g.dma_start(out=wbf[k], in_=ins[k]), "cast", 16)
            cast_done[k] = ctx.cnt["cast"]
    G(_cc)

    def _rb2(s_):
        ctx.wait(s_, "coll")
        ctx.inc(s_.dma_start(out=tmp1[:, 0:520], in_=kv_out[0]), "ld", 16)
        ctx.inc(s_.dma_start(out=tmp2[:, 0:520], in_=kv_out[1]), "ld", 16)
    Y(_rb2)

    def _vc(v):
        ctx.wait(v, "ld")
        ctx.inc(v.tensor_add(kvsb[:, 0:520], tmp1[:, 0:520], tmp2[:, 0:520]),
                "ve")
        for pr in range(8):
            ctx.inc(v.tensor_copy(kv2d[0:64, pr * 128:pr * 128 + 64],
                                  kvsb[0:64, pr * 65:pr * 65 + 64]), "ve")
            ctx.inc(v.tensor_copy(kv2d[64:128, pr * 128 + 64:pr * 128 + 128],
                                  kvsb[64:128, pr * 65:pr * 65 + 64]), "ve")
            ctx.inc(v.tensor_copy(ks2[0:64, pr * 2:pr * 2 + 1],
                                  kvsb[0:64, pr * 65 + 64:pr * 65 + 65]), "ve")
            ctx.inc(v.tensor_copy(ks2[64:128, pr * 2 + 1:pr * 2 + 2],
                                  kvsb[64:128, pr * 65 + 64:pr * 65 + 65]),
                    "ve")
    V(_vc)

    barrier()
    sfree("vext", "kftm")
    tap("kvsb", kvsb[:, :], [128, 8 * 65], F32)
    if stop_after == "ph2":
        return finish(), tap_list

    wb["C"] = sbuf("wbufC", [128, 32 * 128], BF16)
    wb["D"] = sbuf("wbufD", [128, 32 * 128], BF16)
    xn0c = sbuf("xn0c", [128, 8 * CH], BF16)
    q2cc = sbuf("q2cc", [128, 8 * CH], BF16)
    xn1c = sbuf("xn1c", [128, 8 * CH], BF16)
    ha = sbuf("ha", [128, 32 * CH], BF16)
    hb = sbuf("hb", [128, 32 * CH], BF16)

    for c in range(NCH):
        def _lc(s_, c=c):
            ctx.wait(s_, "ve")
            ctx.wait(s_, "td")
            for k in range(8):
                ctx.inc(s_.dma_start(out=xn0c[:, k * CH:(k + 1) * CH],
                                     in_=xn0_park[k][:, c * CH:(c + 1) * CH]),
                        "ld", 16)
                ctx.inc(s_.dma_start(out=q2cc[:, k * CH:(k + 1) * CH],
                                     in_=q2_park[k][:, c * CH:(c + 1) * CH]),
                        "ld", 16)
        Y(_lc)

        for pr in range(NPAIR):
            def _t(t, c=c, pr=pr):
                ctx.wait(t, "ve")
                ctx.wait(t, "sc")
                if pr == 0:
                    ctx.wait(t, "ld")
                t.matmul(ps[2][0:2, 0:CH], ks2[:, pr * 2:(pr + 1) * 2],
                         q2cc[:, pr * CH:(pr + 1) * CH], start=True, stop=True)
                mm = t.matmul(ps[3][:, 0:CH], kv2d[:, pr * 128:(pr + 1) * 128],
                              q2cc[:, pr * CH:(pr + 1) * CH], start=True,
                              stop=True)
                ctx.inc(mm, "td")
            T(_t)

            def _v(v, c=c, pr=pr):
                ctx.wait(v, "td")
                ctx.inc(v.tensor_scalar_max(tmp1[0:2, 0:CH], ps[2][0:2, 0:CH],
                                            1e-6), "ve")
                ctx.inc(v.reciprocal(tmp1[0:2, CH:2 * CH], tmp1[0:2, 0:CH]),
                        "ve")
                ctx.inc(v.tensor_copy(bfrow[0:2, 0:CH], tmp1[0:2, CH:2 * CH]),
                        "ve")
            V(_v)

            def _t2(t, c=c, pr=pr):
                ctx.wait(t, "ve")
                mm = t.matmul(ps[4][:, 0:CH], exp2[:, :], bfrow[0:2, 0:CH],
                              start=True, stop=True)
                ctx.inc(mm, "td")
            T(_t2)

            def _v2(v, c=c, pr=pr):
                ctx.wait(v, "td")
                ctx.inc(v.tensor_copy(tmp1[:, 0:CH], ps[4][:, 0:CH]), "ve")
                ctx.inc(v.tensor_mul(hb[:, pr * CH:(pr + 1) * CH],
                                     ps[3][:, 0:CH], tmp1[:, 0:CH]), "ve")
            V(_v2)

        def _lx(s_, c=c):
            ctx.wait(s_, "ve")
            ctx.wait(s_, "st")
            for k in range(8):
                ctx.inc(s_.dma_start(out=xchunk[:, k * CH:(k + 1) * CH],
                                     in_=ins["x_fm"][k][:, c * CH:(c + 1) * CH]),
                        "ld", 16)
        Y(_lx)

        def ev_wout(oc, pb, pb2):
            def _v(v, oc=oc, pb=pb):
                ctx.wait(v, "td")
                ctx.wait(v, "ld")
                ctx.inc(v.scalar_tensor_tensor(
                    xchunk[:, oc * CH:(oc + 1) * CH], ps[pb][:, 0:CH], 1.0,
                    xchunk[:, oc * CH:(oc + 1) * CH], ALU.bypass, ALU.add),
                    "ve")
            V(_v)
            return ("ve", ctx.cnt["ve"])

        mm_stream(ins["wout"], 8, 8,
                  lambda kc: hb[:, kc * CH:(kc + 1) * CH], ev_wout,
                  pbanks=(5, 6))
        if c == 3:
            tap("attnfm3", hb[:, 0:8 * CH], [128, 8 * CH], BF16)
            tap("accA", xchunk[:, :], [128, 8 * CH], F32)

        def ev_w12(oc, pb, pb2, c=c):
            def _s(sc, oc=oc, pb=pb):
                ctx.wait(sc, "td")
                ctx.wait(sc, "ve")
                ctx.inc(sc.activation(silb[:, 0:CH], ps[pb][:, 0:CH], AF.Silu,
                                      scale=scal[:, oc:oc + 1]), "sc")
            S(_s)

            def _v(v, oc=oc, pb2=pb2):
                ctx.wait(v, "sc")
                ctx.inc(v.tensor_scalar_mul(tmp1[:, 0:CH], ps[pb2][:, 0:CH],
                                            scal[:, 32 + oc:33 + oc]), "ve")
                ctx.inc(v.tensor_mul(ha[:, oc * CH:(oc + 1) * CH],
                                     silb[:, 0:CH], tmp1[:, 0:CH]), "ve")
            V(_v)
            return ("ve", ctx.cnt["ve"])

        def fw(t):
            ctx.wait(t, "ve")
        mm_stream(ins["w1"], 32, 8,
                  lambda kc: xn0c[:, kc * CH:(kc + 1) * CH], ev_w12,
                  strips2=ins["w2"], tensor_first_wait=fw,
                  pbanks=(0, 1), pbanks2=(2, 3))

        def ev_h1(oc, pb, pb2):
            def _s(sc, oc=oc, pb=pb):
                ctx.wait(sc, "td")
                ctx.inc(sc.activation(hb[:, oc * CH:(oc + 1) * CH],
                                      ps[pb][:, 0:CH], AF.Gelu,
                                      scale=scal[:, 64 + oc:65 + oc]), "sc")
            S(_s)
            return ("sc", ctx.cnt["sc"])

        if c == 3:
            tap("ha3", ha[:, :], [128, 32 * CH], BF16)

        def fw1(t):
            ctx.wait(t, "ve")
        mm_stream(ins["h1"], 32, 32,
                  lambda kc: ha[:, kc * CH:(kc + 1) * CH], ev_h1,
                  tensor_first_wait=fw1, pbanks=(0, 1),
                  bufs=("C", "D"))

        def ev_h2(oc, pb, pb2):
            def _s(sc, oc=oc, pb=pb):
                ctx.wait(sc, "td")
                ctx.inc(sc.activation(ha[:, oc * CH:(oc + 1) * CH],
                                      ps[pb][:, 0:CH], AF.Gelu,
                                      scale=scal[:, 96 + oc:97 + oc]), "sc")
            S(_s)
            return ("sc", ctx.cnt["sc"])

        def fw2(t):
            ctx.wait(t, "sc")
        mm_stream(ins["h2"], 32, 32,
                  lambda kc: hb[:, kc * CH:(kc + 1) * CH], ev_h2,
                  tensor_first_wait=fw2, pbanks=(0, 1),
                  bufs=("C", "D"))

        def ev_w3(oc, pb, pb2):
            def _v(v, oc=oc, pb=pb):
                ctx.wait(v, "td")
                ctx.inc(v.scalar_tensor_tensor(
                    xchunk[:, oc * CH:(oc + 1) * CH], ps[pb][:, 0:CH],
                    scal[:, 128 + oc:129 + oc],
                    xchunk[:, oc * CH:(oc + 1) * CH], ALU.mult, ALU.add), "ve")
            V(_v)
            return ("ve", ctx.cnt["ve"])

        def fw3(t):
            ctx.wait(t, "sc")
        mm_stream(ins["w3"], 8, 32,
                  lambda kc: ha[:, kc * CH:(kc + 1) * CH], ev_w3,
                  tensor_first_wait=fw3, pbanks=(0, 1),
                  bufs=("C", "D"))

        if c == 3:
            tap("accB", xchunk[:, :], [128, 8 * CH], F32)

        emit_ln(lambda k: xchunk[:, k * CH:(k + 1) * CH],
                lambda k: xn1c[:, k * CH:(k + 1) * CH], CH)

        def _pk(s_, c=c):
            ctx.wait(s_, "ve")
            for k in range(8):
                ctx.inc(s_.dma_start(out=acc_park[c][k],
                                     in_=xchunk[:, k * CH:(k + 1) * CH]),
                        "st", 16)
                ctx.inc(s_.dma_start(out=ag_in[c][k],
                                     in_=xn1c[:, k * CH:(k + 1) * CH]),
                        "st", 16)
        Y(_pk)

        def _cg(g, c=c):
            ctx.wait(g, "st")
            ctx.inc(g.collective_compute("AllGather", ALU.bypass,
                                         replica_groups=rg, ins=[ag_in[c]],
                                         outs=[ag_out[c]]), "coll")
        G(_cg)

    barrier()
    sfree("hb", "ha", "xn1c", "q2cc", "xn0c", "wbufD", "wbufC",
          "bfrow", "ks2", "kv2d", "kvsb")
    del wb["C"], wb["D"]
    tap("acc3", xchunk[:, :], [128, 8 * CH], F32)
    tap("xn1c", xn1c[:, :], [128, 8 * CH], BF16)
    if stop_after == "ph3":
        return finish(), tap_list

    xn1g = sbuf("xn1g", [128, 8 * 1024], BF16)
    dtbuf = sbuf("dtbuf", [128, 8 * 1024], F32)
    xcbuf = sbuf("xcbuf", [128, 8 * 1024], F32)
    xiy = sbuf("xiy", [128, 8 * 1027], F32)
    yown = sbuf("yown", [128, 8 * 1024], BF16)
    tmpa = sbuf("tmpa", [128, 1024], F32)
    brow = sbuf("brow", [16, 2048], BF16)
    mscratch = sbuf("mscratch", [128, 8 * 1024], BF16)
    xn1cR = mscratch[:, 0:8 * CH]
    pe_sb = mscratch[:, 8 * CH:12 * CH]
    cbc = mscratch[:, 12 * CH:13 * CH]
    ucb = mscratch[:, 13 * CH:14 * CH]
    gsc = sbuf("gsc", [128, CH], F32)
    wb["C"] = sbuf("wbufC2", [128, 8 * 128], BF16)
    wb["D"] = sbuf("wbufD2", [128, 8 * 128], BF16)
    wfree["C"] = 0
    wfree["D"] = 0

    xinner = xiy
    ybuf = xiy

    pending = []
    defer_q = []

    def make_tail(j):
        steps = []
        yagc = 6 + j
        marks = {}

        def st_ry():
            def _ry(s_, j=j, yagc=yagc):
                ctx.wait(s_, "coll", yagc)
                ctx.wait(s_, "ve")
                for k in range(8):
                    ctx.inc(s_.dma_start(
                        out=mscratch[:, k * 1024:(k + 1) * 1024],
                        in_=yag_out[j][0][k]), "ld", 16)
                    ctx.inc(s_.dma_start(
                        out=xn1g[:, k * 1024:(k + 1) * 1024],
                        in_=yag_out[j][1][k]), "ld", 16)
            Y(_ry)
        steps.append(st_ry)

        for k in range(8):
            def st_vy(k=k):
                def _vy(v, j=j, k=k):
                    ctx.wait(v, "ld")
                    ctx.inc(v.tensor_scalar_mul(
                        tmp3[:, 0:1024], xn1g[:, k * 1024:(k + 1) * 1024],
                        flag1[:, 0:1]), "ve")
                    ctx.inc(v.scalar_tensor_tensor(
                        xn1g[:, k * 1024:(k + 1) * 1024],
                        mscratch[:, k * 1024:(k + 1) * 1024], flag0[:, 0:1],
                        tmp3[:, 0:1024], ALU.mult, ALU.add), "ve")
                V(_vy)
                if k == 7:
                    marks["vy"] = ctx.cnt["ve"]
            steps.append(st_vy)

        def st_ua():
            def _ua(s_, j=j):
                ctx.wait(s_, "ve")
                ctx.wait(s_, "st")
                for k in range(8):
                    ctx.inc(s_.dma_start(out=xchunk[:, k * CH:(k + 1) * CH],
                                         in_=acc_park[j][k]), "ld", 16)
                for k in range(8):
                    ctx.inc(s_.dma_start(out=xn1cR[:, k * CH:(k + 1) * CH],
                                         in_=ag_in[j][k]), "ld", 16)
            Y(_ua)
        steps.append(st_ua)

        def st_gate1():
            ld_moe = ctx.cnt["ld"]
            for t4 in range(4):
                def _t5(t, j=j, t4=t4, ld_moe=ld_moe):
                    ctx.wait(t, "ld", ld_moe)
                    pw = pfree.get(t4 % 2)
                    if pw:
                        ctx.wait(t, pw[0], pw[1])
                    pb = t4 % 2
                    for kc in range(8):
                        mm = t.matmul(ps[pb][:, 0:8],
                                      xn1cR[:, kc * CH + t4 * 128:
                                            kc * CH + (t4 + 1) * 128],
                                      gate_sb[:, kc * 8:(kc + 1) * 8],
                                      start=(kc == 0), stop=(kc == 7))
                    ctx.inc(mm, "td")
                T(_t5)

                def _v5(v, t4=t4):
                    ctx.wait(v, "td")
                    pb = t4 % 2
                    ctx.inc(v.reduce_max(logit[:, 32 + t4:33 + t4],
                                         ps[pb][:, 0:8],
                                         axis=mybir.AxisListType.X), "ve")
                    ctx.wait(v, "ve")
                    ctx.inc(v.tensor_scalar(logit[:, t4 * 8:(t4 + 1) * 8],
                                            ps[pb][:, 0:8],
                                            logit[:, 32 + t4:33 + t4], None,
                                            ALU.is_ge), "ve")
                V(_v5)
                pfree[t4 % 2] = ("ve", ctx.cnt["ve"])
        steps.append(st_gate1)

        def st_gate2():
            def _s6(sc, j=j):
                ctx.wait(sc, "ve")
                ctx.inc(sc.activation(logit[:, 36:40], logit[:, 32:36],
                                      AF.Sigmoid), "sc")
            S(_s6)

            def _v6(v, j=j):
                ctx.wait(v, "sc")
                for t4 in range(4):
                    ctx.inc(v.tensor_scalar_mul(
                        logitb[:, t4 * 8:(t4 + 1) * 8],
                        logit[:, t4 * 8:(t4 + 1) * 8],
                        logit[:, 36 + t4:37 + t4]), "ve")
            V(_v6)

            for t4 in range(4):
                def _t6(t, j=j, t4=t4):
                    ctx.wait(t, "ve")
                    pw = pfree.get(2)
                    if pw:
                        ctx.wait(t, pw[0], pw[1])
                    mm = t.matmul(ps[2][0:8, 0:128],
                                  logitb[:, t4 * 8:(t4 + 1) * 8],
                                  ident[:, :], start=True, stop=True)
                    ctx.inc(mm, "td")
                T(_t6)

                def _s7(v, j=j, t4=t4):
                    ctx.wait(v, "td")
                    ctx.inc(v.tensor_copy(cfm[0:8, t4 * 128:(t4 + 1) * 128],
                                          ps[2][0:8, 0:128]), "ve")
                V(_s7)
                pfree[2] = ("ve", ctx.cnt["ve"])
        steps.append(st_gate2)

        for e in range(NE):
            def st_cbc(e=e, j=j):
                def _t7(t, e=e):
                    ctx.wait(t, "ve")
                    pw = pfree.get(3)
                    if pw:
                        ctx.wait(t, pw[0], pw[1])
                    mm = t.matmul(ps[3][:, 0:CH],
                                  sel8[:, e * 128:(e + 1) * 128],
                                  cfm[0:8, 0:CH], start=True, stop=True)
                    ctx.inc(mm, "td")
                T(_t7)

                def _s8(v, e=e):
                    ctx.wait(v, "td")
                    ctx.inc(v.tensor_copy(cbc[:, 0:CH], ps[3][:, 0:CH]),
                            "ve")
                V(_s8)
                pfree[3] = ("ve", ctx.cnt["ve"])
            steps.append(st_cbc)

            def ev_gu(oc, pb, pb2, e=e):
                def _v(v, oc=oc, pb=pb, pb2=pb2):
                    ctx.wait(v, "td")
                    ctx.inc(v.tensor_mul(gsc[:, 0:CH], ps[pb][:, 0:CH],
                                         cbc[:, 0:CH]), "ve")
                    ctx.inc(v.tensor_mul(ucb[:, 0:CH], ps[pb2][:, 0:CH],
                                         cbc[:, 0:CH]), "ve")
                V(_v)
                ret = ("ve", ctx.cnt["ve"])

                def _s(sc, oc=oc):
                    ctx.wait(sc, "ve")
                    ctx.inc(sc.activation(silb[:, 0:CH], gsc[:, 0:CH],
                                          AF.Silu), "sc")
                S(_s)

                def dv2(oc=oc, e=e):
                    def _v2(v, oc=oc):
                        ctx.wait(v, "sc")
                        ctx.inc(v.tensor_mul(pe_sb[:, oc * CH:(oc + 1) * CH],
                                             silb[:, 0:CH], ucb[:, 0:CH]),
                                "ve")
                    V(_v2)
                    if oc == 3:
                        marks["gu", e] = ctx.cnt["ve"]
                defer_q.append(dv2)
                return ret

            def fwg(t):
                ctx.wait(t, "ld")
            mm_stream(wbf["wgate"][e], 4, 8,
                      lambda kc: xn1cR[:, kc * CH:(kc + 1) * CH], ev_gu,
                      strips2=wbf["wup"][e], tensor_first_wait=fwg,
                      pbanks=(0, 1), pbanks2=(2, 3),
                      cast_gate=cast_done["wup"] if (j == 0 and e == 0)
                      else None, steps=steps, fast=True)

            def ev_dn(oc, pb, pb2):
                def _v(v, oc=oc, pb=pb):
                    ctx.wait(v, "td")
                    ctx.inc(v.scalar_tensor_tensor(
                        xchunk[:, oc * CH:(oc + 1) * CH], ps[pb][:, 0:CH],
                        1.0, xchunk[:, oc * CH:(oc + 1) * CH],
                        ALU.bypass, ALU.add), "ve")
                V(_v)
                return ("ve", ctx.cnt["ve"])

            def fwd(t, e=e):
                ctx.wait(t, "ve", marks["gu", e])
            mm_stream(wbf["wdown"][e], 8, 4,
                      lambda kc: pe_sb[:, kc * CH:(kc + 1) * CH], ev_dn,
                      tensor_first_wait=fwd, pbanks=(0, 1),
                      cast_gate=cast_done["wdown"] if (j == 0 and e == 0)
                      else None, steps=steps, fast=True)

        def yrhs(kc, g_):
            if kc < 8:
                return yown[:, kc * 1024 + g_ * CH:kc * 1024 + (g_ + 1) * CH]
            return xn1g[:, (kc - 8) * 1024 + g_ * CH:
                        (kc - 8) * 1024 + (g_ + 1) * CH]

        for g_ in range(2):
            sel = flag1 if g_ == 0 else flag0

            def ev_op(oc, pb, pb2, j=j, g_=g_, sel=sel):
                def _v(v, oc=oc, pb=pb, sel=sel):
                    ctx.wait(v, "td")
                    ctx.inc(v.tensor_scalar_mul(
                        tmp3[:, 0:CH], ps[pb][:, 0:CH], sel[:, 0:1]),
                        "ve")
                    ctx.inc(v.scalar_tensor_tensor(
                        xchunk[:, oc * CH:(oc + 1) * CH], tmp3[:, 0:CH],
                        1.0, xchunk[:, oc * CH:(oc + 1) * CH],
                        ALU.bypass, ALU.add), "ve")
                V(_v)
                return ("ve", ctx.cnt["ve"])

            def fwo(t):
                ctx.wait(t, "ve", marks["vy"])
            mm_stream(wbf["oproj"], 8, 16,
                      lambda kc, g_=g_: yrhs(kc, g_), ev_op,
                      tensor_first_wait=fwo, pbanks=(0, 1),
                      cast_gate=cast_done["oproj"] if (j == 0 and g_ == 0)
                      else None, steps=steps, fast=True)

        def st_so():
            def _so(s_, j=j):
                ctx.wait(s_, "ve")
                for k in range(8):
                    ctx.inc(s_.dma_start(out=out[k][:, j * CH:(j + 1) * CH],
                                         in_=xchunk[:, k * CH:(k + 1) * CH]),
                            "st", 16)
            Y(_so)
        steps.append(st_so)
        return steps

    for j in range(NCH):
        agc = 2 + j

        def _rb(s_, j=j, agc=agc):
            ctx.wait(s_, "coll", agc)
            ctx.wait(s_, "ve")
            ctx.wait(s_, "td")
            for k in range(8):
                ctx.inc(s_.dma_start(out=xn1g[:, k * 1024:k * 1024 + CH],
                                     in_=ag_out[j][0][k]), "ld", 16)
                ctx.inc(s_.dma_start(out=xn1g[:, k * 1024 + CH:(k + 1) * 1024],
                                     in_=ag_out[j][1][k]), "ld", 16)
        Y(_rb)

        def _vh(v, j=j):
            ctx.wait(v, "sc")
            if j > 0:
                for k in range(8):
                    ctx.inc(v.tensor_copy(
                        halo[:, k * 3:(k + 1) * 3],
                        xinner[:, k * 1027 + 1024:k * 1027 + 1027]), "ve")
            else:
                ctx.inc(v.memset(halo[:, :], 0.0), "ve")
            for k in range(8):
                ctx.inc(v.tensor_copy(xinner[:, k * 1027:k * 1027 + 3],
                                      halo[:, k * 3:(k + 1) * 3]), "ve")
        V(_vh)

        for g_ in range(2):
            def ev_xpg(oc, pb, pb2, j=j, g_=g_):
                def _s(v, oc=oc, pb=pb):
                    ctx.wait(v, "td")
                    ctx.inc(v.tensor_copy(
                        xinner[:, oc * 1027 + 3 + g_ * CH:
                               oc * 1027 + 3 + (g_ + 1) * CH],
                        ps[pb][:, 0:CH]), "ve")
                V(_s)
                return ("ve", ctx.cnt["ve"])

            def fwx(t, j=j, g_=g_):
                ctx.wait(t, "ld")
            mm_stream(wbf["xproj"], 8, 8,
                      lambda kc, g_=g_: xn1g[:, kc * 1024 + g_ * CH:
                                             kc * 1024 + (g_ + 1) * CH],
                      ev_xpg, tensor_first_wait=fwx, pbanks=(0, 1),
                      cast_gate=cast_done["xproj"] if (j == 0 and g_ == 0)
                      else None, fast=True)

        for k in range(8):
            def _v(v, j=j, k=k):
                ctx.wait(v, "sc")
                ctx.inc(v.tensor_scalar_mul(
                    xcbuf[:, k * 1024:(k + 1) * 1024],
                    xinner[:, k * 1027 + 0:k * 1027 + 1024],
                    convw_sb[:, k * 4:k * 4 + 1]), "ve")
                for kk in range(1, 4):
                    ctx.inc(v.scalar_tensor_tensor(
                        xcbuf[:, k * 1024:(k + 1) * 1024],
                        xinner[:, k * 1027 + kk:k * 1027 + 1024 + kk],
                        convw_sb[:, k * 4 + kk:k * 4 + kk + 1],
                        xcbuf[:, k * 1024:(k + 1) * 1024],
                        ALU.mult, ALU.add), "ve")
            V(_v)

            def _s(sc, j=j, k=k):
                ctx.wait(sc, "ve")
                ctx.inc(sc.activation(xcbuf[:, k * 1024:(k + 1) * 1024],
                                      xcbuf[:, k * 1024:(k + 1) * 1024],
                                      AF.Silu, bias=scal[:, 144 + k:145 + k]),
                        "sc")
            S(_s)

        for g_ in range(2):
            def ev_dt(oc, pb, pb2, j=j, g_=g_):
                def _s(sc, oc=oc, pb=pb):
                    ctx.wait(sc, "td")
                    dst = dtbuf[:, oc * 1024 + g_ * CH:
                                oc * 1024 + (g_ + 1) * CH]
                    ctx.inc(sc.activation(dst, ps[pb][:, 0:CH], AF.Exp,
                                          bias=scal[:, 136 + oc:137 + oc]),
                            "sc")
                S(_s)
                return ("sc", ctx.cnt["sc"])

            def fwd2(t):
                ctx.wait(t, "ld")
            mm_stream(wbf["dtproj"], 8, 8,
                      lambda kc, g_=g_: xn1g[:, kc * 1024 + g_ * CH:
                                             kc * 1024 + (g_ + 1) * CH],
                      ev_dt, tensor_first_wait=fwd2, pbanks=(0, 1),
                      cast_gate=cast_done["dtproj"] if (j == 0 and g_ == 0)
                      else None, fast=True)

        def _sp2(sc, j=j):
            for oc in range(8):
                for gg in range(2):
                    dst = dtbuf[:, oc * 1024 + gg * CH:
                                oc * 1024 + (gg + 1) * CH]
                    ctx.inc(sc.activation(dst, dst, AF.Ln,
                                          bias=actc[:, 5:6]), "sc")
        S(_sp2)
        sc_dt = ctx.cnt["sc"]

        def _tbc(t, j=j):
            ctx.wait(t, "cast")
            ctx.wait(t, "sc")
            for g_ in range(2):
                for kc in range(8):
                    t.matmul(ps[4 + g_][0:16, 0:CH],
                             bcw[:, kc * 16:(kc + 1) * 16],
                             xn1g[:, kc * 1024 + g_ * CH:
                                  kc * 1024 + (g_ + 1) * CH],
                             start=(kc == 0), stop=(kc == 7))
                for kc in range(8):
                    mm = t.matmul(ps[6 + g_][0:16, 0:CH],
                                  bcw[:, 128 + kc * 16:128 + (kc + 1) * 16],
                                  xn1g[:, kc * 1024 + g_ * CH:
                                       kc * 1024 + (g_ + 1) * CH],
                                  start=(kc == 0), stop=(kc == 7))
            ctx.inc(mm, "td")
        T(_tbc)

        def _sbc(v, j=j):
            ctx.wait(v, "td")
            for g_ in range(2):
                ctx.inc(v.tensor_copy(brow[0:16, g_ * CH:(g_ + 1) * CH],
                                      ps[4 + g_][0:16, 0:CH]), "ve")
                ctx.inc(v.tensor_copy(
                    brow[0:16, 1024 + g_ * CH:1024 + (g_ + 1) * CH],
                    ps[6 + g_][0:16, 0:CH]), "ve")
        V(_sbc)

        for k in range(8):
            def _v(v, j=j, k=k):
                ctx.wait(v, "sc")
                ctx.inc(v.tensor_scalar_mul(
                    ybuf[:, k * 1027:k * 1027 + 1024],
                    xcbuf[:, k * 1024:(k + 1) * 1024],
                    scal[:, 152 + k:153 + k]), "ve")
                ctx.inc(v.tensor_mul(xcbuf[:, k * 1024:(k + 1) * 1024],
                                     xcbuf[:, k * 1024:(k + 1) * 1024],
                                     dtbuf[:, k * 1024:(k + 1) * 1024]), "ve")
            V(_v)

        ve_after = {}
        sc_after = {}
        abufs = [tmp1, tmpa]
        n_steps = len(pending)
        done_steps = 0

        def emit_sa(s_i, k0):
            i0 = s_i * 8 + k0
            ab0 = abufs[i0 % 2]

            def _sa(sc, s_i=s_i, k0=k0, i0=i0, ab0=ab0):
                if i0 >= 2:
                    ctx.wait(sc, "ve", ve_after[i0 - 2])
                ctx.inc(sc.activation(
                    ab0[:, 0:1024], dtbuf[:, k0 * 1024:(k0 + 1) * 1024],
                    AF.Exp,
                    scale=a_sb[:, k0 * 16 + s_i:k0 * 16 + s_i + 1]), "sc")
                sc_after[i0] = ctx.cnt["sc"]
            S(_sa)

        for s_i in range(NS):
            ve_bc = ve_after.get(s_i * 8 - 1)

            def _tb(t, j=j, s_i=s_i, ve_bc=ve_bc):
                if ve_bc is None:
                    ctx.wait(t, "ve")
                else:
                    ctx.wait(t, "ve", ve_bc)
                for g_ in range(2):
                    t.matmul(psW0[:, g_ * CH:(g_ + 1) * CH],
                             sel16[:, s_i * 128:(s_i + 1) * 128],
                             brow[0:16, g_ * CH:(g_ + 1) * CH],
                             start=True, stop=True)
                    mm = t.matmul(psW1[:, g_ * CH:(g_ + 1) * CH],
                                  sel16[:, s_i * 128:(s_i + 1) * 128],
                                  brow[0:16, 1024 + g_ * CH:
                                       1024 + (g_ + 1) * CH],
                                  start=True, stop=True)
                ctx.inc(mm, "td")
            T(_tb)
            td_tb = ctx.cnt["td"]

            emit_sa(s_i, 0)
            emit_sa(s_i, 1)

            for k in range(8):
                col = s_i * 8 + k
                i = s_i * 8 + k
                ab = abufs[i % 2]

                def _vs(v, j=j, s_i=s_i, k=k, col=col, i=i, ab=ab,
                        td_tb=td_tb):
                    ctx.wait(v, "sc", sc_after[i])
                    if k == 0:
                        if s_i == NS - 1:
                            ctx.wait(v, "td")
                        else:
                            ctx.wait(v, "td", td_tb)
                    ctx.inc(v.tensor_mul(tmp2[:, 0:1024],
                                         xcbuf[:, k * 1024:(k + 1) * 1024],
                                         psW0[:, 0:1024]), "ve")
                    ctx.inc(v.tensor_tensor_scan(tmp3[:, 0:1024],
                                                 ab[:, 0:1024],
                                                 tmp2[:, 0:1024],
                                                 state[:, col:col + 1],
                                                 ALU.mult, ALU.add), "ve")
                    ctx.wait(v, "ve")
                    ctx.inc(v.tensor_copy(state[:, col:col + 1],
                                          tmp3[:, 1023:1024]), "ve")
                    ctx.inc(v.tensor_mul(tmp2[:, 0:1024], tmp3[:, 0:1024],
                                         psW1[:, 0:1024]), "ve")
                    if s_i < NS - 1:
                        ctx.inc(v.tensor_add(
                            ybuf[:, k * 1027:k * 1027 + 1024],
                            ybuf[:, k * 1027:k * 1027 + 1024],
                            tmp2[:, 0:1024]), "ve")
                    else:
                        ctx.inc(v.tensor_add(
                            yown[:, k * 1024:(k + 1) * 1024],
                            ybuf[:, k * 1027:k * 1027 + 1024],
                            tmp2[:, 0:1024]), "ve")
                V(_vs)
                ve_after[i] = ctx.cnt["ve"]

                if k + 2 < 8:
                    emit_sa(s_i, k + 2)

                slot = s_i * 8 + k
                while pending and done_steps * 120 < n_steps * (slot + 1):
                    while defer_q:
                        defer_q.pop(0)()
                    pending.pop(0)()
                    done_steps += 1

        while pending:
            while defer_q:
                defer_q.pop(0)()
            pending.pop(0)()
        while defer_q:
            defer_q.pop(0)()

        def _sy(s_, j=j):
            ctx.wait(s_, "ve")
            for k in range(8):
                ctx.inc(s_.dma_start(out=yag_in[j][k],
                                     in_=yown[:, k * 1024:(k + 1) * 1024]),
                        "st", 16)
        Y(_sy)

        def _cy(g, j=j):
            ctx.wait(g, "st")
            ctx.inc(g.collective_compute("AllGather", ALU.bypass,
                                         replica_groups=rg, ins=[yag_in[j]],
                                         outs=[yag_out[j]]), "coll")
        G(_cy)

        pending = make_tail(j)

    while pending:
        while defer_q:
            defer_q.pop(0)()
        pending.pop(0)()
    while defer_q:
        defer_q.pop(0)()

    tap("dtbuf", dtbuf[:, :], [128, 8 * 1024], F32)
    tap("xcbuf", xcbuf[:, :], [128, 8 * 1024], F32)
    tap("yown", yown[:, :], [128, 8 * 1024], BF16)
    return finish(), tap_list


def kernel(**inputs):
    nc, _ = build(8)
    in_maps = [prep_core_inputs(c, inputs) for c in range(8)]
    res = run_bass_kernel_spmd(nc, in_maps, core_ids=list(range(8)))
    full = np.zeros((B, T, D), dtype=np.float32)
    for core in range(8):
        b, h = core // 2, core % 2
        o = np.asarray(res.results[core]["out"]).reshape(D, NCH, CH)
        for c in range(NCH):
            g = 2 * c + h
            full[b, g * CH:(g + 1) * CH, :] = o[:, c, :].T
    return full
